# version 02
# baseline (speedup 1.0000x reference)
import numpy as np
import ml_dtypes
import concourse.bass as bass
import concourse.mybir as mybir
from concourse.bass_utils import run_bass_kernel_spmd

F32 = mybir.dt.float32
BF16 = mybir.dt.bfloat16
I32 = mybir.dt.int32
AF = mybir.ActivationFunctionType
ALU = mybir.AluOpType
AXL = mybir.AxisListType
bfloat16 = ml_dtypes.bfloat16

NCORES = 8
B, T, D = 2, 8192, 2048
H, DH, BLK = 16, 128, 256
HID = 5632
DC, DR = 512, 64
TPC = T * B // NCORES
NBLK = TPC // BLK
ND = D // 128
NH = HID // 128
SOFT_SC = 1.0 / float(np.sqrt(DH + DR))
LN_EPS = 1e-5
RMS_EPS = 1e-6
BIG = 1.0e30

ENGS = ("sp", "act", "dve", "pe", "pool")
NDMASEM = 8
CHANS = (tuple(f"sp{i}" for i in range(NDMASEM)) +
         tuple(f"pD{i}" for i in range(NDMASEM)) +
         ("act", "dve", "pe", "pool"))


def _semaphore_raw(self, name):
    return self.semaphore(name).__enter__()


bass.Bass.semaphore_raw = _semaphore_raw


class Buf:

    __slots__ = ("w", "rs", "name")

    def __init__(self, name=""):
        self.w = None
        self.rs = []
        self.name = name


def _evs(x):
    if x is None:
        return ()
    if isinstance(x, tuple) and x and isinstance(x[0], tuple) and \
            isinstance(x[0][0], str):
        return x
    return (x,)


class Sched:

    def __init__(self, nc):
        self.nc = nc
        self.ops = {e: [] for e in ENGS}
        self.cnt = {c: 0 for c in CHANS}
        self.sems = {}
        self.nbuf = 0
        self.dmaidx = {"sp": 0, "pool": 0}

    def buf(self, name=""):
        self.nbuf += 1
        return Buf(name or f"b{self.nbuf}")

    def alias_buf(self, parent, name=""):
        b = self.buf(name)
        b.w = parent.w
        b.rs = list(parent.rs)
        return b

    def op(self, eng, fn, reads=(), writes=(), region_of=(), deps=(), inc=None):
        if inc is None:
            inc = 16 if eng == "sp" else 1
        dep = set()
        for b in reads:
            dep.update(_evs(b.w))
        for b in tuple(writes) + tuple(region_of):
            dep.update(_evs(b.w))
            dep.update(b.rs)
        for d in deps:
            dep.update(_evs(d))
        if inc == 16:
            i = self.dmaidx[eng]
            self.dmaidx[eng] = (i + 1) % NDMASEM
            chan = (f"sp{i}" if eng == "sp" else f"pD{i}")
            if self.cnt[chan]:
                dep.add((chan, self.cnt[chan]))
        else:
            chan = eng
        self.cnt[chan] += inc
        ev = (chan, self.cnt[chan])
        self.ops[eng].append((sorted(dep), fn, inc, chan))
        for b in reads:
            b.rs.append(ev)
        for b in writes:
            b.w = ev
            b.rs = []
        return ev

    def seal(self, b, events):
        b.w = tuple(events)
        b.rs = []

    def dma(self, out, in_, eng="sp", reads=(), writes=(), region_of=(),
            deps=(), **kw):
        return self.op(eng, lambda q: q.dma_start(out=out, in_=in_, **kw),
                       reads=reads, writes=writes, region_of=region_of,
                       deps=deps, inc=16)

    def dmaT(self, out, in_, eng="sp", reads=(), writes=(), region_of=(),
             deps=()):
        return self.op(eng, lambda q: q.dma_start_transpose(out, in_),
                       reads=reads, writes=writes, region_of=region_of,
                       deps=deps, inc=16)

    def emit(self, block):
        for c in CHANS:
            self.sems[c] = self.nc.semaphore_raw(f"S_{c}")

        def mk(eng_name):
            def body(q):
                seen = {}
                for deps, fn, inc, chan in self.ops[eng_name]:
                    for (se, sv) in deps:
                        if seen.get(se, 0) < sv:
                            q.wait_ge(self.sems[se], sv)
                            seen[se] = sv
                    ins = fn(q)
                    if inc:
                        ins.then_inc(self.sems[chan], inc)
            body.__name__ = "_" + eng_name
            return body

        block.sync(mk("sp"))
        block.scalar(mk("act"))
        block.vector(mk("dve"))
        block.tensor(mk("pe"))
        block.gpsimd(mk("pool"))


class Bufs:
    def __init__(self, nc, S):
        self.nc = nc
        self.S = S
        self.tiles = {}

    def sb(self, name, shape, dtype):
        if name not in self.tiles:
            t = self.nc.sbuf_tensor(name, list(shape), dtype).__enter__()
            self.tiles[name] = (t.ap() if hasattr(t, "ap") else t,
                                self.S.buf(name))
        return self.tiles[name]

    def ps(self, name, shape, dtype):
        if name not in self.tiles:
            t = self.nc.psum_tensor(name, list(shape), dtype).__enter__()
            self.tiles[name] = (t.ap() if hasattr(t, "ap") else t,
                                self.S.buf(name))
        return self.tiles[name]

    def view(self, name, base_name, sl):
        if name not in self.tiles:
            t, b = self.tiles[base_name]
            self.tiles[name] = (t[:, sl], self.S.alias_buf(b, name))
        return self.tiles[name]

    def const(self, value, parts=128):
        name = f"cst_{value!r}_{parts}"
        if name not in self.tiles:
            t = self.nc.sbuf_tensor(name, [parts, 1], F32).__enter__()
            t = t.ap() if hasattr(t, "ap") else t
            b = self.S.buf(name)
            self.S.op("pool", lambda q: q.memset(t, float(value)), writes=[b])
            self.tiles[name] = (t, b)
        return self.tiles[name]


class Arena:

    def __init__(self, nc, S, name, cols, dtype=BF16):
        t = nc.sbuf_tensor(name, [128, cols], dtype).__enter__()
        self.t = t.ap() if hasattr(t, "ap") else t
        self.S = S
        self.views = []

    def view(self, name, lo, hi):
        evs = set()
        keep = []
        for (l2, h2, b2) in self.views:
            if l2 < hi and lo < h2:
                evs.update(_evs(b2.w))
                evs.update(b2.rs)
                if l2 < lo or h2 > hi:
                    keep.append((l2, h2, b2))
            else:
                keep.append((l2, h2, b2))
        b = self.S.buf(name)
        b.w = tuple(sorted(evs)) if evs else None
        self.views = keep + [(lo, hi, b)]
        return self.t[:, lo:hi], b


def emit_identity(S, ident, b_id):
    S.op("pool", lambda q: q.memset(ident, 0.0), writes=[b_id])
    S.op("pool", lambda q: q.affine_select(
        out=ident, in_=ident, compare_op=ALU.not_equal,
        fill=1.0, base=0, pattern=[[-1, ident.shape[1]]],
        channel_multiplier=1), reads=[b_id], writes=[b_id])


def host_consts(core):
    q = core % 4
    c = {}
    ar = np.arange(1, BLK + 1, dtype=np.float32)
    c["qdecay_idx"] = np.tile(ar, (128, 1))
    c["kdecay_idx"] = np.tile(BLK - ar, (128, 1))
    idx = ar[:, None] - ar[None, :]
    ddT = idx.T.copy()
    ddT[ddT < 0] = BIG
    c["ddT0"] = np.ascontiguousarray(ddT[0:128, :], np.float32)
    c["ddT1"] = np.ascontiguousarray(ddT[128:256, :], np.float32)
    c["blkj"] = (np.arange(NBLK, dtype=np.float32) * BLK).reshape(1, -1)
    dist = np.full(3, BIG, np.float32)
    for p in range(3):
        if p < q:
            dist[p] = float(TPC * (q - 1 - p))
    c["dist"] = dist.reshape(1, -1)
    c["positions"] = (q * TPC + np.arange(TPC)).astype(np.float32).reshape(1, -1)
    invf = 1.0 / 10000.0 ** (np.arange(DR // 2, dtype=np.float32) / (DR // 2))
    c["invfreq4"] = np.tile(invf, 4).reshape(-1, 1)
    return c


CONST_SPECS = [
    ("qdecay_idx", (128, BLK), F32),
    ("kdecay_idx", (128, BLK), F32),
    ("ddT0", (128, BLK), F32),
    ("ddT1", (128, BLK), F32),
    ("blkj", (1, NBLK), F32),
    ("dist", (1, 3), F32),
    ("positions", (1, TPC), F32),
    ("invfreq4", (128, 1), F32),
]


def emit_setup(S, bufs):
    nc = S.nc
    ident, b_id = bufs.sb("ident", (128, 128), BF16)
    ones1, b_o1 = bufs.sb("ones1", (1, 128), F32)
    onesb, b_ob = bufs.sb("onesb", (128, 1), BF16)
    emit_identity(S, ident, b_id)
    S.op("pool", lambda q: q.memset(ones1, 1.0), writes=[b_o1])
    S.op("pool", lambda q: q.memset(onesb, 1.0), writes=[b_ob])
    return (ident, b_id), (ones1, b_o1), (onesb, b_ob)


def emit_bcast_row(S, bufs, psum, b_ps, ones1, b_o1, row_ap, row_buf, name, n,
                   parts=128):
    t, b = bufs.sb(name, (parts, n), F32)
    S.op("pe", lambda q: q.matmul(psum[0:parts, 0:n], ones1[:, 0:parts],
                                  row_ap, start=True, stop=True),
         reads=[b_o1, row_buf], writes=[b_ps])
    S.op("dve", lambda q: q.tensor_copy(t, psum[0:parts, 0:n]),
         reads=[b_ps], writes=[b])
    return t, b


def emit_ln(S, bufs, x_hbm, b_x, ln_fm, b_ln, tok0, ntok, tag, ident, b_id,
            eps=LN_EPS):
    ntiles = ntok // 128
    xt = [bufs.sb(f"st_f{i}", (128, D), F32) for i in range(2)]
    sq = [bufs.sb(f"st_b{i}", (128, D), BF16) for i in range(2)]
    st = [bufs.sb(f"lnst{i}", (128, 8), F32) for i in range(2)]
    ltm = [bufs.sb(f"st_b{2 + i}", (128, D), BF16) for i in range(2)]
    pbf, b_pbf = bufs.ps("pbf", (128, 1024), BF16)
    wr_evs = []
    for i in range(ntiles):
        bb = i % 2
        (tx, b_tx), (tq, b_tq), (ts_, b_ts), (tl, b_tl) = \
            xt[bb], sq[bb], st[bb], ltm[bb]
        S.dma(tx, x_hbm[tok0 + i * 128: tok0 + (i + 1) * 128, :],
              reads=[b_x], writes=[b_tx])
        S.op("dve", lambda q, tx=tx, ts_=ts_: q.reduce_sum(
            ts_[:, 0:1], tx, axis=AXL.X), reads=[b_tx], writes=[b_ts])
        S.op("act", lambda q, tx=tx, tq=tq, ts_=ts_: q.activation(
            tq, tx, AF.Square, accum_out=ts_[:, 1:2]),
            reads=[b_tx], writes=[b_tq, b_ts])
        S.op("dve", lambda q, ts_=ts_: q.tensor_scalar_mul(
            ts_[:, 2:3], ts_[:, 0:1], 1.0 / D), reads=[b_ts], writes=[b_ts])
        S.op("act", lambda q, ts_=ts_: q.activation(
            ts_[:, 3:4], ts_[:, 0:1], AF.Square, scale=1.0 / D),
            reads=[b_ts], writes=[b_ts])
        S.op("dve", lambda q, ts_=ts_: q.tensor_scalar(
            ts_[:, 4:5], ts_[:, 1:2], 1.0 / D, ts_[:, 3:4], ALU.mult,
            ALU.subtract), reads=[b_ts], writes=[b_ts])
        te_, b_te_ = bufs.const(eps)
        S.op("act", lambda q, ts_=ts_, te_=te_: q.activation(
            ts_[:, 5:6], ts_[:, 4:5], AF.Ln, bias=te_),
            reads=[b_ts, b_te_], writes=[b_ts])
        S.op("act", lambda q, ts_=ts_: q.activation(
            ts_[:, 6:7], ts_[:, 5:6], AF.Exp, scale=-0.5),
            reads=[b_ts], writes=[b_ts])
        S.op("dve", lambda q, tx=tx, tl=tl, ts_=ts_: q.tensor_scalar(
            tl, tx, ts_[:, 2:3], ts_[:, 6:7], ALU.subtract, ALU.mult),
            reads=[b_tx, b_ts], writes=[b_tl])
        for g in range(2):
            def tgroup(q, tl=tl, g=g):
                r = None
                for c8 in range(8):
                    c = g * 8 + c8
                    r = q.transpose(pbf[:, c8 * 128:(c8 + 1) * 128],
                                    tl[:, c * 128:(c + 1) * 128], ident)
                return r
            S.op("pe", tgroup, reads=[b_tl, b_id], writes=[b_pbf])
            dst = ln_fm.rearrange("p (c t) -> p c t", c=ND)[
                :, g * 8:(g + 1) * 8, i * 128:(i + 1) * 128]
            wr_evs.append(S.op("dve", lambda q, dst=dst: q.tensor_copy(
                dst, pbf.rearrange("p (c t) -> p c t", c=8)),
                reads=[b_pbf], region_of=[b_ln]))
    S.seal(b_ln, wr_evs)


def build_p1():
    nc = bass.Bass("TRN2", target_bir_lowering=False)
    x = nc.dram_tensor("x", [TPC, D], F32, kind="ExternalInput").ap()
    w_qkv = nc.dram_tensor("w_qkv", [3 * D, D], F32, kind="ExternalInput").ap()
    w_gate = nc.dram_tensor("w_gate", [D, D], F32, kind="ExternalInput").ap()
    slope = nc.dram_tensor("slope", [1, H], F32, kind="ExternalInput").ap()
    consts = {n: nc.dram_tensor(n, list(s), d, kind="ExternalInput").ap()
              for n, s, d in CONST_SPECS}
    attn_local = nc.dram_tensor("attn_local", [D, TPC], BF16,
                                kind="ExternalOutput").ap()
    qdec_g = nc.dram_tensor("qdec_g", [D, TPC], BF16,
                            kind="ExternalOutput").ap()
    gate_o = nc.dram_tensor("gate_o", [D, TPC], BF16,
                            kind="ExternalOutput").ap()
    kvsum = nc.dram_tensor("kvsum", [D, DH], F32, kind="ExternalOutput").ap()
    wbf = nc.dram_tensor("wbf", [4 * D, D], BF16).ap()
    qkv_fm = nc.dram_tensor("qkv_fm", [3 * D, TPC], BF16).ap()

    S = Sched(nc)
    bufs = Bufs(nc, S)
    b_x, b_wbf, b_qkvfm = S.buf("x"), S.buf("wbf"), S.buf("qkv_fm")
    b_attnl, b_qdecg, b_gateo, b_kvsum = (S.buf("attn_local"), S.buf("qdec_g"),
                                          S.buf("gate_o"), S.buf("kvsum"))

    with nc.Block() as block:
        (ident, b_id), (ones1, b_o1), (onesb, b_ob) = emit_setup(S, bufs)

        ln_fm, b_ln = bufs.sb("ln_fm", (128, ND * TPC), BF16)
        emit_ln(S, bufs, x, b_x, ln_fm, b_ln, 0, TPC, "1", ident, b_id)

        wc = [bufs.tiles[f"st_f{i}"] for i in range(2)]
        wcb = [bufs.tiles[f"st_b{i}"] for i in range(2)]
        wbf_evs = []
        for i in range(4 * ND):
            bb = i % 2
            (tc, b_tc), (tb, b_tb) = wc[bb], wcb[bb]
            src = (w_qkv[i * 128:(i + 1) * 128, :] if i < 3 * ND else
                   w_gate[(i - 3 * ND) * 128:(i - 3 * ND + 1) * 128, :])
            S.dma(tc, src, writes=[b_tc])
            S.op("dve", lambda q, tc=tc, tb=tb: q.tensor_copy(tb, tc),
                 reads=[b_tc], writes=[b_tb])
            wbf_evs.append(S.dma(wbf[i * 128:(i + 1) * 128, :], tb, eng="pool",
                                 reads=[b_tb], region_of=[b_wbf]))
        S.seal(b_wbf, wbf_evs)

        pk, b_pk = [], []
        for i in range(6):
            t, b = bufs.ps(f"pk{i}", (128, 512), F32)
            pk.append(t)
            b_pk.append(b)

        wt = [bufs.sb(f"wt{i}", (128, 512 * ND), BF16) for i in range(2)]
        ev = [bufs.sb(f"evs{i}", (128, 512), BF16) for i in range(3)]
        qkv_evs, gate_evs = [], []
        for jb in range((4 * D) // 512):
            bb = jb % 2
            tw, b_tw = wt[bb]
            wt_evs = []
            for dc in range(ND):
                wt_evs.append(S.dmaT(
                    tw[:, dc * 512:(dc + 1) * 512],
                    wbf[jb * 512:(jb + 1) * 512, dc * 128:(dc + 1) * 128],
                    reads=[b_wbf], region_of=[b_tw]))
            S.seal(b_tw, wt_evs)
            for j4 in range(4):
                jc = jb * 4 + j4
                for nt in range(4):
                    k = (jc * 4 + nt) % 6
                    tp, b_tp = pk[k], b_pk[k]

                    def mmg(q, tw=tw, j4=j4, nt=nt, tp=tp):
                        r = None
                        for dc in range(ND):
                            r = q.matmul(
                                tp,
                                tw[:, dc * 512 + j4 * 128:
                                   dc * 512 + j4 * 128 + 128],
                                ln_fm[:, dc * TPC + nt * 512:
                                      dc * TPC + (nt + 1) * 512],
                                start=(dc == 0), stop=(dc == ND - 1))
                        return r
                    S.op("pe", mmg, reads=[b_tw, b_ln], writes=[b_tp])
                    te, b_te = ev[(jc * 4 + nt) % 3]
                    func = AF.Silu if jc < 3 * ND else AF.Sigmoid
                    S.op("act", lambda q, tp=tp, te=te, func=func:
                         q.activation(te, tp, func), reads=[b_tp],
                         writes=[b_te])
                    if jc < 3 * ND:
                        qkv_evs.append(S.dma(
                            qkv_fm[jc * 128:(jc + 1) * 128,
                                   nt * 512:(nt + 1) * 512],
                            te, eng="pool", reads=[b_te], region_of=[b_qkvfm]))
                    else:
                        row = (jc - 3 * ND) * 128
                        gate_evs.append(S.dma(
                            gate_o[row:row + 128, nt * 512:(nt + 1) * 512],
                            te, eng="pool", reads=[b_te], region_of=[b_gateo]))
        S.seal(b_qkvfm, qkv_evs)
        S.seal(b_gateo, gate_evs)

        sl_row, b_slr = bufs.sb("sl_row", (1, H), F32)
        S.dma(sl_row, slope, writes=[b_slr])
        sb_s, b_ss = emit_bcast_row(S, bufs, pk[0], b_pk[0], ones1, b_o1,
                                    sl_row, b_slr, "sb_s", H)
        bj_row, b_bjr = bufs.sb("bj_row", (1, NBLK), F32)
        S.dma(bj_row, consts["blkj"], writes=[b_bjr])
        bjb, b_bjb = emit_bcast_row(S, bufs, pk[1], b_pk[1], ones1, b_o1,
                                    bj_row, b_bjr, "bjb", NBLK)
        qdi, b_qdi = arf.view("qdi", 0, BLK)
        kdi, b_kdi = arf.view("kdi", BLK, 2 * BLK)
        dd0, b_dd0 = arf.view("dd0", 2 * BLK, 3 * BLK)
        dd1, b_dd1 = arf.view("dd1", 3 * BLK, 4 * BLK)
        S.dma(qdi, consts["qdecay_idx"], writes=[b_qdi])
        S.dma(kdi, consts["kdecay_idx"], writes=[b_kdi])
        S.dma(dd0, consts["ddT0"], writes=[b_dd0])
        S.dma(dd1, consts["ddT1"], writes=[b_dd1])
        dblk, b_dblk = bufs.sb("dblk", (128, H), F32)
        S.op("act", lambda q: q.activation(dblk, sb_s, AF.Exp,
                                           scale=-float(BLK)),
             reads=[b_ss], writes=[b_dblk])
        dgl, b_dgl = bufs.sb("dgl", (128, H * NBLK), F32)
        dgl_evs = []
        for h in range(H):
            dgl_evs.append(S.op("dve", lambda q, h=h: q.tensor_scalar_mul(
                dgl[:, h * NBLK:(h + 1) * NBLK], bjb, sb_s[:, h:h + 1]),
                reads=[b_bjb, b_ss], region_of=[b_dgl]))
        S.seal(b_dgl, dgl_evs)
        S.op("act", lambda q: q.activation(dgl, dgl, AF.Exp, scale=-1.0),
             reads=[b_dgl], writes=[b_dgl])
        qdb, b_qdb = bufs.view("qdb", "wt0", slice(0, H * BLK))
        kdb, b_kdb = bufs.view("kdb", "wt0", slice(H * BLK, 2 * H * BLK))
        ddb, b_ddb = bufs.view("ddb", "wt1", slice(0, 2 * H * BLK))
        tmpd, b_tmpd = bufs.sb("tmpd", (128, BLK), F32)
        for h in range(H):
            for (srcT, b_src, dstT, b_dst, col) in (
                    (qdi, b_qdi, qdb, b_qdb, h * BLK),
                    (kdi, b_kdi, kdb, b_kdb, h * BLK),
                    (dd0, b_dd0, ddb, b_ddb, 2 * h * BLK),
                    (dd1, b_dd1, ddb, b_ddb, (2 * h + 1) * BLK)):
                S.op("dve", lambda q, srcT=srcT, h=h: q.tensor_scalar_mul(
                    tmpd, srcT, sb_s[:, h:h + 1]),
                    reads=[b_src, b_ss], writes=[b_tmpd])
                S.op("act", lambda q, dstT=dstT, col=col: q.activation(
                    dstT[:, col:col + BLK], tmpd, AF.Exp, scale=-1.0),
                    reads=[b_tmpd], region_of=[b_dst])
        for b in (b_qdb, b_kdb, b_ddb):
            b.w = tuple(b.w if isinstance(b.w, tuple) else ())
        ready = ("act", S.cnt["act"])
        for b in (b_qdb, b_kdb, b_ddb):
            S.seal(b, [ready])

        pbf, b_pbf = bufs.tiles["pbf"]
        qf = [bufs.sb(f"qf{i}", (128, TPC), BF16) for i in range(2)]
        kf = [bufs.sb(f"kf{i}", (128, TPC), BF16) for i in range(2)]
        vf = [bufs.sb(f"vf{i}", (128, TPC), BF16) for i in range(2)]
        kvf, b_kvf = bufs.sb("kvf", (128, DH), F32)
        kvb, b_kvb = bufs.sb("kvb", (128, DH), BF16)
        qdx, b_qdx = bufs.sb("qdx", (128, BLK), BF16)
        qgx, b_qgx = bufs.sb("qgx", (128, BLK), BF16)
        kdx, b_kdx = bufs.sb("kdx", (128, BLK), BF16)
        vkm, b_vkm = bufs.sb("vkm", (128, 2 * 128), BF16)
        kkm, b_kkm = bufs.sb("kkm", (128, 2 * 128), BF16)
        ptt, b_ptt = bufs.sb("ptt", (128, 2 * BLK), BF16)
        aev = [bufs.sb(f"aev{i}", (128, BLK), BF16) for i in range(2)]
        attnl_evs, qdecg_evs, kvsum_evs = [], [], []
        for h in range(H):
            hb = h % 2
            (tq, b_tq), (tk, b_tk), (tv, b_tv) = qf[hb], kf[hb], vf[hb]
            S.dma(tq, qkv_fm[h * 128:(h + 1) * 128, :],
                  reads=[b_qkvfm], writes=[b_tq])
            S.dma(tk, qkv_fm[D + h * 128:D + (h + 1) * 128, :],
                  reads=[b_qkvfm], writes=[b_tk])
            S.dma(tv, qkv_fm[2 * D + h * 128:2 * D + (h + 1) * 128, :],
                  reads=[b_qkvfm], writes=[b_tv])
            S.op("dve", lambda q: q.memset(kvf, 0.0), writes=[b_kvf])
            S.op("dve", lambda q: q.memset(kvb, 0.0), writes=[b_kvb])
            for j in range(NBLK):
                t0 = j * BLK
                S.op("dve", lambda q, tq=tq, t0=t0, h=h: q.tensor_mul(
                    qdx, tq[:, t0:t0 + BLK], qdb[:, h * BLK:(h + 1) * BLK]),
                    reads=[b_tq, b_qdb], writes=[b_qdx])
                S.op("dve", lambda q, h=h, j=j: q.tensor_scalar_mul(
                    qgx, qdx, dgl[:, h * NBLK + j:h * NBLK + j + 1]),
                    reads=[b_qdx, b_dgl], writes=[b_qgx])
                qdecg_evs.append(S.dma(
                    qdec_g[h * 128:(h + 1) * 128, t0:t0 + BLK], qgx,
                    eng="pool", reads=[b_qgx], region_of=[b_qdecg]))
                S.op("dve", lambda q, tk=tk, t0=t0, h=h: q.tensor_mul(
                    kdx, tk[:, t0:t0 + BLK], kdb[:, h * BLK:(h + 1) * BLK]),
                    reads=[b_tk, b_kdb], writes=[b_kdx])

                def tgroup(q, tv=tv, t0=t0):
                    for cc in range(2):
                        q.transpose(pbf[:, cc * 128:(cc + 1) * 128],
                                    tv[:, t0 + cc * 128:t0 + (cc + 1) * 128],
                                    ident)
                    for cc in range(2):
                        r = q.transpose(pbf[:, 512 + cc * 128:
                                            512 + (cc + 1) * 128],
                                        kdx[:, cc * 128:(cc + 1) * 128], ident)
                    return r
                S.op("pe", tgroup, reads=[b_tv, b_kdx, b_id], writes=[b_pbf])
                S.op("dve", lambda q: q.tensor_copy(
                    vkm, pbf[:, 0:256]), reads=[b_pbf], writes=[b_vkm])
                S.op("dve", lambda q: q.tensor_copy(
                    kkm, pbf[:, 512:768]), reads=[b_pbf], writes=[b_kkm])
                for cc in range(2):
                    tp, b_tp = pk[4 + cc], b_pk[4 + cc]
                    S.op("pe", lambda q, tk=tk, tq=tq, t0=t0, cc=cc, tp=tp:
                         q.matmul(tp[:, 0:BLK],
                                  tk[:, t0 + cc * 128:t0 + (cc + 1) * 128],
                                  tq[:, t0:t0 + BLK], start=True, stop=True),
                         reads=[b_tk, b_tq], writes=[b_tp])
                    S.op("dve", lambda q, cc=cc, h=h, tp=tp: q.tensor_mul(
                        ptt[:, cc * BLK:(cc + 1) * BLK], tp[:, 0:BLK],
                        ddb[:, (2 * h + cc) * BLK:(2 * h + cc + 1) * BLK]),
                        reads=[b_tp, b_ddb], region_of=[b_ptt],
                        deps=[b_ptt.w])
                S.seal(b_ptt, [("dve", S.cnt["dve"])])
                tp, b_tp = pk[j % 2], b_pk[j % 2]

                def outg(q, tp=tp):
                    q.matmul(tp[:, 0:BLK], kvb, qdx, start=True, stop=False)
                    q.matmul(tp[:, 0:BLK], vkm[:, 0:128], ptt[:, 0:BLK],
                             start=False, stop=False)
                    return q.matmul(tp[:, 0:BLK], vkm[:, 128:256],
                                    ptt[:, BLK:2 * BLK], start=False,
                                    stop=True)
                S.op("pe", outg, reads=[b_kvb, b_qdx, b_vkm, b_ptt],
                     writes=[b_tp])
                ta, b_ta = aev[j % 2]
                S.op("act", lambda q, tp=tp, ta=ta: q.activation(
                    ta, tp[:, 0:BLK], AF.Copy), reads=[b_tp], writes=[b_ta])
                attnl_evs.append(S.dma(
                    attn_local[h * 128:(h + 1) * 128, t0:t0 + BLK], ta,
                    eng="pool", reads=[b_ta], region_of=[b_attnl]))
                tpv, b_tpv = pk[2 + (j % 2)], b_pk[2 + (j % 2)]

                def kvg(q, tpv=tpv):
                    q.matmul(tpv[:, 0:DH], kkm[:, 0:128], vkm[:, 0:128],
                             start=True, stop=False)
                    return q.matmul(tpv[:, 0:DH], kkm[:, 128:256],
                                    vkm[:, 128:256], start=False, stop=True)
                S.op("pe", kvg, reads=[b_kkm, b_vkm], writes=[b_tpv])
                S.op("dve", lambda q, h=h, tpv=tpv: q.scalar_tensor_tensor(
                    kvf, kvf, dblk[:, h:h + 1], tpv[:, 0:DH],
                    ALU.mult, ALU.add),
                    reads=[b_dblk, b_tpv, b_kvf], writes=[b_kvf])
                if j < NBLK - 1:
                    S.op("dve", lambda q: q.tensor_copy(kvb, kvf),
                         reads=[b_kvf], writes=[b_kvb])
            kvsum_evs.append(S.dma(kvsum[h * 128:(h + 1) * 128, :], kvf,
                                   eng="pool", reads=[b_kvf],
                                   region_of=[b_kvsum]))
        S.seal(b_attnl, attnl_evs)
        S.seal(b_qdecg, qdecg_evs)
        S.seal(b_kvsum, kvsum_evs)
        S.op("pool", lambda q: q.nop(),
             deps=[b_attnl.w, b_qdecg.w, b_gateo.w, b_kvsum.w])
        S.emit(block)
    return nc


def emit_resid_tm(S, bufs, delta_hbm, b_delta, xsrc, b_xs, xdst, b_xd, tag,
                  ntok=TPC):
    dst_evs = []
    dtm = [bufs.sb(f"st_b{i}", (128, D), BF16) for i in range(2)]
    xt = [bufs.sb(f"st_f{i}", (128, D), F32) for i in range(2)]
    for i in range(ntok // 128):
        bb = i % 2
        (tdt, b_tdt), (tx, b_tx) = dtm[bb], xt[bb]
        S.dma(tdt, delta_hbm[i * 128:(i + 1) * 128, :], reads=[b_delta],
              writes=[b_tdt])
        S.dma(tx, xsrc[i * 128:(i + 1) * 128, :], reads=[b_xs], writes=[b_tx])
        S.op("dve", lambda q, tx=tx, tdt=tdt: q.tensor_add(tx, tx, tdt),
             reads=[b_tdt, b_tx], writes=[b_tx])
        dst_evs.append(S.dma(xdst[i * 128:(i + 1) * 128, :], tx, eng="pool",
                             reads=[b_tx], region_of=[b_xd]))
    S.seal(b_xd, dst_evs)


def emit_dequant(S, bufs, wq, s_sb, b_ssb, scol, wbf_hbm, b_wbf, nrow_chunks,
                 width, tag):
    ti = [bufs.sb("dqi0", (128, 2048), I32)] * 2
    to = [bufs.sb("dqo0", (128, 2048), BF16)] * 2
    evs = []
    col_chunks = [(c0, min(2048, width - c0)) for c0 in range(0, width, 2048)]
    n = 0
    for i in range(nrow_chunks):
        for (c0, w) in col_chunks:
            bb = n % 2
            n += 1
            (tt, b_tt), (tb, b_tb) = ti[bb], to[bb]
            S.dma(tt[:, 0:w], wq[i * 128:(i + 1) * 128, c0:c0 + w],
                  writes=[b_tt])
            S.op("dve", lambda q, tt=tt, tb=tb, i=i, w=w: q.tensor_scalar(
                tb[:, 0:w], tt[:, 0:w], -128.0,
                s_sb[:, scol + i:scol + i + 1], ALU.add, ALU.mult),
                reads=[b_tt, b_ssb], writes=[b_tb])
            evs.append(S.dma(wbf_hbm[i * 128:(i + 1) * 128, c0:c0 + w],
                             tb[:, 0:w], eng="pool", reads=[b_tb],
                             region_of=[b_wbf]))
    S.seal(b_wbf, evs)


def emit_col_load(S, bufs, name, vec_hbm, ncol, parts=128):
    t, b = bufs.sb(name, (parts, ncol), F32)
    evs = []
    for c in range(ncol):
        evs.append(S.dma(t[:, c:c + 1],
                         vec_hbm[c * parts:(c + 1) * parts].rearrange(
                             "(p o) -> p o", o=1),
                         region_of=[b]))
    S.seal(b, evs)
    return t, b


def emit_cast_pass(S, bufs, src_hbm, dst_hbm, b_dst, nrow_chunks, width, tag):
    ti = [bufs.sb("st_f0", (128, D), F32)] * 2
    to = [bufs.sb(f"cpo{tag}0", (128, width), BF16)] * 2
    evs = []
    for i in range(nrow_chunks):
        bb = i % 2
        (tt, b_tt), (tb, b_tb) = ti[bb], to[bb]
        S.dma(tt, src_hbm[i * 128:(i + 1) * 128, :], writes=[b_tt])
        S.op("dve", lambda q, tt=tt, tb=tb: q.tensor_copy(tb, tt),
             reads=[b_tt], writes=[b_tb])
        evs.append(S.dma(dst_hbm[i * 128:(i + 1) * 128, :], tb, eng="pool",
                         reads=[b_tb], region_of=[b_dst]))
    S.seal(b_dst, evs)


def emit_ffn(S, bufs, arena, pk, b_pk, x_hbm, b_x, xdst, b_xd,
             wg_bf, b_wg, wu_bf, b_wu, wd_bf, b_wd, delta_hbm, b_delta,
             ident, b_id, tag, TG=512):
    NT = TG // 512
    NJB = HID // 512
    C_H = ND * TG
    C_ST = C_H + NH * TG
    delta_evs = []
    dev = [bufs.sb(f"fdev{i}", (128, 512), BF16) for i in range(3)]
    for g in range(TPC // TG):
        ln_f, b_lnf = arena.view(f"lnf{tag}{g}", 0, C_H)
        emit_ln(S, bufs, x_hbm, b_x, ln_f, b_lnf, g * TG, TG, f"f{tag}{g}",
                ident, b_id)
        h_slab, b_h = arena.view(f"hsl{tag}{g}", C_H, C_ST)
        wst = [arena.view(f"wst{tag}{g}{i}", C_ST + i * ND * 512,
                          C_ST + (i + 1) * ND * 512) for i in range(2)]
        for which, wsrc, b_ws in (("g", wg_bf, b_wg), ("u", wu_bf, b_wu)):
            for jb in range(NJB):
                bb = jb % 2
                tw, b_tw = wst[bb]
                wevs = [S.dmaT(tw[:, dc * 512:(dc + 1) * 512],
                               wsrc[jb * 512:(jb + 1) * 512,
                                    dc * 128:(dc + 1) * 128],
                               reads=[b_ws], region_of=[b_tw])
                        for dc in range(ND)]
                S.seal(b_tw, wevs)
                for j4 in range(4):
                    jc = jb * 4 + j4
                    for nt in range(NT):
                        k = (jc * NT + nt) % 4
                        tp, b_tp = pk[k], b_pk[k]

                        def mmg(q, tw=tw, j4=j4, nt=nt, tp=tp, ln_f=ln_f):
                            r = None
                            for dc in range(ND):
                                r = q.matmul(
                                    tp,
                                    tw[:, dc * 512 + j4 * 128:
                                       dc * 512 + j4 * 128 + 128],
                                    ln_f[:, dc * TG + nt * 512:
                                         dc * TG + (nt + 1) * 512],
                                    start=(dc == 0), stop=(dc == ND - 1))
                            return r
                        S.op("pe", mmg, reads=[b_tw, b_lnf], writes=[b_tp])
                        hcols = h_slab[:, jc * TG + nt * 512:
                                       jc * TG + (nt + 1) * 512]
                        if which == "g":
                            S.op("act", lambda q, tp=tp, hcols=hcols:
                                 q.activation(hcols, tp, AF.Silu),
                                 reads=[b_tp], region_of=[b_h])
                        else:
                            S.op("dve", lambda q, tp=tp, hcols=hcols:
                                 q.tensor_mul(hcols, hcols, tp),
                                 reads=[b_tp], region_of=[b_h])
            S.seal(b_h, [("act" if which == "g" else "dve",
                          S.cnt["act" if which == "g" else "dve"])])
        wsd, b_wsd = arena.view(f"wsd{tag}{g}", C_ST, C_ST + NH * 512)
        for dmb in range(4):
            wevs = [S.dmaT(wsd[:, jc * 512:(jc + 1) * 512],
                           wd_bf[dmb * 512:(dmb + 1) * 512,
                                 jc * 128:(jc + 1) * 128],
                           reads=[b_wd], region_of=[b_wsd])
                    for jc in range(NH)]
            S.seal(b_wsd, wevs)
            for tc in range(TG // 128):
                k = 4 + (tc % 2)
                tp, b_tp = pk[k], b_pk[k]

                def mmd(q, tc=tc, tp=tp, h_slab=h_slab, wsd=wsd):
                    r = None
                    for jc in range(NH):
                        r = q.matmul(
                            tp, h_slab[:, jc * TG + tc * 128:
                                       jc * TG + tc * 128 + 128],
                            wsd[:, jc * 512:(jc + 1) * 512],
                            start=(jc == 0), stop=(jc == NH - 1))
                    return r
                S.op("pe", mmd, reads=[b_h, b_wsd], writes=[b_tp])
                te, b_te = dev[tc % 3]
                S.op("act", lambda q, tp=tp, te=te: q.activation(
                    te, tp, AF.Copy), reads=[b_tp], writes=[b_te])
                delta_evs.append(S.dma(
                    delta_hbm[g * TG + tc * 128:g * TG + (tc + 1) * 128,
                              dmb * 512:(dmb + 1) * 512],
                    te, eng="pool", reads=[b_te], region_of=[b_delta]))
    S.seal(b_delta, delta_evs)
    emit_resid_tm(S, bufs, delta_hbm, b_delta, x_hbm, b_x, xdst, b_xd,
                  f"f{tag}")


def build_p2():
    nc = bass.Bass("TRN2", target_bir_lowering=False)
    x = nc.dram_tensor("x", [TPC, D], F32, kind="ExternalInput").ap()
    attn_l = nc.dram_tensor("attn_local", [D, TPC], BF16,
                            kind="ExternalInput").ap()
    qdec_g = nc.dram_tensor("qdec_g", [D, TPC], BF16,
                            kind="ExternalInput").ap()
    gate_o = nc.dram_tensor("gate_o", [D, TPC], BF16,
                            kind="ExternalInput").ap()
    kvpre = nc.dram_tensor("kvpre", [3 * D, DH], F32,
                           kind="ExternalInput").ap()
    slope = nc.dram_tensor("slope", [1, H], F32, kind="ExternalInput").ap()
    w_out = nc.dram_tensor("w_out", [D, D], F32, kind="ExternalInput").ap()
    norm_w = nc.dram_tensor("norm_w", [D], F32, kind="ExternalInput").ap()
    wq_g = nc.dram_tensor("wq_gate", [HID, D], I32, kind="ExternalInput").ap()
    s_g = nc.dram_tensor("s_gate", [HID], F32, kind="ExternalInput").ap()
    wq_u = nc.dram_tensor("wq_up", [HID, D], I32, kind="ExternalInput").ap()
    s_u = nc.dram_tensor("s_up", [HID], F32, kind="ExternalInput").ap()
    wq_d = nc.dram_tensor("wq_down", [D, HID], I32, kind="ExternalInput").ap()
    s_d = nc.dram_tensor("s_down", [D], F32, kind="ExternalInput").ap()
    w_dkv = nc.dram_tensor("w_dkv", [D, DC], F32, kind="ExternalInput").ap()
    w_dq = nc.dram_tensor("w_dq", [D, DC], F32, kind="ExternalInput").ap()
    w_uk = nc.dram_tensor("w_uk", [DC, D], F32, kind="ExternalInput").ap()
    w_uv = nc.dram_tensor("w_uv", [DC, D], F32, kind="ExternalInput").ap()
    w_uq = nc.dram_tensor("w_uq", [DC, D], F32, kind="ExternalInput").ap()
    w_kr = nc.dram_tensor("w_kr", [D, DR], F32, kind="ExternalInput").ap()
    w_qr = nc.dram_tensor("w_qr", [DC, H * DR], F32,
                          kind="ExternalInput").ap()
    consts = {n: nc.dram_tensor(n, list(s), d, kind="ExternalInput").ap()
              for n, s, d in CONST_SPECS}
    x2 = nc.dram_tensor("x2", [TPC, D], F32, kind="ExternalOutput").ap()
    kv_pub = nc.dram_tensor("kv_pub", [2 * D + DR, TPC], BF16,
                            kind="ExternalOutput").ap()
    qf_pub = nc.dram_tensor("qf_pub", [D + H * DR, TPC], BF16,
                            kind="ExternalOutput").ap()
    x1 = nc.dram_tensor("x1", [TPC, D], F32, kind="ExternalOutput").ap()
    wo_bf = nc.dram_tensor("wo_bf", [D, D], BF16).ap()
    wg_bf = nc.dram_tensor("wg_bf", [HID, D], BF16).ap()
    wu_bf = nc.dram_tensor("wu_bf", [HID, D], BF16).ap()
    wd_bf = nc.dram_tensor("wd_bf", [D, HID], BF16).ap()
    u3_bf = nc.dram_tensor("u3_bf", [3 * DC, D], BF16).ap()
    delta_a = nc.dram_tensor("delta_a", [TPC, D], BF16).ap()
    delta_f = nc.dram_tensor("delta_f", [TPC, D], BF16).ap()

    S = Sched(nc)
    bufs = Bufs(nc, S)
    arena = Arena(nc, S, "arena", 54272, BF16)
    arf = Arena(nc, S, "arf", 3584, F32)
    b_x, b_x1, b_x2 = S.buf("x"), S.buf("x1"), S.buf("x2")
    b_wo, b_wg, b_wu, b_wd = (S.buf("wo_bf"), S.buf("wg_bf"),
                              S.buf("wu_bf"), S.buf("wd_bf"))
    b_u3 = S.buf("u3_bf")
    b_da, b_df = S.buf("delta_a"), S.buf("delta_f")
    b_kvp, b_qfp = S.buf("kv_pub"), S.buf("qf_pub")

    with nc.Block() as block:
        (ident, b_id), (ones1, b_o1), (onesb, b_ob) = emit_setup(S, bufs)
        pk, b_pk = [], []
        for i in range(6):
            t, b = bufs.ps(f"pk{i}", (128, 512), F32)
            pk.append(t)
            b_pk.append(b)
        bufs.ps("pbf", (128, 1024), BF16)

        emit_cast_pass(S, bufs, w_out, wo_bf, b_wo, ND, D, "wo")
        u3_evs = []
        for wi, wsrc in enumerate((w_uk, w_uv, w_uq)):
            for cc in range(4):
                (tt, b_tt) = bufs.tiles["st_f0"]
                (tb, b_tb) = bufs.tiles["cpowo0"]
                S.dma(tt, wsrc[cc * 128:(cc + 1) * 128, :], writes=[b_tt])
                S.op("dve", lambda q, tt=tt, tb=tb: q.tensor_copy(tb, tt),
                     reads=[b_tt], writes=[b_tb])
                u3_evs.append(S.dma(
                    u3_bf[wi * DC + cc * 128:wi * DC + (cc + 1) * 128, :],
                    tb, eng="pool", reads=[b_tb], region_of=[b_u3]))
        S.seal(b_u3, u3_evs)
        sgu, b_sgu = emit_col_load(S, bufs, "sgu", s_g, NH)
        suu, b_suu = emit_col_load(S, bufs, "suu", s_u, NH)
        sdd, b_sdd = emit_col_load(S, bufs, "sdd", s_d, ND)
        emit_dequant(S, bufs, wq_g, sgu, b_sgu, 0, wg_bf, b_wg, NH, D, "g")
        emit_dequant(S, bufs, wq_u, suu, b_suu, 0, wu_bf, b_wu, NH, D, "u")
        emit_dequant(S, bufs, wq_d, sdd, b_sdd, 0, wd_bf, b_wd, ND, HID, "d")

        sl_row, b_slr = bufs.sb("sl_row", (1, H), F32)
        S.dma(sl_row, slope, writes=[b_slr])
        sb_s, b_ss = emit_bcast_row(S, bufs, pk[0], b_pk[0], ones1, b_o1,
                                    sl_row, b_slr, "sb_s", H)
        di_row, b_dir = bufs.sb("di_row", (1, 3), F32)
        S.dma(di_row, consts["dist"], writes=[b_dir])
        d_b, b_db = emit_bcast_row(S, bufs, pk[1], b_pk[1], ones1, b_o1,
                                   di_row, b_dir, "d_b", 3)
        wpb, b_wpb = bufs.sb("wpb", (128, 3 * H), F32)
        for p in range(3):
            S.op("dve", lambda q, p=p: q.tensor_scalar_mul(
                wpb[:, p * H:(p + 1) * H], sb_s, d_b[:, p:p + 1]),
                reads=[b_ss, b_db], region_of=[b_wpb])
        S.seal(b_wpb, [("dve", S.cnt["dve"])])
        S.op("act", lambda q: q.activation(wpb, wpb, AF.Exp, scale=-1.0),
             reads=[b_wpb], writes=[b_wpb])
        nw, b_nw = emit_col_load(S, bufs, "nw", norm_w, ND)

        slab, b_slab = arena.view("slab", 0, ND * TPC)
        kvp = [bufs.sb(f"kvp{i}", (128, 3 * DH), F32) for i in range(2)]
        kvs, b_kvs = bufs.sb("kvs", (128, DH), F32)
        kvsb, b_kvsb = bufs.sb("kvsb", (128, DH), BF16)
        qdt = [bufs.sb(f"st_b{i}", (128, TPC), BF16) for i in range(2)]
        att = [bufs.sb(f"st_b{2 + i}", (128, TPC), BF16) for i in range(2)]
        sqt, b_sqt = bufs.sb("sqt", (128, TPC), BF16)
        eps_r, b_epsr = bufs.const(RMS_EPS, parts=1)
        for h in range(H):
            bb = h % 2
            (tkv, b_tkv), (tqd, b_tqd), (tat, b_tat) = kvp[bb], qdt[bb], att[bb]
            kvl_evs = []
            for p in range(3):
                kvl_evs.append(S.dma(
                    tkv[:, p * DH:(p + 1) * DH],
                    kvpre[p * D + h * 128:p * D + (h + 1) * 128, :],
                    region_of=[b_tkv]))
            S.seal(b_tkv, kvl_evs)
            S.op("dve", lambda q: q.memset(kvs, 0.0), writes=[b_kvs])
            for p in range(3):
                S.op("dve", lambda q, p=p, h=h, tkv=tkv: q.scalar_tensor_tensor(
                    kvs, tkv[:, p * DH:(p + 1) * DH],
                    wpb[:, p * H + h:p * H + h + 1], kvs, ALU.mult, ALU.add),
                    reads=[b_tkv, b_wpb, b_kvs], writes=[b_kvs])
            S.op("dve", lambda q: q.tensor_copy(kvsb, kvs),
                 reads=[b_kvs], writes=[b_kvsb])
            S.dma(tqd, qdec_g[h * 128:(h + 1) * 128, :], writes=[b_tqd])
            S.dma(tat, attn_l[h * 128:(h + 1) * 128, :], writes=[b_tat])
            for nt in range(4):
                tp, b_tp = pk[nt % 2], b_pk[nt % 2]
                S.op("pe", lambda q, tqd=tqd, nt=nt, tp=tp: q.matmul(
                    tp, kvsb, tqd[:, nt * 512:(nt + 1) * 512],
                    start=True, stop=True),
                    reads=[b_kvsb, b_tqd], writes=[b_tp])
                scols = slab[:, h * TPC + nt * 512:h * TPC + (nt + 1) * 512]
                S.op("dve", lambda q, tp=tp, scols=scols, tat=tat, nt=nt:
                     q.tensor_add(scols, tat[:, nt * 512:(nt + 1) * 512], tp),
                     reads=[b_tp, b_tat], region_of=[b_slab])
            S.seal(b_slab, [("dve", S.cnt["dve"])] if h == H - 1 else
                   list(_evs(b_slab.w)) + [("dve", S.cnt["dve"])])
            S.op("act", lambda q, h=h: q.activation(
                sqt, slab[:, h * TPC:(h + 1) * TPC], AF.Square),
                reads=[b_slab], writes=[b_sqt])
            for mt in range(4):
                tp, b_tp = pk[2 + mt], b_pk[2 + mt]
                S.op("pe", lambda q, mt=mt, tp=tp, h=h: q.matmul(
                    tp[0:1, :], onesb, sqt[:, mt * 512:(mt + 1) * 512],
                    start=(h == 0), stop=(h == H - 1)),
                    reads=[b_ob, b_sqt],
                    writes=[b_tp] if h == H - 1 else [],
                    region_of=[] if h == H - 1 else [b_tp])
        r_b, b_rb = arf.view("r_b", 0, TPC)
        rr, b_rr = r_b[0:1, :], S.buf("rr")
        for mt in range(4):
            tp, b_tp = pk[2 + mt], b_pk[2 + mt]
            S.op("act", lambda q, tp=tp, mt=mt: q.activation(
                rr[:, mt * 512:(mt + 1) * 512], tp[0:1, :], AF.Ln,
                scale=1.0 / D, bias=eps_r), reads=[b_tp, b_epsr],
                region_of=[b_rr])
        S.seal(b_rr, [("act", S.cnt["act"])])
        S.op("act", lambda q: q.activation(rr, rr, AF.Exp, scale=-0.5),
             reads=[b_rr], writes=[b_rr])
        rb_evs = []
        for mt in range(4):
            tp, b_tp = pk[mt % 2], b_pk[mt % 2]
            S.op("pe", lambda q, tp=tp, mt=mt: q.matmul(
                tp, ones1, rr[:, mt * 512:(mt + 1) * 512],
                start=True, stop=True), reads=[b_o1, b_rr], writes=[b_tp])
            rb_evs.append(S.op("dve", lambda q, tp=tp, mt=mt: q.tensor_copy(
                r_b[:, mt * 512:(mt + 1) * 512], tp), reads=[b_tp],
                region_of=[b_rb]))
        S.seal(b_rb, rb_evs)
        gt = [bufs.sb(f"gt{i}", (128, TPC), BF16) for i in range(2)]
        t1, b_t1 = bufs.sb("t1y", (128, TPC), BF16)
        for h in range(H):
            bb = h % 2
            (tg, b_tg) = gt[bb]
            S.dma(tg, gate_o[h * 128:(h + 1) * 128, :], writes=[b_tg])
            S.op("dve", lambda q, h=h: q.scalar_tensor_tensor(
                t1, slab[:, h * TPC:(h + 1) * TPC], nw[:, h:h + 1], r_b,
                ALU.mult, ALU.mult), reads=[b_slab, b_nw, b_rb],
                writes=[b_t1])
            S.op("dve", lambda q, h=h, tg=tg: q.tensor_mul(
                slab[:, h * TPC:(h + 1) * TPC], t1, tg),
                reads=[b_t1, b_tg], region_of=[b_slab])
        S.seal(b_slab, [("dve", S.cnt["dve"])])

        woT = [arena.view(f"woT{i}", ND * TPC + i * ND * 512,
                          ND * TPC + (i + 1) * ND * 512) for i in range(2)]
        dev = [bufs.sb(f"fdev{i}", (128, 512), BF16) for i in range(3)]
        da_evs = []
        for dmb in range(4):
            tw, b_tw = woT[dmb % 2]
            wevs = [S.dmaT(tw[:, jc * 512:(jc + 1) * 512],
                           wo_bf[dmb * 512:(dmb + 1) * 512,
                                 jc * 128:(jc + 1) * 128],
                           reads=[b_wo], region_of=[b_tw])
                    for jc in range(ND)]
            S.seal(b_tw, wevs)
            for tc in range(TPC // 128):
                tp, b_tp = pk[4 + tc % 2], b_pk[4 + tc % 2]

                def mmw(q, tc=tc, tp=tp, tw=tw):
                    r = None
                    for jc in range(ND):
                        r = q.matmul(
                            tp, slab[:, jc * TPC + tc * 128:
                                     jc * TPC + tc * 128 + 128],
                            tw[:, jc * 512:(jc + 1) * 512],
                            start=(jc == 0), stop=(jc == ND - 1))
                    return r
                S.op("pe", mmw, reads=[b_slab, b_tw], writes=[b_tp])
                te, b_te = dev[tc % 3]
                S.op("act", lambda q, tp=tp, te=te: q.activation(
                    te, tp, AF.Copy), reads=[b_tp], writes=[b_te])
                da_evs.append(S.dma(
                    delta_a[tc * 128:(tc + 1) * 128,
                            dmb * 512:(dmb + 1) * 512],
                    te, eng="pool", reads=[b_te], region_of=[b_da]))
        S.seal(b_da, da_evs)
        emit_resid_tm(S, bufs, delta_a, b_da, x, b_x, x1, b_x1, "a")

        emit_ffn(S, bufs, arena, pk, b_pk, x1, b_x1, x2, b_x2,
                 wg_bf, b_wg, wu_bf, b_wu, wd_bf, b_wd, delta_f, b_df,
                 ident, b_id, "1")

        C_LN, C_DKV, C_DQ = 0, 8192, 16384
        C_UP = 24576
        C_KR, C_QR = 40960, 41984
        C_CKV, C_CQ = 46080, 48128
        C_QRS = 50176
        dkv_sb, b_dkvs = arena.view("dkv_sb", C_DKV, C_DKV + 8192)
        dq_sb, b_dqs = arena.view("dq_sb", C_DQ, C_DQ + 8192)
        kr_sb_w, b_krs = arena.view("kr_sb_w", C_KR, C_KR + 1024)
        qr_sb_w, b_qrs = arena.view("qr_sb_w", C_QR, C_QR + 4096)
        for dc in range(ND):
            (tt, b_tt) = bufs.tiles["st_f0"]
            (tb, b_tb) = bufs.tiles["cpowo0"]
            S.dma(tt[:, 0:DC], w_dkv[dc * 128:(dc + 1) * 128, :],
                  writes=[b_tt])
            S.op("dve", lambda q, tt=tt, tb=tb: q.tensor_copy(
                tb[:, 0:DC], tt[:, 0:DC]), reads=[b_tt], writes=[b_tb])
            S.op("dve", lambda q, tb=tb, dc=dc: q.tensor_copy(
                dkv_sb[:, dc * 512:(dc + 1) * 512], tb[:, 0:DC]),
                reads=[b_tb], region_of=[b_dkvs])
            S.dma(tt[:, 0:DC], w_dq[dc * 128:(dc + 1) * 128, :],
                  writes=[b_tt])
            S.op("dve", lambda q, tt=tt, tb=tb: q.tensor_copy(
                tb[:, 0:DC], tt[:, 0:DC]), reads=[b_tt], writes=[b_tb])
            S.op("dve", lambda q, tb=tb, dc=dc: q.tensor_copy(
                dq_sb[:, dc * 512:(dc + 1) * 512], tb[:, 0:DC]),
                reads=[b_tb], region_of=[b_dqs])
            S.dma(tt[:, 0:DR], w_kr[dc * 128:(dc + 1) * 128, :],
                  writes=[b_tt])
            S.op("dve", lambda q, tt=tt, dc=dc: q.tensor_copy(
                kr_sb_w[:, dc * 64:(dc + 1) * 64], tt[:, 0:DR]),
                reads=[b_tt], region_of=[b_krs])
        S.seal(b_dkvs, [("dve", S.cnt["dve"])])
        S.seal(b_dqs, [("dve", S.cnt["dve"])])
        S.seal(b_krs, [("dve", S.cnt["dve"])])
        for cc in range(4):
            (tt, b_tt) = bufs.tiles["st_f0"]
            S.dma(tt[:, 0:H * DR], w_qr[cc * 128:(cc + 1) * 128, :],
                  writes=[b_tt])
            dstv = qr_sb_w[:, cc * 1024:(cc + 1) * 1024].rearrange(
                "p (two h f) -> p two h f", two=2, h=H, f=DR // 2)
            srcv = tt[:, 0:H * DR].rearrange(
                "p (h two f) -> p two h f", h=H, two=2, f=DR // 2)
            S.op("dve", lambda q, dstv=dstv, srcv=srcv: q.tensor_copy(
                dstv, srcv), reads=[b_tt], region_of=[b_qrs])
        S.seal(b_qrs, [("dve", S.cnt["dve"])])
        pos_row, b_pos = bufs.sb("pos_row", (1, TPC), F32)
        S.dma(pos_row, consts["positions"], writes=[b_pos])
        invf, b_invf = bufs.sb("invf", (128, 1), F32)
        S.dma(invf, consts["invfreq4"], writes=[b_invf])
        c_n2pi, b_n2pi = bufs.const(-2.0 * np.pi)
        c_hpi, b_hpi = bufs.const(0.5 * np.pi)
        TGC = 512
        kv_evs, qf_evs = [], []
        upw = [arena.view(f"upw{i}", C_UP + i * 8192, C_UP + (i + 1) * 8192)
               for i in range(2)]
        for g in range(TPC // TGC):
            lnc, b_lnc = arena.view(f"lnc{g}", C_LN, C_LN + ND * TGC)
            emit_ln(S, bufs, x2, b_x2, lnc, b_lnc, g * TGC, TGC, f"c{g}",
                    ident, b_id)
            ckv, b_ckv = arena.view(f"ckv{g}", C_CKV, C_CKV + 2048)
            cq, b_cq = arena.view(f"cq{g}", C_CQ, C_CQ + 2048)
            ev3 = [bufs.sb(f"evc{i}", (128, 512), BF16) for i in range(3)]
            for (dst, b_dst, wsb, b_wsb) in ((ckv, b_ckv, dkv_sb, b_dkvs),
                                             (cq, b_cq, dq_sb, b_dqs)):
                cevs = []
                for cc2 in range(4):
                    tp, b_tp = pk[cc2 % 2], b_pk[cc2 % 2]

                    def mmc(q, cc2=cc2, tp=tp, wsb=wsb, lnc=lnc):
                        r = None
                        for dc in range(ND):
                            r = q.matmul(
                                tp, wsb[:, dc * 512 + cc2 * 128:
                                        dc * 512 + cc2 * 128 + 128],
                                lnc[:, dc * TGC:(dc + 1) * TGC],
                                start=(dc == 0), stop=(dc == ND - 1))
                        return r
                    S.op("pe", mmc, reads=[b_wsb, b_lnc], writes=[b_tp])
                    cevs.append(S.op(
                        "act", lambda q, tp=tp, dst=dst, cc2=cc2:
                        q.activation(dst[:, cc2 * 512:(cc2 + 1) * 512], tp,
                                     AF.Copy), reads=[b_tp],
                        region_of=[b_dst]))
                S.seal(b_dst, cevs)
            for wi, which in ((0, "K"), (1, "V"), (2, "Q")):
                tw, b_tw = upw[wi % 2]
                wevs = [S.dma(tw[:, cc2 * 2048:(cc2 + 1) * 2048],
                              u3_bf[wi * DC + cc2 * 128:
                                    wi * DC + (cc2 + 1) * 128, :],
                              reads=[b_u3], region_of=[b_tw])
                        for cc2 in range(4)]
                S.seal(b_tw, wevs)
                if which in ("K", "Q"):
                    src, b_src = (ckv, b_ckv) if which == "K" else (cq, b_cq)
                    for ec in range(ND):
                        tp, b_tp = pk[2 + ec % 2], b_pk[2 + ec % 2]

                        def mmk(q, ec=ec, tp=tp, tw=tw, src=src):
                            r = None
                            for cc2 in range(4):
                                r = q.matmul(
                                    tp, tw[:, cc2 * 2048 + ec * 128:
                                           cc2 * 2048 + ec * 128 + 128],
                                    src[:, cc2 * 512:(cc2 + 1) * 512],
                                    start=(cc2 == 0), stop=(cc2 == 3))
                            return r
                        S.op("pe", mmk, reads=[b_tw, b_src], writes=[b_tp])
                        te, b_te = ev3[ec % 3]
                        S.op("act", lambda q, tp=tp, te=te: q.activation(
                            te, tp, AF.Copy), reads=[b_tp], writes=[b_te])
                        dst = kv_pub if which == "K" else qf_pub
                        evl = kv_evs if which == "K" else qf_evs
                        evl.append(S.dma(
                            dst[ec * 128:(ec + 1) * 128,
                                g * TGC:(g + 1) * TGC], te, eng="pool",
                            reads=[b_te],
                            region_of=[b_kvp if which == "K" else b_qfp]))
                else:
                    for tc in range(4):
                        for eb in range(4):
                            tp, b_tp = pk[2 + eb % 2], b_pk[2 + eb % 2]

                            def mmv(q, tc=tc, eb=eb, tp=tp, tw=tw, ckv=ckv):
                                r = None
                                for cc2 in range(4):
                                    r = q.matmul(
                                        tp, ckv[:, cc2 * 512 + tc * 128:
                                                cc2 * 512 + tc * 128 + 128],
                                        tw[:, cc2 * 2048 + eb * 512:
                                           cc2 * 2048 + (eb + 1) * 512],
                                        start=(cc2 == 0), stop=(cc2 == 3))
                                return r
                            S.op("pe", mmv, reads=[b_tw, b_ckv],
                                 writes=[b_tp])
                            te, b_te = ev3[eb % 3]
                            S.op("act", lambda q, tp=tp, te=te: q.activation(
                                te, tp, AF.Copy), reads=[b_tp], writes=[b_te])
                            kv_evs.append(S.dma(
                                kv_pub[D + g * TGC + tc * 128:
                                       D + g * TGC + (tc + 1) * 128,
                                       eb * 512:(eb + 1) * 512], te,
                                eng="pool", reads=[b_te], region_of=[b_kvp]))
            qrp, b_qrp = arena.view(f"qrp{g}", C_QRS, C_QRS + 8 * TGC)
            qrevs = []
            for rc in range(8):
                tp, b_tp = pk[4 + rc % 2], b_pk[4 + rc % 2]

                def mmr(q, rc=rc, tp=tp, cq=cq):
                    r = None
                    for cc2 in range(4):
                        r = q.matmul(
                            tp, qr_sb_w[:, cc2 * 1024 + rc * 128:
                                        cc2 * 1024 + rc * 128 + 128],
                            cq[:, cc2 * 512:(cc2 + 1) * 512],
                            start=(cc2 == 0), stop=(cc2 == 3))
                    return r
                S.op("pe", mmr, reads=[b_qrs, b_cq], writes=[b_tp])
                qrevs.append(S.op("act", lambda q, tp=tp, rc=rc: q.activation(
                    qrp[:, rc * TGC:(rc + 1) * TGC], tp, AF.Copy),
                    reads=[b_tp], region_of=[b_qrp]))
            S.seal(b_qrp, qrevs)
            krp, b_krp = bufs.sb("krp", (64, TGC), BF16)

            def mmkr(q, lnc=lnc):
                r = None
                for dc in range(ND):
                    r = q.matmul(pk[0][0:64, :],
                                 kr_sb_w[:, dc * 64:(dc + 1) * 64],
                                 lnc[:, dc * TGC:(dc + 1) * TGC],
                                 start=(dc == 0), stop=(dc == ND - 1))
                return r
            S.op("pe", mmkr, reads=[b_krs, b_lnc], writes=[b_pk[0]])
            S.op("act", lambda q: q.activation(krp, pk[0][0:64, :], AF.Copy),
                 reads=[b_pk[0]], writes=[b_krp])
            pb, b_pb = arf.view(f"pb{g}", 0, TGC)
            S.op("pe", lambda q, g=g: q.matmul(
                pk[1], ones1, pos_row[:, g * TGC:(g + 1) * TGC],
                start=True, stop=True), reads=[b_o1, b_pos],
                writes=[b_pk[1]])
            ang, b_ang = arf.view(f"ang{g}", TGC, 2 * TGC)
            S.op("dve", lambda q: q.tensor_copy(pb, pk[1]), reads=[b_pk[1]],
                 writes=[b_pb])
            S.op("dve", lambda q: q.tensor_scalar_mul(ang, pb, invf),
                 reads=[b_pb, b_invf], writes=[b_ang])
            ta, b_ta = arf.view(f"rota{g}", 5 * TGC, 6 * TGC)
            tb2, b_tb2 = arf.view(f"rotb{g}", 6 * TGC, 7 * TGC)
            sc_t = {}
            tki, b_tki = bufs.sb("tkir", (128, TGC), I32)
            for trig in ("sin", "cos"):
                tt1, b_tt1 = arf.view(f"tt1{g}{trig}", 2 * TGC, 3 * TGC)
                off_t = 3 * TGC if trig == "sin" else 4 * TGC
                tst, b_tst = arf.view(f"tst{g}{trig}", off_t, off_t + TGC)
                off = 0.0 if trig == "sin" else 0.5 * np.pi
                S.op("dve", lambda q, tt1=tt1, off=off: q.tensor_scalar(
                    tt1, ang, off, 1.0 / (2.0 * np.pi), ALU.add, ALU.mult),
                    reads=[b_ang], writes=[b_tt1])
                S.op("dve", lambda q, tt1=tt1: q.tensor_copy(tki, tt1),
                     reads=[b_tt1], writes=[b_tki])
                S.op("dve", lambda q, tt1=tt1: q.tensor_copy(tt1, tki),
                     reads=[b_tki], writes=[b_tt1])
                S.op("dve", lambda q, tt1=tt1, tst=tst: q.scalar_tensor_tensor(
                    tst, tt1, c_n2pi, ang, ALU.mult, ALU.add),
                    reads=[b_tt1, b_n2pi, b_ang], writes=[b_tst])
                S.op("dve", lambda q, tst=tst, off=off: q.tensor_single_scalar(
                    tb2, tst, float(np.pi) - off, ALU.is_gt),
                    reads=[b_tst], writes=[b_tb2])
                S.op("dve", lambda q, tst=tst: q.scalar_tensor_tensor(
                    tst, tb2, c_n2pi, tst, ALU.mult, ALU.add),
                    reads=[b_tb2, b_n2pi, b_tst], writes=[b_tst])
                if trig == "sin":
                    S.op("act", lambda q, tst=tst: q.activation(
                        tst, tst, AF.Sin), reads=[b_tst], writes=[b_tst])
                else:
                    S.op("act", lambda q, tst=tst: q.activation(
                        tst, tst, AF.Sin, bias=c_hpi),
                        reads=[b_tst, b_hpi], writes=[b_tst])
                sc_t[trig] = (tst, b_tst)
            (tsin, b_tsin), (tcos, b_tcos) = sc_t["sin"], sc_t["cos"]
            rot, b_rot = arena.view(f"rot{g}", C_CKV, C_CKV + 8 * TGC)
            rot_evs = []
            for c in range(4):
                x1c = qrp[:, c * TGC:(c + 1) * TGC]
                x2c = qrp[:, (4 + c) * TGC:(5 + c) * TGC]
                S.op("dve", lambda q, x1c=x1c: q.tensor_mul(ta, x1c, tcos),
                     reads=[b_qrp, b_tcos], writes=[b_ta])
                S.op("dve", lambda q, x2c=x2c: q.tensor_mul(tb2, x2c, tsin),
                     reads=[b_qrp, b_tsin], writes=[b_tb2])
                rot_evs.append(S.op(
                    "dve", lambda q, c=c: q.tensor_sub(
                        rot[:, c * TGC:(c + 1) * TGC], ta, tb2),
                    reads=[b_ta, b_tb2], region_of=[b_rot]))
                S.op("dve", lambda q, x1c=x1c: q.tensor_mul(ta, x1c, tsin),
                     reads=[b_qrp, b_tsin], writes=[b_ta])
                S.op("dve", lambda q, x2c=x2c: q.tensor_mul(tb2, x2c, tcos),
                     reads=[b_qrp, b_tcos], writes=[b_tb2])
                rot_evs.append(S.op(
                    "dve", lambda q, c=c: q.tensor_add(
                        rot[:, (4 + c) * TGC:(5 + c) * TGC], ta, tb2),
                    reads=[b_ta, b_tb2], region_of=[b_rot]))
            S.seal(b_rot, rot_evs)
            for h in range(H):
                c = h // 4
                r0 = (h % 4) * 32
                qf_evs.append(S.dma(
                    qf_pub[D + h * DR:D + h * DR + 32,
                           g * TGC:(g + 1) * TGC],
                    rot[r0:r0 + 32, c * TGC:(c + 1) * TGC], eng="pool",
                    reads=[b_rot], region_of=[b_qfp]))
                qf_evs.append(S.dma(
                    qf_pub[D + h * DR + 32:D + (h + 1) * DR,
                           g * TGC:(g + 1) * TGC],
                    rot[r0:r0 + 32, (4 + c) * TGC:(5 + c) * TGC], eng="pool",
                    reads=[b_rot], region_of=[b_qfp]))
            if "kr2" not in bufs.tiles:
                bufs.tiles["kr2"] = (sqt[0:32, 1024:1024 + TGC],
                                     S.alias_buf(b_sqt, "kr2"))
            kr2, b_kr2 = bufs.tiles["kr2"]
            S.op("dve", lambda q: q.tensor_copy(kr2, krp[32:64, :]),
                 reads=[b_krp], writes=[b_kr2])
            for part in range(2):
                ft, st = (tcos, tsin) if part == 0 else (tsin, tcos)
                S.op("dve", lambda q, ft=ft: q.tensor_mul(
                    ta[0:32, :], krp[0:32, :], ft[0:32, :]),
                    reads=[b_krp, b_tcos, b_tsin], writes=[b_ta])
                S.op("dve", lambda q, st=st: q.tensor_mul(
                    tb2[0:32, :], kr2, st[0:32, :]),
                    reads=[b_kr2, b_tcos, b_tsin], writes=[b_tb2])
                alu = ALU.subtract if part == 0 else ALU.add
                if f"krot{part}" not in bufs.tiles:
                    bufs.tiles[f"krot{part}"] = (
                        sqt[0:32, part * TGC:(part + 1) * TGC],
                        S.alias_buf(b_sqt, f"krot{part}"))
                tcb, b_tcb = bufs.tiles[f"krot{part}"]
                S.op("dve", lambda q, tcb=tcb, alu=alu: q.tensor_tensor(
                    tcb, ta[0:32, :], tb2[0:32, :], alu),
                    reads=[b_ta, b_tb2], writes=[b_tcb])
                kv_evs.append(S.dma(
                    kv_pub[2 * D + part * 32:2 * D + (part + 1) * 32,
                           g * TGC:(g + 1) * TGC], tcb, eng="pool",
                    reads=[b_tcb], region_of=[b_kvp]))
        S.seal(b_kvp, kv_evs)
        S.seal(b_qfp, qf_evs)
        S.op("pool", lambda q: q.nop(),
             deps=[b_kvp.w, b_qfp.w, b_x2.w])
        S.emit(block)
    return nc


def build_p3():
    nc = bass.Bass("TRN2", target_bir_lowering=False)
    x2 = nc.dram_tensor("x2", [TPC, D], F32, kind="ExternalInput").ap()
    kvg = nc.dram_tensor("kvg", [4 * (2 * D + DR), TPC], BF16,
                         kind="ExternalInput").ap()
    qf = nc.dram_tensor("qf", [D + H * DR, TPC], BF16,
                        kind="ExternalInput").ap()
    w_o = nc.dram_tensor("w_o", [D, D], F32, kind="ExternalInput").ap()
    b_o = nc.dram_tensor("b_o", [1, D], F32, kind="ExternalInput").ap()
    wq_g = nc.dram_tensor("wq_gate", [HID, D], I32, kind="ExternalInput").ap()
    s_g = nc.dram_tensor("s_gate", [HID], F32, kind="ExternalInput").ap()
    wq_u = nc.dram_tensor("wq_up", [HID, D], I32, kind="ExternalInput").ap()
    s_u = nc.dram_tensor("s_up", [HID], F32, kind="ExternalInput").ap()
    wq_d = nc.dram_tensor("wq_down", [D, HID], I32, kind="ExternalInput").ap()
    s_d = nc.dram_tensor("s_down", [D], F32, kind="ExternalInput").ap()
    xout = nc.dram_tensor("xout", [TPC, D], F32, kind="ExternalOutput").ap()
    x3 = nc.dram_tensor("x3", [TPC, D], F32).ap()
    wg_bf = nc.dram_tensor("wg_bf", [HID, D], BF16).ap()
    wu_bf = nc.dram_tensor("wu_bf", [HID, D], BF16).ap()
    wd_bf = nc.dram_tensor("wd_bf", [D, HID], BF16).ap()
    delta_f = nc.dram_tensor("delta_f", [TPC, D], BF16).ap()
    SLAB = 2 * D + DR

    S = Sched(nc)
    bufs = Bufs(nc, S)
    arena = Arena(nc, S, "arena", 65536, BF16)
    b_x2, b_x3, b_xout = S.buf("x2"), S.buf("x3"), S.buf("xout")
    b_wg, b_wu, b_wd = S.buf("wg_bf"), S.buf("wu_bf"), S.buf("wd_bf")
    b_df = S.buf("delta_f")

    with nc.Block() as block:
        (ident, b_id), (ones1, b_o1), (onesb, b_ob) = emit_setup(S, bufs)
        pk, b_pk = [], []
        for i in range(6):
            t, b = bufs.ps(f"pk{i}", (128, 512), F32)
            pk.append(t)
            b_pk.append(b)
        bufs.ps("pbf", (128, 1024), BF16)

        sgu, b_sgu = emit_col_load(S, bufs, "sgu", s_g, NH)
        suu, b_suu = emit_col_load(S, bufs, "suu", s_u, NH)
        sdd, b_sdd = emit_col_load(S, bufs, "sdd", s_d, ND)
        emit_dequant(S, bufs, wq_g, sgu, b_sgu, 0, wg_bf, b_wg, NH, D, "g")
        emit_dequant(S, bufs, wq_u, suu, b_suu, 0, wu_bf, b_wu, NH, D, "u")
        emit_dequant(S, bufs, wq_d, sdd, b_sdd, 0, wd_bf, b_wd, ND, HID, "d")

        C_CTX = 0
        C_KV = 32768
        ctx, b_ctx = arena.view("ctx", C_CTX, C_CTX + ND * TPC)
        krT, b_kr = arena.view("krT", C_KV, C_KV + 8192)
        ksb, b_ks = arena.view("ksb", C_KV + 8192, C_KV + 16384)
        vsb, b_vs = arena.view("vsb", C_KV + 16384, C_KV + 24576)
        qsb, b_qs = arena.view("qsb", C_KV + 24576, C_KV + 26624)
        qrs, b_qr = arena.view("qrs", C_KV + 26624, C_KV + 28672)
        psb = [bufs.sb(f"psb{i}", (128, 512), BF16) for i in range(2)]
        lrow, b_lr = bufs.sb("lrow", (1, 512), F32)
        lbs, b_lb = bufs.sb("lbs", (128, 512), F32)
        kr_evs = [S.dma(krT[0:64, s * TPC:(s + 1) * TPC],
                        kvg[s * SLAB + 2 * D:s * SLAB + 2 * D + DR, :],
                        region_of=[b_kr]) for s in range(4)]
        S.seal(b_kr, kr_evs)
        ctx_evs = []
        for h in range(H):
            kevs = [S.dma(ksb[:, s * TPC:(s + 1) * TPC],
                          kvg[s * SLAB + h * 128:s * SLAB + (h + 1) * 128, :],
                          region_of=[b_ks]) for s in range(4)]
            S.seal(b_ks, kevs)
            vevs = []
            for s in range(4):
                for kc in range(ND):
                    vevs.append(S.dma(
                        vsb[:, s * 2048 + kc * 128:s * 2048 + (kc + 1) * 128],
                        kvg[s * SLAB + D + kc * 128:
                            s * SLAB + D + (kc + 1) * 128,
                            h * 128:(h + 1) * 128],
                        region_of=[b_vs]))
            S.seal(b_vs, vevs)
            e_q = S.dma(qsb, qf[h * 128:(h + 1) * 128, :], writes=[b_qs])
            e_qr = S.dma(qrs[0:64, :], qf[D + h * DR:D + (h + 1) * DR, :],
                         writes=[b_qr])
            for mt in range(4):
                b_l = b_pk[4]
                for kc in range(64):
                    tp, b_tp = pk[kc % 2], b_pk[kc % 2]

                    def mmS(q, kc=kc, mt=mt, tp=tp):
                        q.matmul(tp, ksb[:, kc * 128:(kc + 1) * 128],
                                 qsb[:, mt * 512:(mt + 1) * 512],
                                 start=True, stop=False)
                        return q.matmul(
                            tp, krT[0:64, kc * 128:(kc + 1) * 128],
                            qrs[0:64, mt * 512:(mt + 1) * 512],
                            start=False, stop=True)
                    S.op("pe", mmS, reads=[b_ks, b_kr, b_qs, b_qr],
                         writes=[b_tp])
                    tpb, b_tpb = psb[kc % 2]
                    S.op("act", lambda q, tp=tp, tpb=tpb: q.activation(
                        tpb, tp, AF.Exp, scale=SOFT_SC),
                        reads=[b_tp], writes=[b_tpb])
                    S.op("pe", lambda q, tpb=tpb, kc=kc: q.matmul(
                        pk[4][0:1, :], onesb, tpb,
                        start=(kc == 0), stop=(kc == 63)),
                        reads=[b_ob, b_tpb],
                        writes=[b_l] if kc == 63 else [],
                        region_of=[b_l] if kc < 63 else [])
                    b_cx = b_pk[2 + mt % 2]
                    S.op("pe", lambda q, tpb=tpb, kc=kc, mt=mt: q.matmul(
                        pk[2 + mt % 2], vsb[:, kc * 128:(kc + 1) * 128], tpb,
                        start=(kc == 0), stop=(kc == 63)),
                        reads=[b_vs, b_tpb],
                        writes=[b_cx] if kc == 63 else [],
                        region_of=[b_cx] if kc < 63 else [])
                S.op("dve", lambda q: q.tensor_copy(lrow, pk[4][0:1, :]),
                     reads=[b_pk[4]], writes=[b_lr])
                S.op("dve", lambda q: q.reciprocal(lrow, lrow),
                     reads=[b_lr], writes=[b_lr])
                S.op("pe", lambda q: q.matmul(pk[5], ones1, lrow,
                                              start=True, stop=True),
                     reads=[b_o1, b_lr], writes=[b_pk[5]])
                S.op("dve", lambda q: q.tensor_copy(lbs, pk[5]),
                     reads=[b_pk[5]], writes=[b_lb])
                ccols = ctx[:, h * TPC + mt * 512:h * TPC + (mt + 1) * 512]
                ctx_evs.append(S.op(
                    "dve", lambda q, ccols=ccols, mt=mt: q.tensor_mul(
                        ccols, pk[2 + mt % 2], lbs),
                    reads=[b_pk[2 + mt % 2], b_lb], region_of=[b_ctx]))
        S.seal(b_ctx, ctx_evs)

        wosb, b_wo = arena.view("wosb", C_KV, C_KV + ND * TPC)
        bufs.sb("st_f0", (128, D), F32)
        bufs.sb("st_f1", (128, D), F32)
        wevs = []
        for ec in range(ND):
            (tt, b_tt) = bufs.tiles["st_f0"]
            S.dma(tt, w_o[ec * 128:(ec + 1) * 128, :], writes=[b_tt])
            wevs.append(S.op("dve", lambda q, tt=tt, ec=ec: q.tensor_copy(
                wosb[:, ec * 2048:(ec + 1) * 2048], tt),
                reads=[b_tt], region_of=[b_wo]))
        S.seal(b_wo, wevs)
        bo_row, b_bor = bufs.sb("bo_row", (1, D), F32)
        S.dma(bo_row, b_o, writes=[b_bor])
        bob, b_bob = arf.view("bob", 0, D)
        bob_evs = []
        for mt in range(4):
            S.op("pe", lambda q, mt=mt: q.matmul(
                pk[mt % 2], ones1, bo_row[:, mt * 512:(mt + 1) * 512],
                start=True, stop=True), reads=[b_o1, b_bor],
                writes=[b_pk[mt % 2]])
            bob_evs.append(S.op(
                "dve", lambda q, mt=mt: q.tensor_copy(
                    bob[:, mt * 512:(mt + 1) * 512], pk[mt % 2]),
                reads=[b_pk[mt % 2]], region_of=[b_bob]))
        S.seal(b_bob, bob_evs)
        x3_evs = []
        xts = [bufs.tiles[f"st_f{i}"] for i in range(2)]
        for tc in range(TPC // 128):
            (tx, b_tx) = xts[tc % 2]
            S.dma(tx, x2[tc * 128:(tc + 1) * 128, :], reads=[b_x2],
                  writes=[b_tx])
            S.op("dve", lambda q, tx=tx: q.tensor_add(tx, tx, bob),
                 reads=[b_bob], writes=[b_tx])
            for dmb in range(4):
                tp, b_tp = pk[2 + dmb % 2], b_pk[2 + dmb % 2]

                def mmo(q, tc=tc, dmb=dmb, tp=tp):
                    r = None
                    for ec in range(ND):
                        r = q.matmul(
                            tp, ctx[:, ec * TPC + tc * 128:
                                    ec * TPC + tc * 128 + 128],
                            wosb[:, ec * 2048 + dmb * 512:
                                 ec * 2048 + (dmb + 1) * 512],
                            start=(ec == 0), stop=(ec == ND - 1))
                    return r
                S.op("pe", mmo, reads=[b_ctx, b_wo], writes=[b_tp])
                S.op("dve", lambda q, tx=tx, tp=tp, dmb=dmb: q.tensor_add(
                    tx[:, dmb * 512:(dmb + 1) * 512],
                    tx[:, dmb * 512:(dmb + 1) * 512], tp),
                    reads=[b_tp], writes=[b_tx])
            x3_evs.append(S.dma(x3[tc * 128:(tc + 1) * 128, :], tx,
                                eng="pool", reads=[b_tx], region_of=[b_x3]))
        S.seal(b_x3, x3_evs)

        emit_ffn(S, bufs, arena, pk, b_pk, x3, b_x3, xout, b_xout,
                 wg_bf, b_wg, wu_bf, b_wu, wd_bf, b_wd, delta_f, b_df,
                 ident, b_id, "2")
        S.op("pool", lambda q: q.nop(), deps=[b_xout.w])
        S.emit(block)
    return nc


def _p3_inmaps(inputs, r2):
    in_maps = []
    for c in range(NCORES):
        b = c // 4
        slabs = [r2[4 * b + p]["kv_pub"] for p in range(4)]
        m = {
            "x2": r2[c]["x2"],
            "kvg": np.ascontiguousarray(np.concatenate(slabs, axis=0)),
            "qf": r2[c]["qf_pub"],
            "w_o": np.asarray(inputs["L0_attn_b_W_O"]),
            "b_o": np.asarray(inputs["L0_attn_b_b_O"]).reshape(1, D),
            "wq_gate": np.asarray(inputs["L0_ffn_wq_gate"]),
            "s_gate": np.asarray(inputs["L0_ffn_s_gate"]),
            "wq_up": np.asarray(inputs["L0_ffn_wq_up"]),
            "s_up": np.asarray(inputs["L0_ffn_s_up"]),
            "wq_down": np.asarray(inputs["L0_ffn_wq_down"]),
            "s_down": np.asarray(inputs["L0_ffn_s_down"]),
        }
        in_maps.append(m)
    return in_maps


def _run_p3(inputs, r2):
    nc = build_p3()
    res = run_bass_kernel_spmd(nc, _p3_inmaps(inputs, r2),
                               core_ids=list(range(NCORES)))
    return res.results


def _split_tokens(a):
    flat = np.ascontiguousarray(np.asarray(a).reshape(B * T, *a.shape[2:]))
    return [flat[c * TPC:(c + 1) * TPC] for c in range(NCORES)]


def _p1_inmaps(inputs):
    xs = _split_tokens(inputs["L0_attn_a_hidden_states"])
    in_maps = []
    for c in range(NCORES):
        cst = host_consts(c)
        m = {"x": np.ascontiguousarray(xs[c]),
             "w_qkv": np.asarray(inputs["L0_attn_a_w_qkv"]),
             "w_gate": np.asarray(inputs["L0_attn_a_w_gate"]),
             "slope": np.asarray(inputs["L0_attn_a_slope_rate"]).reshape(1, H)}
        for n, s, d in CONST_SPECS:
            m[n] = np.ascontiguousarray(cst[n].reshape(s))
        in_maps.append(m)
    return in_maps


def _run_p1(inputs):
    nc = build_p1()
    res = run_bass_kernel_spmd(nc, _p1_inmaps(inputs),
                               core_ids=list(range(NCORES)))
    return res.results


def _p2_inmaps(inputs, r1):
    xs = _split_tokens(inputs["L0_attn_a_hidden_states"])
    kvs = [r1[c]["kvsum"] for c in range(NCORES)]
    in_maps = []
    for c in range(NCORES):
        cst = host_consts(c)
        b = c // 4
        pre = [kvs[4 * b + p] for p in range(3)]
        m = {
            "x": np.ascontiguousarray(xs[c]),
            "attn_local": r1[c]["attn_local"],
            "qdec_g": r1[c]["qdec_g"],
            "gate_o": r1[c]["gate_o"],
            "kvpre": np.ascontiguousarray(np.concatenate(pre, axis=0)),
            "slope": np.asarray(
                inputs["L0_attn_a_slope_rate"]).reshape(1, H),
            "w_out": np.asarray(inputs["L0_attn_a_w_out"]),
            "norm_w": np.asarray(inputs["L0_attn_a_norm_w"]),
            "wq_gate": np.asarray(inputs["L0_ffn_wq_gate"]),
            "s_gate": np.asarray(inputs["L0_ffn_s_gate"]),
            "wq_up": np.asarray(inputs["L0_ffn_wq_up"]),
            "s_up": np.asarray(inputs["L0_ffn_s_up"]),
            "wq_down": np.asarray(inputs["L0_ffn_wq_down"]),
            "s_down": np.asarray(inputs["L0_ffn_s_down"]),
            "w_dkv": np.asarray(inputs["L0_attn_b_W_DKV"]),
            "w_dq": np.asarray(inputs["L0_attn_b_W_DQ"]),
            "w_uk": np.asarray(inputs["L0_attn_b_W_UK"]),
            "w_uv": np.asarray(inputs["L0_attn_b_W_UV"]),
            "w_uq": np.asarray(inputs["L0_attn_b_W_UQ"]),
            "w_kr": np.asarray(inputs["L0_attn_b_W_KR"]),
            "w_qr": np.asarray(inputs["L0_attn_b_W_QR"]),
        }
        for n, s, d in CONST_SPECS:
            m[n] = np.ascontiguousarray(cst[n].reshape(s))
        in_maps.append(m)
    return in_maps


def _run_p2(inputs, r1):
    nc = build_p2()
    res = run_bass_kernel_spmd(nc, _p2_inmaps(inputs, r1),
                               core_ids=list(range(NCORES)))
    return res.results


SPAN = 4096
NBK = SPAN // BLK
QT = 2048
MAGIC1 = 0x5EED0001
MAGIC2 = 0x5EED0002


def host_consts_fused(core):
    half = core % 2
    quarter = (core // 2) % 2
    c = {}
    ar = np.arange(1, BLK + 1, dtype=np.float32)
    c["qdecay_idx"] = np.tile(ar, (128, 1))
    c["kdecay_idx"] = np.tile(BLK - ar, (128, 1))
    idx = ar[:, None] - ar[None, :]
    ddT = idx.T.copy()
    ddT[ddT < 0] = BIG
    c["ddT0"] = np.ascontiguousarray(ddT[0:128, :], np.float32)
    c["ddT1"] = np.ascontiguousarray(ddT[128:256, :], np.float32)
    c["blkj"] = (np.arange(NBK, dtype=np.float32) * BLK).reshape(1, -1)
    c["dist0"] = np.full((128, 1), 0.0 if half == 1 else BIG, np.float32)
    c["positions_k"] = (half * SPAN +
                        np.arange(SPAN)).astype(np.float32).reshape(1, -1)
    c["positions_q"] = (half * SPAN + quarter * QT +
                        np.arange(QT)).astype(np.float32).reshape(1, -1)
    invf = 1.0 / 10000.0 ** (np.arange(DR // 2, dtype=np.float32) / (DR // 2))
    c["invfreq4"] = np.tile(invf, 4).reshape(-1, 1)
    c["qsel"] = np.full((128, 1), 1.0 if quarter == 0 else 0.0, np.float32)
    return c


FUSED_CONST_SPECS = [
    ("qdecay_idx", (128, BLK), F32),
    ("kdecay_idx", (128, BLK), F32),
    ("ddT0", (128, BLK), F32),
    ("ddT1", (128, BLK), F32),
    ("blkj", (1, NBK), F32),
    ("dist0", (128, 1), F32),
    ("positions_k", (1, SPAN), F32),
    ("positions_q", (1, QT), F32),
    ("invfreq4", (128, 1), F32),
    ("qsel", (128, 1), F32),
]


def emit_exchange_write(S, nc, sems, srcs_and_dsts, flag_sh, flag_tile,
                        b_ft, magic, deps, name):
    exsems = sems

    def closure(q):
        for arm in nc.switch_core_id(engines=[q], n=128):
            role = int(arm) % 2
            n = 0
            cnts = [0, 0, 0, 0]
            for (src_fn, dst_fn) in srcs_and_dsts:
                si = n % 4
                n += 1
                cnts[si] += 16
                q.dma_start(out=dst_fn(role), in_=src_fn(role)).then_inc(
                    exsems[si], 16)
            for si in range(4):
                if cnts[si]:
                    q.wait_ge(exsems[si], cnts[si])
            q.dma_start(out=flag_sh[role:role + 1, :],
                        in_=flag_tile).then_inc(exsems[0], 16)
            q.wait_ge(exsems[0], cnts[0] + 16)
        return q.nop()

    return S.op("pool", closure, reads=[b_ft], deps=list(deps), inc=1)


def emit_poll(S, nc, psem, flag_sh, fr_tile, b_fr, magic, deps, name):

    def closure(q):
        with q.register(f"nd{name}") as nd, q.register(f"tg{name}") as tg, \
             q.register(f"v0{name}") as v0, q.register(f"v1{name}") as v1:
            q.reg_mov(nd, 1)
            q.reg_mov(tg, 0)
            with q.While(nd):
                q.reg_add(tg, tg, 16)
                q.dma_start(out=fr_tile[0:1, 0:16],
                            in_=flag_sh[0:1, :]).then_inc(psem, 16)
                q.reg_add(tg, tg, 16)
                q.dma_start(out=fr_tile[0:1, 16:32],
                            in_=flag_sh[1:2, :]).then_inc(psem, 16)
                q.wait_ge(psem, tg)
                q.reg_load(v0, fr_tile[0:1, 0:1])
                q.reg_load(v1, fr_tile[0:1, 16:17])
                q.reg_alu(v0, v0, magic, ALU.subtract)
                q.reg_alu(v1, v1, magic, ALU.subtract)
                q.reg_alu(nd, v0, v1, ALU.bitwise_or)
        return q.nop()

    return S.op("pool", closure, deps=list(deps), reads=(), writes=[b_fr],
                inc=1)


def build_fused():
    nc = bass.Bass("TRN2", target_bir_lowering=False)
    x = nc.dram_tensor("x", [SPAN, D], F32, kind="ExternalInput").ap()
    w_qkv = nc.dram_tensor("w_qkv", [3 * D, D], F32, kind="ExternalInput").ap()
    w_gate = nc.dram_tensor("w_gate", [D, D], F32, kind="ExternalInput").ap()
    slope = nc.dram_tensor("slope", [1, H], F32, kind="ExternalInput").ap()
    w_out = nc.dram_tensor("w_out", [D, D], F32, kind="ExternalInput").ap()
    norm_w = nc.dram_tensor("norm_w", [D], F32, kind="ExternalInput").ap()
    wq_g = nc.dram_tensor("wq_gate", [HID, D], I32, kind="ExternalInput").ap()
    s_g = nc.dram_tensor("s_gate", [HID], F32, kind="ExternalInput").ap()
    wq_u = nc.dram_tensor("wq_up", [HID, D], I32, kind="ExternalInput").ap()
    s_u = nc.dram_tensor("s_up", [HID], F32, kind="ExternalInput").ap()
    wq_d = nc.dram_tensor("wq_down", [D, HID], I32, kind="ExternalInput").ap()
    s_d = nc.dram_tensor("s_down", [D], F32, kind="ExternalInput").ap()
    w_dkv = nc.dram_tensor("w_dkv", [D, DC], F32, kind="ExternalInput").ap()
    w_dq = nc.dram_tensor("w_dq", [D, DC], F32, kind="ExternalInput").ap()
    w_uk = nc.dram_tensor("w_uk", [DC, D], F32, kind="ExternalInput").ap()
    w_uv = nc.dram_tensor("w_uv", [DC, D], F32, kind="ExternalInput").ap()
    w_uq = nc.dram_tensor("w_uq", [DC, D], F32, kind="ExternalInput").ap()
    w_kr = nc.dram_tensor("w_kr", [D, DR], F32, kind="ExternalInput").ap()
    w_qr = nc.dram_tensor("w_qr", [DC, H * DR], F32,
                          kind="ExternalInput").ap()
    w_o2 = nc.dram_tensor("w_o2", [D, D], F32, kind="ExternalInput").ap()
    b_o2 = nc.dram_tensor("b_o2", [1, D], F32, kind="ExternalInput").ap()
    consts = {n: nc.dram_tensor(n, list(s), d, kind="ExternalInput").ap()
              for n, s, d in FUSED_CONST_SPECS}
    xq_out = nc.dram_tensor("xq_out", [QT, D], F32, kind="ExternalOutput").ap()
    wbf = nc.dram_tensor("wbf", [4 * D, D], BF16).ap()
    qkv_fm = nc.dram_tensor("qkv_fm", [3 * D, SPAN], BF16).ap()
    attn_l = nc.dram_tensor("attn_l", [D, SPAN], BF16).ap()
    qdec = nc.dram_tensor("qdec", [D, SPAN], BF16).ap()
    gate_s = nc.dram_tensor("gate_s", [D, SPAN], BF16).ap()
    kvsum_l = nc.dram_tensor("kvsum_l", [D, DH], F32).ap()
    wo_bf = nc.dram_tensor("wo_bf", [D, D], BF16).ap()
    x1 = nc.dram_tensor("x1", [SPAN, D], F32).ap()
    x2 = nc.dram_tensor("x2", [SPAN, D], F32).ap()
    xq = nc.dram_tensor("xq", [QT, D], F32).ap()
    x3 = nc.dram_tensor("x3", [QT, D], F32).ap()
    delta_a = nc.dram_tensor("delta_a", [SPAN, D], BF16).ap()
    delta_f = nc.dram_tensor("delta_f", [SPAN, D], BF16).ap()
    delta_2 = nc.dram_tensor("delta_2", [QT, D], BF16).ap()
    wg_bf = nc.dram_tensor("wg_bf", [HID, D], BF16).ap()
    wu_bf = nc.dram_tensor("wu_bf", [HID, D], BF16).ap()
    wd_bf = nc.dram_tensor("wd_bf", [D, HID], BF16).ap()
    u3_bf = nc.dram_tensor("u3_bf", [3 * DC, D], BF16).ap()
    Kl = nc.dram_tensor("Kl", [D, SPAN], BF16).ap()
    Vl = nc.dram_tensor("Vl", [SPAN, D], BF16).ap()
    kRl = nc.dram_tensor("kRl", [DR, SPAN], BF16).ap()
    qf_l = nc.dram_tensor("qf_l", [D + H * DR, QT], BF16).ap()
    sh_sum = nc.dram_tensor("sh_sum", [2, D, DH], F32,
                            addr_space="Shared").ap()
    sh_K = nc.dram_tensor("sh_K", [2, D, SPAN], BF16,
                          addr_space="Shared").ap()
    sh_V = nc.dram_tensor("sh_V", [2, SPAN, D], BF16,
                          addr_space="Shared").ap()
    sh_kR = nc.dram_tensor("sh_kR", [2, DR, SPAN], BF16,
                           addr_space="Shared").ap()
    fl_sum = nc.dram_tensor("fl_sum", [2, 16], I32, addr_space="Shared").ap()
    fl_kv = nc.dram_tensor("fl_kv", [2, 16], I32, addr_space="Shared").ap()

    S = Sched(nc)
    bufs = Bufs(nc, S)
    arena = Arena(nc, S, "arena", 61440, BF16)
    arf = Arena(nc, S, "arf", 3584, F32)
    exsemsA = [nc.semaphore_raw(f"EXA{i}") for i in range(4)]
    exsemsB = [nc.semaphore_raw(f"EXB{i}") for i in range(4)]
    psem1 = nc.semaphore_raw("PL1")
    psem2 = nc.semaphore_raw("PL2")
    b_x = S.buf("x")
    b_wbf, b_qkvfm = S.buf("wbf"), S.buf("qkv_fm")
    b_attnl, b_qdec, b_gates = S.buf("attn_l"), S.buf("qdec"), S.buf("gate_s")
    b_kvsl, b_wobf = S.buf("kvsum_l"), S.buf("wo_bf")
    b_x1, b_x2, b_xq, b_x3 = S.buf("x1"), S.buf("x2"), S.buf("xq"), S.buf("x3")
    b_da, b_df, b_d2 = S.buf("delta_a"), S.buf("delta_f"), S.buf("delta_2")
    b_wg, b_wu, b_wd, b_u3 = (S.buf("wg_bf"), S.buf("wu_bf"), S.buf("wd_bf"),
                              S.buf("u3_bf"))
    b_Kl, b_Vl, b_kRl, b_qfl = (S.buf("Kl"), S.buf("Vl"), S.buf("kRl"),
                                S.buf("qf_l"))
    b_shsum, b_shK, b_shV, b_shkR = (S.buf("sh_sum"), S.buf("sh_K"),
                                     S.buf("sh_V"), S.buf("sh_kR"))
    b_xqo = S.buf("xq_out")

    with nc.Block() as block:
        (ident, b_id), (ones1, b_o1), (onesb, b_ob) = emit_setup(S, bufs)
        pk, b_pk = [], []
        for i in range(6):
            t, b = bufs.ps(f"pk{i}", (128, 512), F32)
            pk.append(t)
            b_pk.append(b)
        bufs.ps("pbf", (128, 1024), BF16)
        ftile, b_ft = bufs.sb("ftile", (1, 16), I32)
        frtile, b_fr = bufs.sb("frtile", (1, 32), I32)

        bufs.sb("st_f0", (128, D), F32)
        bufs.sb("st_f1", (128, D), F32)
        bufs.sb("st_b0", (128, D), BF16)
        bufs.sb("st_b1", (128, D), BF16)
        bufs.sb("st_b2", (128, D), BF16)
        bufs.sb("st_b3", (128, D), BF16)
        wc = [bufs.tiles[f"st_f{i}"] for i in range(2)]
        wcb = [bufs.tiles[f"st_b{i}"] for i in range(2)]
        wbf_evs = []
        for i in range(4 * ND):
            bb = i % 2
            (tc, b_tc), (tb, b_tb) = wc[bb], wcb[bb]
            src = (w_qkv[i * 128:(i + 1) * 128, :] if i < 3 * ND else
                   w_gate[(i - 3 * ND) * 128:(i - 3 * ND + 1) * 128, :])
            S.dma(tc, src, writes=[b_tc])
            S.op("dve", lambda q, tc=tc, tb=tb: q.tensor_copy(tb, tc),
                 reads=[b_tc], writes=[b_tb])
            wbf_evs.append(S.dma(wbf[i * 128:(i + 1) * 128, :], tb, eng="pool",
                                 reads=[b_tb], region_of=[b_wbf]))
        S.seal(b_wbf, wbf_evs)

        ln_fm, b_ln = arena.view("p1ln", 0, ND * TPC)
        wt = [arena.view(f"p1wt{i}", ND * TPC + i * ND * 512,
                         ND * TPC + (i + 1) * ND * 512) for i in range(2)]
        ev = [bufs.sb(f"evs{i}", (128, 512), BF16) for i in range(3)]
        qkv_evs, gate_evs = [], []
        for ch in range(2):
            t0 = ch * TPC
            emit_ln(S, bufs, x, b_x, ln_fm, b_ln, t0, TPC, "1", ident, b_id)
            for jb in range((4 * D) // 512):
                bb = jb % 2
                tw, b_tw = wt[bb]
                wt_evs = [S.dmaT(tw[:, dc * 512:(dc + 1) * 512],
                                 wbf[jb * 512:(jb + 1) * 512,
                                     dc * 128:(dc + 1) * 128],
                                 reads=[b_wbf], region_of=[b_tw])
                          for dc in range(ND)]
                S.seal(b_tw, wt_evs)
                for j4 in range(4):
                    jc = jb * 4 + j4
                    for nt in range(4):
                        k = (jc * 4 + nt) % 6
                        tp, b_tp = pk[k], b_pk[k]

                        def mmg(q, tw=tw, j4=j4, nt=nt, tp=tp):
                            r = None
                            for dc in range(ND):
                                r = q.matmul(
                                    tp,
                                    tw[:, dc * 512 + j4 * 128:
                                       dc * 512 + j4 * 128 + 128],
                                    ln_fm[:, dc * TPC + nt * 512:
                                          dc * TPC + (nt + 1) * 512],
                                    start=(dc == 0), stop=(dc == ND - 1))
                            return r
                        S.op("pe", mmg, reads=[b_tw, b_ln], writes=[b_tp])
                        te, b_te = ev[(jc * 4 + nt) % 3]
                        func = AF.Silu if jc < 3 * ND else AF.Sigmoid
                        S.op("act", lambda q, tp=tp, te=te, func=func:
                             q.activation(te, tp, func), reads=[b_tp],
                             writes=[b_te])
                        cc0 = t0 + nt * 512
                        if jc < 3 * ND:
                            qkv_evs.append(S.dma(
                                qkv_fm[jc * 128:(jc + 1) * 128, cc0:cc0 + 512],
                                te, eng="pool", reads=[b_te],
                                region_of=[b_qkvfm]))
                        else:
                            row = (jc - 3 * ND) * 128
                            gate_evs.append(S.dma(
                                gate_s[row:row + 128, cc0:cc0 + 512],
                                te, eng="pool", reads=[b_te],
                                region_of=[b_gates]))
        S.seal(b_qkvfm, qkv_evs)
        S.seal(b_gates, gate_evs)

        sl_row, b_slr = bufs.sb("sl_row", (1, H), F32)
        S.dma(sl_row, slope, writes=[b_slr])
        sb_s, b_ss = emit_bcast_row(S, bufs, pk[0], b_pk[0], ones1, b_o1,
                                    sl_row, b_slr, "sb_s", H)
        bj_row, b_bjr = bufs.sb("bj_row", (1, NBK), F32)
        S.dma(bj_row, consts["blkj"], writes=[b_bjr])
        bjb, b_bjb = emit_bcast_row(S, bufs, pk[1], b_pk[1], ones1, b_o1,
                                    bj_row, b_bjr, "bjb", NBK)
        qdi, b_qdi = arf.view("qdi", 0, BLK)
        kdi, b_kdi = arf.view("kdi", BLK, 2 * BLK)
        dd0, b_dd0 = arf.view("dd0", 2 * BLK, 3 * BLK)
        dd1, b_dd1 = arf.view("dd1", 3 * BLK, 4 * BLK)
        S.dma(qdi, consts["qdecay_idx"], writes=[b_qdi])
        S.dma(kdi, consts["kdecay_idx"], writes=[b_kdi])
        S.dma(dd0, consts["ddT0"], writes=[b_dd0])
        S.dma(dd1, consts["ddT1"], writes=[b_dd1])
        dblk, b_dblk = bufs.sb("dblk", (128, H), F32)
        S.op("act", lambda q: q.activation(dblk, sb_s, AF.Exp,
                                           scale=-float(BLK)),
             reads=[b_ss], writes=[b_dblk])
        dgl, b_dgl = arf.view("dgl", 4 * BLK, 4 * BLK + H * NBK)
        dgl_evs = []
        for h in range(H):
            dgl_evs.append(S.op("dve", lambda q, h=h: q.tensor_scalar_mul(
                dgl[:, h * NBK:(h + 1) * NBK], bjb, sb_s[:, h:h + 1]),
                reads=[b_bjb, b_ss], region_of=[b_dgl]))
        S.seal(b_dgl, dgl_evs)
        S.op("act", lambda q: q.activation(dgl, dgl, AF.Exp, scale=-1.0),
             reads=[b_dgl], writes=[b_dgl])
        qdb, b_qdb = arena.view("qdb", 0, H * BLK)
        kdb, b_kdb = arena.view("kdb", H * BLK, 2 * H * BLK)
        ddb, b_ddb = arena.view("ddb", 2 * H * BLK, 4 * H * BLK)
        tmpd, b_tmpd = bufs.sb("tmpd", (128, BLK), F32)
        for h in range(H):
            for (srcT, b_src, dstT, b_dst, col) in (
                    (qdi, b_qdi, qdb, b_qdb, h * BLK),
                    (kdi, b_kdi, kdb, b_kdb, h * BLK),
                    (dd0, b_dd0, ddb, b_ddb, 2 * h * BLK),
                    (dd1, b_dd1, ddb, b_ddb, (2 * h + 1) * BLK)):
                S.op("dve", lambda q, srcT=srcT, h=h: q.tensor_scalar_mul(
                    tmpd, srcT, sb_s[:, h:h + 1]),
                    reads=[b_src, b_ss], writes=[b_tmpd])
                S.op("act", lambda q, dstT=dstT, col=col: q.activation(
                    dstT[:, col:col + BLK], tmpd, AF.Exp, scale=-1.0),
                    reads=[b_tmpd], region_of=[b_dst])
        ready = ("act", S.cnt["act"])
        for b in (b_qdb, b_kdb, b_ddb):
            S.seal(b, [ready])

        pbf, b_pbf = bufs.tiles["pbf"]
        C_QKV = 4 * H * BLK
        qf = [arena.view(f"sqf{i}", C_QKV + i * SPAN, C_QKV + (i + 1) * SPAN)
              for i in range(2)]
        kf = [arena.view(f"skf{i}", C_QKV + (2 + i) * SPAN,
                         C_QKV + (3 + i) * SPAN) for i in range(2)]
        vf = [arena.view(f"svf{i}", C_QKV + (4 + i) * SPAN,
                         C_QKV + (5 + i) * SPAN) for i in range(2)]
        kvf, b_kvf = bufs.sb("kvf", (128, DH), F32)
        kvb, b_kvb = bufs.sb("kvb", (128, DH), BF16)
        qdx, b_qdx = bufs.sb("qdx", (128, BLK), BF16)
        qgx, b_qgx = bufs.sb("qgx", (128, BLK), BF16)
        kdx, b_kdx = bufs.sb("kdx", (128, BLK), BF16)
        vkm, b_vkm = bufs.sb("vkm", (128, 2 * 128), BF16)
        kkm, b_kkm = bufs.sb("kkm", (128, 2 * 128), BF16)
        ptt, b_ptt = bufs.sb("ptt", (128, 2 * BLK), BF16)
        aev = [bufs.sb(f"aev{i}", (128, BLK), BF16) for i in range(2)]
        attnl_evs, qdec_evs, kvsl_evs = [], [], []
        for h in range(H):
            hb = h % 2
            (tq, b_tq), (tk, b_tk), (tv, b_tv) = qf[hb], kf[hb], vf[hb]
            S.dma(tq, qkv_fm[h * 128:(h + 1) * 128, :],
                  reads=[b_qkvfm], writes=[b_tq])
            S.dma(tk, qkv_fm[D + h * 128:D + (h + 1) * 128, :],
                  reads=[b_qkvfm], writes=[b_tk])
            S.dma(tv, qkv_fm[2 * D + h * 128:2 * D + (h + 1) * 128, :],
                  reads=[b_qkvfm], writes=[b_tv])
            S.op("dve", lambda q: q.memset(kvf, 0.0), writes=[b_kvf])
            S.op("dve", lambda q: q.memset(kvb, 0.0), writes=[b_kvb])
            for j in range(NBK):
                t0 = j * BLK
                S.op("dve", lambda q, tq=tq, t0=t0, h=h: q.tensor_mul(
                    qdx, tq[:, t0:t0 + BLK], qdb[:, h * BLK:(h + 1) * BLK]),
                    reads=[b_tq, b_qdb], writes=[b_qdx])
                S.op("dve", lambda q, h=h, j=j: q.tensor_scalar_mul(
                    qgx, qdx, dgl[:, h * NBK + j:h * NBK + j + 1]),
                    reads=[b_qdx, b_dgl], writes=[b_qgx])
                qdec_evs.append(S.dma(
                    qdec[h * 128:(h + 1) * 128, t0:t0 + BLK], qgx,
                    eng="pool", reads=[b_qgx], region_of=[b_qdec]))
                S.op("dve", lambda q, tk=tk, t0=t0, h=h: q.tensor_mul(
                    kdx, tk[:, t0:t0 + BLK], kdb[:, h * BLK:(h + 1) * BLK]),
                    reads=[b_tk, b_kdb], writes=[b_kdx])

                def tgroup(q, tv=tv, t0=t0):
                    for cc in range(2):
                        q.transpose(pbf[:, cc * 128:(cc + 1) * 128],
                                    tv[:, t0 + cc * 128:t0 + (cc + 1) * 128],
                                    ident)
                    r = None
                    for cc in range(2):
                        r = q.transpose(pbf[:, 512 + cc * 128:
                                            512 + (cc + 1) * 128],
                                        kdx[:, cc * 128:(cc + 1) * 128],
                                        ident)
                    return r
                S.op("pe", tgroup, reads=[b_tv, b_kdx, b_id], writes=[b_pbf])
                S.op("dve", lambda q: q.tensor_copy(vkm, pbf[:, 0:256]),
                     reads=[b_pbf], writes=[b_vkm])
                S.op("dve", lambda q: q.tensor_copy(kkm, pbf[:, 512:768]),
                     reads=[b_pbf], writes=[b_kkm])
                for cc in range(2):
                    tp, b_tp = pk[4 + cc], b_pk[4 + cc]
                    S.op("pe", lambda q, tk=tk, tq=tq, t0=t0, cc=cc, tp=tp:
                         q.matmul(tp[:, 0:BLK],
                                  tk[:, t0 + cc * 128:t0 + (cc + 1) * 128],
                                  tq[:, t0:t0 + BLK], start=True, stop=True),
                         reads=[b_tk, b_tq], writes=[b_tp])
                    S.op("dve", lambda q, cc=cc, h=h, tp=tp: q.tensor_mul(
                        ptt[:, cc * BLK:(cc + 1) * BLK], tp[:, 0:BLK],
                        ddb[:, (2 * h + cc) * BLK:(2 * h + cc + 1) * BLK]),
                        reads=[b_tp, b_ddb], region_of=[b_ptt],
                        deps=[b_ptt.w])
                S.seal(b_ptt, [("dve", S.cnt["dve"])])
                tp, b_tp = pk[j % 2], b_pk[j % 2]

                def outg(q, tp=tp):
                    q.matmul(tp[:, 0:BLK], kvb, qdx, start=True, stop=False)
                    q.matmul(tp[:, 0:BLK], vkm[:, 0:128], ptt[:, 0:BLK],
                             start=False, stop=False)
                    return q.matmul(tp[:, 0:BLK], vkm[:, 128:256],
                                    ptt[:, BLK:2 * BLK], start=False,
                                    stop=True)
                S.op("pe", outg, reads=[b_kvb, b_qdx, b_vkm, b_ptt],
                     writes=[b_tp])
                ta, b_ta = aev[j % 2]
                S.op("act", lambda q, tp=tp, ta=ta: q.activation(
                    ta, tp[:, 0:BLK], AF.Copy), reads=[b_tp], writes=[b_ta])
                attnl_evs.append(S.dma(
                    attn_l[h * 128:(h + 1) * 128, t0:t0 + BLK], ta,
                    eng="pool", reads=[b_ta], region_of=[b_attnl]))
                tpv, b_tpv = pk[2 + (j % 2)], b_pk[2 + (j % 2)]

                def kvg(q, tpv=tpv):
                    q.matmul(tpv[:, 0:DH], kkm[:, 0:128], vkm[:, 0:128],
                             start=True, stop=False)
                    return q.matmul(tpv[:, 0:DH], kkm[:, 128:256],
                                    vkm[:, 128:256], start=False, stop=True)
                S.op("pe", kvg, reads=[b_kkm, b_vkm], writes=[b_tpv])
                S.op("dve", lambda q, h=h, tpv=tpv: q.scalar_tensor_tensor(
                    kvf, kvf, dblk[:, h:h + 1], tpv[:, 0:DH],
                    ALU.mult, ALU.add),
                    reads=[b_dblk, b_tpv, b_kvf], writes=[b_kvf])
                if j < NBK - 1:
                    S.op("dve", lambda q: q.tensor_copy(kvb, kvf),
                         reads=[b_kvf], writes=[b_kvb])
            kvsl_evs.append(S.dma(kvsum_l[h * 128:(h + 1) * 128, :], kvf,
                                  eng="pool", reads=[b_kvf],
                                  region_of=[b_kvsl]))
        S.seal(b_attnl, attnl_evs)
        S.seal(b_qdec, qdec_evs)
        S.seal(b_kvsl, kvsl_evs)

        S.op("pool", lambda q: q.memset(ftile, MAGIC1), writes=[b_ft])
        ex1 = emit_exchange_write(
            S, nc, exsemsA,
            [(lambda r: kvsum_l, lambda r: sh_sum[r])],
            fl_sum, ftile, b_ft, MAGIC1, [b_kvsl.w], "s")
        S.seal(b_shsum, [ex1])
        pl1 = emit_poll(S, nc, psem1, fl_sum, frtile, b_fr, MAGIC1,
                        [ex1], "s")

        emit_cast_pass(S, bufs, w_out, wo_bf, b_wobf, ND, D, "wo")
        u3_evs = []
        for wi, wsrc in enumerate((w_uk, w_uv, w_uq)):
            for cc in range(4):
                (tt, b_tt) = bufs.tiles["st_f0"]
                (tb, b_tb) = bufs.tiles["cpowo0"]
                S.dma(tt, wsrc[cc * 128:(cc + 1) * 128, :], writes=[b_tt])
                S.op("dve", lambda q, tt=tt, tb=tb: q.tensor_copy(tb, tt),
                     reads=[b_tt], writes=[b_tb])
                u3_evs.append(S.dma(
                    u3_bf[wi * DC + cc * 128:wi * DC + (cc + 1) * 128, :],
                    tb, eng="pool", reads=[b_tb], region_of=[b_u3]))
        S.seal(b_u3, u3_evs)
        sgu, b_sgu = emit_col_load(S, bufs, "sgu", s_g, NH)
        suu, b_suu = emit_col_load(S, bufs, "suu", s_u, NH)
        sdd, b_sdd = emit_col_load(S, bufs, "sdd", s_d, ND)
        emit_dequant(S, bufs, wq_g, sgu, b_sgu, 0, wg_bf, b_wg, NH, D, "g")
        emit_dequant(S, bufs, wq_u, suu, b_suu, 0, wu_bf, b_wu, NH, D, "u")
        emit_dequant(S, bufs, wq_d, sdd, b_sdd, 0, wd_bf, b_wd, ND, HID, "d")

        d0b, b_d0b = bufs.sb("d0b", (128, 1), F32)
        S.dma(d0b, consts["dist0"], writes=[b_d0b])
        w0t, b_w0t = bufs.sb("w0t", (128, H), F32)
        S.op("dve", lambda q: q.tensor_scalar_mul(w0t, sb_s, d0b),
             reads=[b_ss, b_d0b], writes=[b_w0t])
        S.op("act", lambda q: q.activation(w0t, w0t, AF.Exp, scale=-1.0),
             reads=[b_w0t], writes=[b_w0t])
        nw, b_nw = emit_col_load(S, bufs, "nw", norm_w, ND)
        ksl, b_ksl = arena.view("ksl", ND * TPC + 2 * ND * 512,
                                ND * TPC + 2 * ND * 512 + H * DH)
        kvsf, b_kvsf = bufs.sb("kvsf", (128, DH), F32)
        kvp0, b_kvp0 = bufs.sb("kvp0", (128, DH), F32)
        ksl_evs = []
        for h in range(H):
            S.dma(kvp0, sh_sum[0][h * 128:(h + 1) * 128, :],
                  deps=[pl1], writes=[b_kvp0])
            S.op("dve", lambda q, h=h: q.tensor_scalar_mul(
                kvsf, kvp0, w0t[:, h:h + 1]),
                reads=[b_kvp0, b_w0t], writes=[b_kvsf])
            ksl_evs.append(S.op(
                "dve", lambda q, h=h: q.tensor_copy(
                    ksl[:, h * DH:(h + 1) * DH], kvsf),
                reads=[b_kvsf], region_of=[b_ksl]))
        S.seal(b_ksl, ksl_evs)

        slab, b_slab = arena.view("a2slab", 0, ND * TPC)
        woT = [arena.view(f"a2woT{i}", ND * TPC + i * ND * 512,
                          ND * TPC + (i + 1) * ND * 512) for i in range(2)]
        qdt = [bufs.tiles[f"st_b{i}"] for i in range(2)]
        att = [bufs.tiles[f"st_b{2 + i}"] for i in range(2)]
        sqt, b_sqt = bufs.sb("sqt", (128, TPC), BF16)
        eps_r, b_epsr = bufs.const(RMS_EPS, parts=1)
        gt = [bufs.tiles[f"st_b{i}"] for i in range(2)]
        t1, b_t1 = bufs.tiles["st_b2"]
        dev = [bufs.sb(f"fdev{i}", (128, 512), BF16) for i in range(3)]
        da_evs = []
        for ch in range(2):
            c0 = ch * TPC
            for h in range(H):
                bb = h % 2
                (tqd, b_tqd), (tat, b_tat) = qdt[bb], att[bb]
                S.dma(tqd, qdec[h * 128:(h + 1) * 128, c0:c0 + TPC],
                      reads=[b_qdec], writes=[b_tqd])
                S.dma(tat, attn_l[h * 128:(h + 1) * 128, c0:c0 + TPC],
                      reads=[b_attnl], writes=[b_tat])
                for nt in range(4):
                    tp, b_tp = pk[nt % 2], b_pk[nt % 2]
                    S.op("pe", lambda q, tqd=tqd, nt=nt, tp=tp, h=h: q.matmul(
                        tp, ksl[:, h * DH:(h + 1) * DH],
                        tqd[:, nt * 512:(nt + 1) * 512],
                        start=True, stop=True),
                        reads=[b_ksl, b_tqd], writes=[b_tp])
                    scols = slab[:, h * TPC + nt * 512:
                                 h * TPC + (nt + 1) * 512]
                    S.op("dve", lambda q, tp=tp, scols=scols, tat=tat, nt=nt:
                         q.tensor_add(scols,
                                      tat[:, nt * 512:(nt + 1) * 512], tp),
                         reads=[b_tp, b_tat], region_of=[b_slab])
                S.seal(b_slab, list(_evs(b_slab.w)) + [("dve",
                                                        S.cnt["dve"])])
                S.op("act", lambda q, h=h: q.activation(
                    sqt, slab[:, h * TPC:(h + 1) * TPC], AF.Square),
                    reads=[b_slab], writes=[b_sqt])
                for mt in range(4):
                    tp, b_tp = pk[2 + mt], b_pk[2 + mt]
                    S.op("pe", lambda q, mt=mt, tp=tp, h=h: q.matmul(
                        tp[0:1, :], onesb, sqt[:, mt * 512:(mt + 1) * 512],
                        start=(h == 0), stop=(h == H - 1)),
                        reads=[b_ob, b_sqt],
                        writes=[b_tp] if h == H - 1 else [],
                        region_of=[] if h == H - 1 else [b_tp])
            r_b, b_rb = arf.view(f"r_b{ch}", 0, TPC)
            rr, b_rr = r_b[0:1, :], S.buf(f"rr{ch}")
            rr_evs = []
            for mt in range(4):
                tp, b_tp = pk[2 + mt], b_pk[2 + mt]
                rr_evs.append(S.op("act", lambda q, tp=tp, mt=mt: q.activation(
                    rr[:, mt * 512:(mt + 1) * 512], tp[0:1, :], AF.Ln,
                    scale=1.0 / D, bias=eps_r), reads=[b_tp, b_epsr],
                    deps=[b_rb.w]))
            S.seal(b_rr, rr_evs)
            S.op("act", lambda q: q.activation(rr, rr, AF.Exp, scale=-0.5),
                 reads=[b_rr], writes=[b_rr])
            rb_evs = []
            for mt in range(4):
                tp, b_tp = pk[mt % 2], b_pk[mt % 2]
                S.op("pe", lambda q, tp=tp, mt=mt: q.matmul(
                    tp, ones1, rr[:, mt * 512:(mt + 1) * 512],
                    start=True, stop=True), reads=[b_o1, b_rr],
                    writes=[b_tp])
                rb_evs.append(S.op(
                    "dve", lambda q, tp=tp, mt=mt: q.tensor_copy(
                        r_b[:, mt * 512:(mt + 1) * 512], tp), reads=[b_tp],
                    region_of=[b_rb]))
            S.seal(b_rb, rb_evs)
            for h in range(H):
                bb = h % 2
                (tg, b_tg) = gt[bb]
                S.dma(tg, gate_s[h * 128:(h + 1) * 128, c0:c0 + TPC],
                      reads=[b_gates], writes=[b_tg])
                S.op("dve", lambda q, h=h: q.scalar_tensor_tensor(
                    t1, slab[:, h * TPC:(h + 1) * TPC], nw[:, h:h + 1], r_b,
                    ALU.mult, ALU.mult), reads=[b_slab, b_nw, b_rb],
                    writes=[b_t1])
                S.op("dve", lambda q, h=h, tg=tg: q.tensor_mul(
                    slab[:, h * TPC:(h + 1) * TPC], t1, tg),
                    reads=[b_t1, b_tg], region_of=[b_slab])
            S.seal(b_slab, [("dve", S.cnt["dve"])])
            for dmb in range(4):
                tw, b_tw = woT[dmb % 2]
                wevs = [S.dmaT(tw[:, jc * 512:(jc + 1) * 512],
                               wo_bf[dmb * 512:(dmb + 1) * 512,
                                     jc * 128:(jc + 1) * 128],
                               reads=[b_wobf], region_of=[b_tw])
                        for jc in range(ND)]
                S.seal(b_tw, wevs)
                for tc in range(TPC // 128):
                    tp, b_tp = pk[4 + tc % 2], b_pk[4 + tc % 2]

                    def mmw(q, tc=tc, tp=tp, tw=tw):
                        r = None
                        for jc in range(ND):
                            r = q.matmul(
                                tp, slab[:, jc * TPC + tc * 128:
                                         jc * TPC + tc * 128 + 128],
                                tw[:, jc * 512:(jc + 1) * 512],
                                start=(jc == 0), stop=(jc == ND - 1))
                        return r
                    S.op("pe", mmw, reads=[b_slab, b_tw], writes=[b_tp])
                    te, b_te = dev[tc % 3]
                    S.op("act", lambda q, tp=tp, te=te: q.activation(
                        te, tp, AF.Copy), reads=[b_tp], writes=[b_te])
                    da_evs.append(S.dma(
                        delta_a[c0 + tc * 128:c0 + (tc + 1) * 128,
                                dmb * 512:(dmb + 1) * 512],
                        te, eng="pool", reads=[b_te], region_of=[b_da]))
        S.seal(b_da, da_evs)
        emit_resid_tm(S, bufs, delta_a, b_da, x, b_x, x1, b_x1, "a",
                      ntok=SPAN)

        for ch in range(2):
            sl = slice(ch * TPC, (ch + 1) * TPC)
            b_x1c = S.alias_buf(b_x1, f"x1c{ch}")
            b_x2c = S.buf(f"x2c{ch}")
            b_dfc = S.alias_buf(b_df, f"dfc{ch}")
            emit_ffn(S, bufs, arena, pk, b_pk, x1[sl, :], b_x1c,
                     x2[sl, :], b_x2c, wg_bf, b_wg, wu_bf, b_wu, wd_bf, b_wd,
                     delta_f[sl, :], b_dfc, ident, b_id, f"1{ch}")
            S.seal(b_x2, list(_evs(b_x2.w)) + list(_evs(b_x2c.w)))

        qsel, b_qsel = bufs.sb("qselt", (128, 1), F32)
        S.dma(qsel, consts["qsel"], writes=[b_qsel])
        xq_evs = []
        for tc in range(QT // 128):
            (ta2, b_ta2) = bufs.tiles["st_f0"]
            (tb2_, b_tb2_) = bufs.tiles["st_f1"]
            S.dma(ta2, x2[tc * 128:(tc + 1) * 128, :], reads=[b_x2],
                  writes=[b_ta2])
            S.dma(tb2_, x2[TPC + tc * 128:TPC + (tc + 1) * 128, :],
                  reads=[b_x2], writes=[b_tb2_])
            S.op("dve", lambda q, ta2=ta2, tb2_=tb2_: q.tensor_sub(
                ta2, ta2, tb2_), reads=[b_tb2_], writes=[b_ta2])
            S.op("dve", lambda q, ta2=ta2, tb2_=tb2_: q.scalar_tensor_tensor(
                ta2, ta2, qsel, tb2_, ALU.mult, ALU.add),
                reads=[b_qsel, b_tb2_], writes=[b_ta2])
            xq_evs.append(S.dma(xq[tc * 128:(tc + 1) * 128, :], ta2,
                                eng="pool", reads=[b_ta2], region_of=[b_xq]))
        S.seal(b_xq, xq_evs)

        C_LN, C_DKV, C_DQ = 0, 8192, 16384
        C_UP = 24576
        C_KR, C_QR = 40960, 41984
        C_CKV, C_CQ = 46080, 48128
        C_QRS = 50176
        dkv_sb, b_dkvs = arena.view("dkv_sb", C_DKV, C_DKV + 8192)
        dq_sb, b_dqs = arena.view("dq_sb", C_DQ, C_DQ + 8192)
        kr_sb_w, b_krs = arena.view("kr_sb_w", C_KR, C_KR + 1024)
        qr_sb_w, b_qrs = arena.view("qr_sb_w", C_QR, C_QR + 4096)
        for dc in range(ND):
            (tt, b_tt) = bufs.tiles["st_f0"]
            (tb, b_tb) = bufs.tiles["cpowo0"]
            S.dma(tt[:, 0:DC], w_dkv[dc * 128:(dc + 1) * 128, :],
                  writes=[b_tt])
            S.op("dve", lambda q, tt=tt, tb=tb: q.tensor_copy(
                tb[:, 0:DC], tt[:, 0:DC]), reads=[b_tt], writes=[b_tb])
            S.op("dve", lambda q, tb=tb, dc=dc: q.tensor_copy(
                dkv_sb[:, dc * 512:(dc + 1) * 512], tb[:, 0:DC]),
                reads=[b_tb], region_of=[b_dkvs])
            S.dma(tt[:, 0:DC], w_dq[dc * 128:(dc + 1) * 128, :],
                  writes=[b_tt])
            S.op("dve", lambda q, tt=tt, tb=tb: q.tensor_copy(
                tb[:, 0:DC], tt[:, 0:DC]), reads=[b_tt], writes=[b_tb])
            S.op("dve", lambda q, tb=tb, dc=dc: q.tensor_copy(
                dq_sb[:, dc * 512:(dc + 1) * 512], tb[:, 0:DC]),
                reads=[b_tb], region_of=[b_dqs])
            S.dma(tt[:, 0:DR], w_kr[dc * 128:(dc + 1) * 128, :],
                  writes=[b_tt])
            S.op("dve", lambda q, tt=tt, dc=dc: q.tensor_copy(
                kr_sb_w[:, dc * 64:(dc + 1) * 64], tt[:, 0:DR]),
                reads=[b_tt], region_of=[b_krs])
        S.seal(b_dkvs, [("dve", S.cnt["dve"])])
        S.seal(b_dqs, [("dve", S.cnt["dve"])])
        S.seal(b_krs, [("dve", S.cnt["dve"])])
        for cc in range(4):
            (tt, b_tt) = bufs.tiles["st_f0"]
            S.dma(tt[:, 0:H * DR], w_qr[cc * 128:(cc + 1) * 128, :],
                  writes=[b_tt])
            dstv = qr_sb_w[:, cc * 1024:(cc + 1) * 1024].rearrange(
                "p (two h f) -> p two h f", two=2, h=H, f=DR // 2)
            srcv = tt[:, 0:H * DR].rearrange(
                "p (h two f) -> p two h f", h=H, two=2, f=DR // 2)
            S.op("dve", lambda q, dstv=dstv, srcv=srcv: q.tensor_copy(
                dstv, srcv), reads=[b_tt], region_of=[b_qrs])
        S.seal(b_qrs, [("dve", S.cnt["dve"])])
        posr, b_posr = bufs.sb("posr", (1, 512), F32)
        invf, b_invf = bufs.sb("invf", (128, 1), F32)
        S.dma(invf, consts["invfreq4"], writes=[b_invf])
        c_n2pi, b_n2pi = bufs.const(-2.0 * np.pi)
        c_hpi, b_hpi = bufs.const(0.5 * np.pi)
        TGC = 512
        upw = [arena.view(f"upw{i}", C_UP + i * 8192, C_UP + (i + 1) * 8192)
               for i in range(2)]
        _dqt, _bdq = bufs.tiles["dqi0"]
        tki, b_tki = _dqt[:, 0:TGC], S.alias_buf(_bdq, "tkir")

        def emit_trig(g_idx, pos_hbm, pcol0, tag):
            pb, b_pb = arf.view(f"pb{tag}", 0, TGC)
            ang, b_ang = arf.view(f"ang{tag}", TGC, 2 * TGC)
            ta_, b_ta_ = arf.view(f"rota{tag}", 5 * TGC, 6 * TGC)
            tb_, b_tb_ = arf.view(f"rotb{tag}", 6 * TGC, 7 * TGC)
            S.dma(posr, pos_hbm[:, pcol0:pcol0 + TGC], writes=[b_posr])
            S.op("pe", lambda q: q.matmul(
                pk[1], ones1, posr, start=True, stop=True),
                reads=[b_o1, b_posr], writes=[b_pk[1]])
            S.op("dve", lambda q: q.tensor_copy(pb, pk[1]), reads=[b_pk[1]],
                 writes=[b_pb])
            S.op("dve", lambda q: q.tensor_scalar_mul(ang, pb, invf),
                 reads=[b_pb, b_invf], writes=[b_ang])
            out = {}
            for trig in ("sin", "cos"):
                tt1, b_tt1 = arf.view(f"tt1{tag}{trig}", 2 * TGC, 3 * TGC)
                off_t = 3 * TGC if trig == "sin" else 4 * TGC
                tst, b_tst = arf.view(f"tst{tag}{trig}", off_t, off_t + TGC)
                off = 0.0 if trig == "sin" else 0.5 * np.pi
                S.op("dve", lambda q, tt1=tt1, off=off: q.tensor_scalar(
                    tt1, ang, off, 1.0 / (2.0 * np.pi), ALU.add, ALU.mult),
                    reads=[b_ang], writes=[b_tt1])
                S.op("dve", lambda q, tt1=tt1: q.tensor_copy(tki, tt1),
                     reads=[b_tt1], writes=[b_tki])
                S.op("dve", lambda q, tt1=tt1: q.tensor_copy(tt1, tki),
                     reads=[b_tki], writes=[b_tt1])
                S.op("dve", lambda q, tt1=tt1, tst=tst:
                     q.scalar_tensor_tensor(tst, tt1, c_n2pi, ang,
                                            ALU.mult, ALU.add),
                     reads=[b_tt1, b_n2pi, b_ang], writes=[b_tst])
                S.op("dve", lambda q, tst=tst, off=off:
                     q.tensor_single_scalar(tb_, tst, float(np.pi) - off,
                                            ALU.is_gt),
                     reads=[b_tst], writes=[b_tb_])
                S.op("dve", lambda q, tst=tst: q.scalar_tensor_tensor(
                    tst, tb_, c_n2pi, tst, ALU.mult, ALU.add),
                    reads=[b_tb_, b_n2pi, b_tst], writes=[b_tst])
                if trig == "sin":
                    S.op("act", lambda q, tst=tst: q.activation(
                        tst, tst, AF.Sin), reads=[b_tst], writes=[b_tst])
                else:
                    S.op("act", lambda q, tst=tst: q.activation(
                        tst, tst, AF.Sin, bias=c_hpi),
                        reads=[b_tst, b_hpi], writes=[b_tst])
                out[trig] = (tst, b_tst)
            return out, (ta_, b_ta_), (tb_, b_tb_)

        Kl_evs, Vl_evs, kRl_evs = [], [], []
        for g in range(SPAN // TGC):
            lnc, b_lnc = arena.view(f"lnc{g}", C_LN, C_LN + ND * TGC)
            emit_ln(S, bufs, x2, b_x2, lnc, b_lnc, g * TGC, TGC, f"c{g}",
                    ident, b_id)
            ckv, b_ckv = arena.view(f"ckv{g}", C_CKV, C_CKV + 2048)
            ev3 = [bufs.sb(f"evc{i}", (128, 512), BF16) for i in range(3)]
            cevs = []
            for cc2 in range(4):
                tp, b_tp = pk[cc2 % 2], b_pk[cc2 % 2]

                def mmc(q, cc2=cc2, tp=tp, lnc=lnc):
                    r = None
                    for dc in range(ND):
                        r = q.matmul(
                            tp, dkv_sb[:, dc * 512 + cc2 * 128:
                                       dc * 512 + cc2 * 128 + 128],
                            lnc[:, dc * TGC:(dc + 1) * TGC],
                            start=(dc == 0), stop=(dc == ND - 1))
                    return r
                S.op("pe", mmc, reads=[b_dkvs, b_lnc], writes=[b_tp])
                cevs.append(S.op(
                    "act", lambda q, tp=tp, cc2=cc2: q.activation(
                        ckv[:, cc2 * 512:(cc2 + 1) * 512], tp, AF.Copy),
                    reads=[b_tp], region_of=[b_ckv]))
            S.seal(b_ckv, cevs)
            for wi, which in ((0, "K"), (1, "V")):
                tw, b_tw = upw[wi % 2]
                wevs = [S.dma(tw[:, cc2 * 2048:(cc2 + 1) * 2048],
                              u3_bf[wi * DC + cc2 * 128:
                                    wi * DC + (cc2 + 1) * 128, :],
                              reads=[b_u3], region_of=[b_tw])
                        for cc2 in range(4)]
                S.seal(b_tw, wevs)
                if which == "K":
                    for ec in range(ND):
                        tp, b_tp = pk[2 + ec % 2], b_pk[2 + ec % 2]

                        def mmk(q, ec=ec, tp=tp, tw=tw):
                            r = None
                            for cc2 in range(4):
                                r = q.matmul(
                                    tp, tw[:, cc2 * 2048 + ec * 128:
                                           cc2 * 2048 + ec * 128 + 128],
                                    ckv[:, cc2 * 512:(cc2 + 1) * 512],
                                    start=(cc2 == 0), stop=(cc2 == 3))
                            return r
                        S.op("pe", mmk, reads=[b_tw, b_ckv], writes=[b_tp])
                        te, b_te = ev3[ec % 3]
                        S.op("act", lambda q, tp=tp, te=te: q.activation(
                            te, tp, AF.Copy), reads=[b_tp], writes=[b_te])
                        Kl_evs.append(S.dma(
                            Kl[ec * 128:(ec + 1) * 128,
                               g * TGC:(g + 1) * TGC], te, eng="pool",
                            reads=[b_te], region_of=[b_Kl]))
                else:
                    for tc in range(4):
                        for eb in range(4):
                            tp, b_tp = pk[2 + eb % 2], b_pk[2 + eb % 2]

                            def mmv(q, tc=tc, eb=eb, tp=tp, tw=tw, ckv=ckv):
                                r = None
                                for cc2 in range(4):
                                    r = q.matmul(
                                        tp, ckv[:, cc2 * 512 + tc * 128:
                                                cc2 * 512 + tc * 128 + 128],
                                        tw[:, cc2 * 2048 + eb * 512:
                                           cc2 * 2048 + (eb + 1) * 512],
                                        start=(cc2 == 0), stop=(cc2 == 3))
                                return r
                            S.op("pe", mmv, reads=[b_tw, b_ckv],
                                 writes=[b_tp])
                            te, b_te = ev3[eb % 3]
                            S.op("act", lambda q, tp=tp, te=te: q.activation(
                                te, tp, AF.Copy), reads=[b_tp],
                                writes=[b_te])
                            Vl_evs.append(S.dma(
                                Vl[g * TGC + tc * 128:
                                   g * TGC + (tc + 1) * 128,
                                   eb * 512:(eb + 1) * 512], te,
                                eng="pool", reads=[b_te], region_of=[b_Vl]))
            krp, b_krp = bufs.sb("krp", (64, TGC), BF16)

            def mmkr(q, lnc=lnc):
                r = None
                for dc in range(ND):
                    r = q.matmul(pk[4][0:64, :],
                                 kr_sb_w[:, dc * 64:(dc + 1) * 64],
                                 lnc[:, dc * TGC:(dc + 1) * TGC],
                                 start=(dc == 0), stop=(dc == ND - 1))
                return r
            S.op("pe", mmkr, reads=[b_krs, b_lnc], writes=[b_pk[4]])
            S.op("act", lambda q: q.activation(krp, pk[4][0:64, :], AF.Copy),
                 reads=[b_pk[4]], writes=[b_krp])
            sc, (ta_, b_ta_), (tb_, b_tb_) = emit_trig(
                g, consts["positions_k"], g * TGC, f"k{g}")
            (tsin, b_tsin), (tcos, b_tcos) = sc["sin"], sc["cos"]
            if "kr2" not in bufs.tiles:
                bufs.tiles["kr2"] = (sqt[0:32, 1024:1024 + TGC],
                                     S.alias_buf(b_sqt, "kr2"))
            kr2, b_kr2 = bufs.tiles["kr2"]
            S.op("dve", lambda q: q.tensor_copy(kr2, krp[32:64, :]),
                 reads=[b_krp], writes=[b_kr2])
            for part in range(2):
                ft_, st_ = (tcos, tsin) if part == 0 else (tsin, tcos)
                S.op("dve", lambda q, ft_=ft_: q.tensor_mul(
                    ta_[0:32, :], krp[0:32, :], ft_[0:32, :]),
                    reads=[b_krp, b_tcos, b_tsin], writes=[b_ta_])
                S.op("dve", lambda q, st_=st_: q.tensor_mul(
                    tb_[0:32, :], kr2, st_[0:32, :]),
                    reads=[b_kr2, b_tcos, b_tsin], writes=[b_tb_])
                alu = ALU.subtract if part == 0 else ALU.add
                if f"krot{part}" not in bufs.tiles:
                    bufs.tiles[f"krot{part}"] = (
                        sqt[0:32, part * TGC:(part + 1) * TGC],
                        S.alias_buf(b_sqt, f"krot{part}"))
                tcb, b_tcb = bufs.tiles[f"krot{part}"]
                S.op("dve", lambda q, tcb=tcb, alu=alu: q.tensor_tensor(
                    tcb, ta_[0:32, :], tb_[0:32, :], alu),
                    reads=[b_ta_, b_tb_], writes=[b_tcb])
                kRl_evs.append(S.dma(
                    kRl[part * 32:(part + 1) * 32, g * TGC:(g + 1) * TGC],
                    tcb, eng="pool", reads=[b_tcb], region_of=[b_kRl]))
        S.seal(b_Kl, Kl_evs)
        S.seal(b_Vl, Vl_evs)
        S.seal(b_kRl, kRl_evs)

        qf_evs = []
        for g in range(QT // TGC):
            lnc, b_lnc = arena.view(f"lnq{g}", C_LN, C_LN + ND * TGC)
            emit_ln(S, bufs, xq, b_xq, lnc, b_lnc, g * TGC, TGC, f"q{g}",
                    ident, b_id)
            cq, b_cq = arena.view(f"cq{g}", C_CQ, C_CQ + 2048)
            ev3 = [bufs.sb(f"evc{i}", (128, 512), BF16) for i in range(3)]
            cevs = []
            for cc2 in range(4):
                tp, b_tp = pk[cc2 % 2], b_pk[cc2 % 2]

                def mmc2(q, cc2=cc2, tp=tp, lnc=lnc):
                    r = None
                    for dc in range(ND):
                        r = q.matmul(
                            tp, dq_sb[:, dc * 512 + cc2 * 128:
                                      dc * 512 + cc2 * 128 + 128],
                            lnc[:, dc * TGC:(dc + 1) * TGC],
                            start=(dc == 0), stop=(dc == ND - 1))
                    return r
                S.op("pe", mmc2, reads=[b_dqs, b_lnc], writes=[b_tp])
                cevs.append(S.op(
                    "act", lambda q, tp=tp, cc2=cc2: q.activation(
                        cq[:, cc2 * 512:(cc2 + 1) * 512], tp, AF.Copy),
                    reads=[b_tp], region_of=[b_cq]))
            S.seal(b_cq, cevs)
            tw, b_tw = upw[0]
            wevs = [S.dma(tw[:, cc2 * 2048:(cc2 + 1) * 2048],
                          u3_bf[2 * DC + cc2 * 128:2 * DC + (cc2 + 1) * 128,
                                :],
                          reads=[b_u3], region_of=[b_tw])
                    for cc2 in range(4)]
            S.seal(b_tw, wevs)
            for ec in range(ND):
                tp, b_tp = pk[2 + ec % 2], b_pk[2 + ec % 2]

                def mmq(q, ec=ec, tp=tp, tw=tw, cq=cq):
                    r = None
                    for cc2 in range(4):
                        r = q.matmul(
                            tp, tw[:, cc2 * 2048 + ec * 128:
                                   cc2 * 2048 + ec * 128 + 128],
                            cq[:, cc2 * 512:(cc2 + 1) * 512],
                            start=(cc2 == 0), stop=(cc2 == 3))
                    return r
                S.op("pe", mmq, reads=[b_tw, b_cq], writes=[b_tp])
                te, b_te = ev3[ec % 3]
                S.op("act", lambda q, tp=tp, te=te: q.activation(
                    te, tp, AF.Copy), reads=[b_tp], writes=[b_te])
                qf_evs.append(S.dma(
                    qf_l[ec * 128:(ec + 1) * 128, g * TGC:(g + 1) * TGC],
                    te, eng="pool", reads=[b_te], region_of=[b_qfl]))
            qrp, b_qrp = arena.view(f"qrp{g}", C_QRS, C_QRS + 8 * TGC)
            qrevs = []
            for rc in range(8):
                tp, b_tp = pk[4 + rc % 2], b_pk[4 + rc % 2]

                def mmr(q, rc=rc, tp=tp, cq=cq):
                    r = None
                    for cc2 in range(4):
                        r = q.matmul(
                            tp, qr_sb_w[:, cc2 * 1024 + rc * 128:
                                        cc2 * 1024 + rc * 128 + 128],
                            cq[:, cc2 * 512:(cc2 + 1) * 512],
                            start=(cc2 == 0), stop=(cc2 == 3))
                    return r
                S.op("pe", mmr, reads=[b_qrs, b_cq], writes=[b_tp])
                qrevs.append(S.op("act", lambda q, tp=tp, rc=rc: q.activation(
                    qrp[:, rc * TGC:(rc + 1) * TGC], tp, AF.Copy),
                    reads=[b_tp], region_of=[b_qrp]))
            S.seal(b_qrp, qrevs)
            sc, (ta_, b_ta_), (tb_, b_tb_) = emit_trig(
                g, consts["positions_q"], g * TGC, f"Q{g}")
            (tsin, b_tsin), (tcos, b_tcos) = sc["sin"], sc["cos"]
            rot, b_rot = arena.view(f"rotq{g}", C_CKV, C_CKV + 8 * TGC)
            rot_evs = []
            for c in range(4):
                x1c = qrp[:, c * TGC:(c + 1) * TGC]
                x2c = qrp[:, (4 + c) * TGC:(5 + c) * TGC]
                S.op("dve", lambda q, x1c=x1c: q.tensor_mul(ta_, x1c, tcos),
                     reads=[b_qrp, b_tcos], writes=[b_ta_])
                S.op("dve", lambda q, x2c=x2c: q.tensor_mul(tb_, x2c, tsin),
                     reads=[b_qrp, b_tsin], writes=[b_tb_])
                rot_evs.append(S.op(
                    "dve", lambda q, c=c: q.tensor_sub(
                        rot[:, c * TGC:(c + 1) * TGC], ta_, tb_),
                    reads=[b_ta_, b_tb_], region_of=[b_rot]))
                S.op("dve", lambda q, x1c=x1c: q.tensor_mul(ta_, x1c, tsin),
                     reads=[b_qrp, b_tsin], writes=[b_ta_])
                S.op("dve", lambda q, x2c=x2c: q.tensor_mul(tb_, x2c, tcos),
                     reads=[b_qrp, b_tcos], writes=[b_tb_])
                rot_evs.append(S.op(
                    "dve", lambda q, c=c: q.tensor_add(
                        rot[:, (4 + c) * TGC:(5 + c) * TGC], ta_, tb_),
                    reads=[b_ta_, b_tb_], region_of=[b_rot]))
            S.seal(b_rot, rot_evs)
            for h in range(H):
                c = h // 4
                r0 = (h % 4) * 32
                qf_evs.append(S.dma(
                    qf_l[D + h * DR:D + h * DR + 32, g * TGC:(g + 1) * TGC],
                    rot[r0:r0 + 32, c * TGC:(c + 1) * TGC], eng="pool",
                    reads=[b_rot], region_of=[b_qfl]))
                qf_evs.append(S.dma(
                    qf_l[D + h * DR + 32:D + (h + 1) * DR,
                         g * TGC:(g + 1) * TGC],
                    rot[r0:r0 + 32, (4 + c) * TGC:(5 + c) * TGC], eng="pool",
                    reads=[b_rot], region_of=[b_qfl]))
        S.seal(b_qfl, qf_evs)

        S.op("pool", lambda q: q.memset(ftile, MAGIC2), reads=[b_fr],
             writes=[b_ft])
        kv_pairs = []
        for i in range(4):
            kv_pairs.append((lambda r, i=i: Kl[i * 512:(i + 1) * 512, :],
                             lambda r, i=i: sh_K[r][i * 512:(i + 1) * 512, :]))
        for i in range(4):
            kv_pairs.append((lambda r, i=i: Vl[i * 1024:(i + 1) * 1024, :],
                             lambda r, i=i: sh_V[r][i * 1024:(i + 1) * 1024,
                                                    :]))
        kv_pairs.append((lambda r: kRl, lambda r: sh_kR[r]))
        ex2 = emit_exchange_write(S, nc, exsemsB, kv_pairs, fl_kv, ftile,
                                  b_ft, MAGIC2,
                                  [b_Kl.w, b_Vl.w, b_kRl.w], "k")
        S.seal(b_shK, [ex2])
        S.seal(b_shV, [ex2])
        S.seal(b_shkR, [ex2])
        pl2 = emit_poll(S, nc, psem2, fl_kv, frtile, b_fr, MAGIC2,
                        [ex2], "k")

        C_CTX = 0
        C_KVA = 32768
        ctx, b_ctx = arena.view("ctx", C_CTX, C_CTX + ND * QT)
        krT, b_kr = arena.view("krT", C_KVA, C_KVA + 8192)
        ksb, b_ks = arena.view("ksb", C_KVA + 8192, C_KVA + 16384)
        vsb, b_vs = arena.view("vsb", C_KVA + 16384, C_KVA + 24576)
        qsb, b_qs = arena.view("qsb", C_KVA + 24576, C_KVA + 26624)
        qrs, b_qr = arena.view("qrs", C_KVA + 26624, C_KVA + 28672)
        psb = [bufs.sb(f"evs{i}", (128, 512), BF16) for i in range(2)]
        lrow, b_lr = posr[0:1, :], S.alias_buf(b_posr, "lrow")
        lbs, b_lb = arf.view("lbs", 0, 512)
        kr_evs = [S.dma(krT[0:64, s * SPAN:(s + 1) * SPAN], sh_kR[s],
                        deps=[pl2], region_of=[b_kr]) for s in range(2)]
        S.seal(b_kr, kr_evs)
        ctx_evs = []
        for h in range(H):
            kevs = [S.dma(ksb[:, s * SPAN:(s + 1) * SPAN],
                          sh_K[s][h * 128:(h + 1) * 128, :],
                          deps=[pl2], region_of=[b_ks]) for s in range(2)]
            S.seal(b_ks, kevs)
            vevs = []
            for s in range(2):
                for kc in range(SPAN // 128):
                    vevs.append(S.dma(
                        vsb[:, s * SPAN + kc * 128:s * SPAN + (kc + 1) * 128],
                        sh_V[s][kc * 128:(kc + 1) * 128,
                                h * 128:(h + 1) * 128],
                        deps=[pl2], region_of=[b_vs]))
            S.seal(b_vs, vevs)
            S.dma(qsb, qf_l[h * 128:(h + 1) * 128, :], reads=[b_qfl],
                  writes=[b_qs])
            S.dma(qrs[0:64, :], qf_l[D + h * DR:D + (h + 1) * DR, :],
                  reads=[b_qfl], writes=[b_qr])
            for mt in range(4):
                b_l = b_pk[4]
                for kc in range(64):
                    tp, b_tp = pk[kc % 2], b_pk[kc % 2]

                    def mmS(q, kc=kc, mt=mt, tp=tp):
                        q.matmul(tp, ksb[:, kc * 128:(kc + 1) * 128],
                                 qsb[:, mt * 512:(mt + 1) * 512],
                                 start=True, stop=False)
                        return q.matmul(
                            tp, krT[0:64, kc * 128:(kc + 1) * 128],
                            qrs[0:64, mt * 512:(mt + 1) * 512],
                            start=False, stop=True)
                    S.op("pe", mmS, reads=[b_ks, b_kr, b_qs, b_qr],
                         writes=[b_tp])
                    tpb, b_tpb = psb[kc % 2]
                    S.op("act", lambda q, tp=tp, tpb=tpb: q.activation(
                        tpb, tp, AF.Exp, scale=SOFT_SC),
                        reads=[b_tp], writes=[b_tpb])
                    S.op("pe", lambda q, tpb=tpb, kc=kc: q.matmul(
                        pk[4][0:1, :], onesb, tpb,
                        start=(kc == 0), stop=(kc == 63)),
                        reads=[b_ob, b_tpb],
                        writes=[b_l] if kc == 63 else [],
                        region_of=[b_l] if kc < 63 else [])
                    b_cx = b_pk[2 + mt % 2]
                    S.op("pe", lambda q, tpb=tpb, kc=kc, mt=mt: q.matmul(
                        pk[2 + mt % 2], vsb[:, kc * 128:(kc + 1) * 128], tpb,
                        start=(kc == 0), stop=(kc == 63)),
                        reads=[b_vs, b_tpb],
                        writes=[b_cx] if kc == 63 else [],
                        region_of=[b_cx] if kc < 63 else [])
                S.op("dve", lambda q: q.tensor_copy(lrow, pk[4][0:1, :]),
                     reads=[b_pk[4]], writes=[b_lr])
                S.op("dve", lambda q: q.reciprocal(lrow, lrow),
                     reads=[b_lr], writes=[b_lr])
                S.op("pe", lambda q: q.matmul(pk[5], ones1, lrow,
                                              start=True, stop=True),
                     reads=[b_o1, b_lr], writes=[b_pk[5]])
                S.op("dve", lambda q: q.tensor_copy(lbs, pk[5]),
                     reads=[b_pk[5]], writes=[b_lb])
                ccols = ctx[:, h * QT + mt * 512:h * QT + (mt + 1) * 512]
                ctx_evs.append(S.op(
                    "dve", lambda q, ccols=ccols, mt=mt: q.tensor_mul(
                        ccols, pk[2 + mt % 2], lbs),
                    reads=[b_pk[2 + mt % 2], b_lb], region_of=[b_ctx]))
        S.seal(b_ctx, ctx_evs)

        wos = [arena.view(f"wos{i}", C_KVA + i * ND * 512,
                          C_KVA + (i + 1) * ND * 512) for i in range(2)]
        bob, b_bob = arf.view("bob", 0, D)
        bob_evs = []
        for mt in range(4):
            S.dma(posr, b_o2[:, mt * 512:(mt + 1) * 512], writes=[b_posr])
            S.op("pe", lambda q, mt=mt: q.matmul(
                pk[mt % 2], ones1, posr,
                start=True, stop=True), reads=[b_o1, b_posr],
                writes=[b_pk[mt % 2]])
            bob_evs.append(S.op(
                "dve", lambda q, mt=mt: q.tensor_copy(
                    bob[:, mt * 512:(mt + 1) * 512], pk[mt % 2]),
                reads=[b_pk[mt % 2]], region_of=[b_bob]))
        S.seal(b_bob, bob_evs)
        x3_evs = []
        xts = [bufs.tiles[f"st_f{i}"] for i in range(2)]
        cbts = [bufs.tiles[f"st_b{i}"] for i in range(2)]
        for dmb in range(4):
            tw, b_tw = wos[dmb % 2]
            wevs = []
            for ec in range(ND):
                (tt, b_tt) = xts[ec % 2]
                (tb, b_tb) = cbts[ec % 2]
                S.dma(tt[:, 0:512],
                      w_o2[ec * 128:(ec + 1) * 128,
                           dmb * 512:(dmb + 1) * 512], writes=[b_tt])
                S.op("dve", lambda q, tt=tt, tb=tb: q.tensor_copy(
                    tb[:, 0:512], tt[:, 0:512]), reads=[b_tt], writes=[b_tb])
                wevs.append(S.op(
                    "dve", lambda q, tb=tb, ec=ec, tw=tw: q.tensor_copy(
                        tw[:, ec * 512:(ec + 1) * 512], tb[:, 0:512]),
                    reads=[b_tb], region_of=[b_tw]))
            S.seal(b_tw, wevs)
            for tc in range(QT // 128):
                tp, b_tp = pk[2 + tc % 2], b_pk[2 + tc % 2]

                def mmo(q, tc=tc, tp=tp, tw=tw):
                    r = None
                    for ec in range(ND):
                        r = q.matmul(
                            tp, ctx[:, ec * QT + tc * 128:
                                    ec * QT + tc * 128 + 128],
                            tw[:, ec * 512:(ec + 1) * 512],
                            start=(ec == 0), stop=(ec == ND - 1))
                    return r
                S.op("pe", mmo, reads=[b_ctx, b_tw], writes=[b_tp])
                (tx, b_tx) = xts[tc % 2]
                S.dma(tx[:, 0:512],
                      xq[tc * 128:(tc + 1) * 128,
                         dmb * 512:(dmb + 1) * 512], reads=[b_xq],
                      writes=[b_tx])
                S.op("dve", lambda q, tx=tx, dmb=dmb: q.tensor_add(
                    tx[:, 0:512], tx[:, 0:512],
                    bob[:, dmb * 512:(dmb + 1) * 512]),
                    reads=[b_bob], writes=[b_tx])
                S.op("dve", lambda q, tx=tx, tp=tp: q.tensor_add(
                    tx[:, 0:512], tx[:, 0:512], tp),
                    reads=[b_tp], writes=[b_tx])
                x3_evs.append(S.dma(
                    x3[tc * 128:(tc + 1) * 128, dmb * 512:(dmb + 1) * 512],
                    tx[:, 0:512], eng="pool", reads=[b_tx],
                    region_of=[b_x3]))
        S.seal(b_x3, x3_evs)

        emit_ffn(S, bufs, arena, pk, b_pk, x3, b_x3, xq_out, b_xqo,
                 wg_bf, b_wg, wu_bf, b_wu, wd_bf, b_wd, delta_2, b_d2,
                 ident, b_id, "2")
        S.op("pool", lambda q: q.nop(), deps=[b_xqo.w])
        S.emit(block)
    return nc


def _fused_inmaps(inputs):
    xs = np.asarray(inputs["L0_attn_a_hidden_states"]).reshape(B * T, D)
    in_maps = []
    for c in range(NCORES):
        cst = host_consts_fused(c)
        b = c // 4
        half = c % 2
        t0 = b * T + half * SPAN
        m = {
            "x": np.ascontiguousarray(xs[t0:t0 + SPAN]),
            "w_qkv": np.asarray(inputs["L0_attn_a_w_qkv"]),
            "w_gate": np.asarray(inputs["L0_attn_a_w_gate"]),
            "slope": np.asarray(
                inputs["L0_attn_a_slope_rate"]).reshape(1, H),
            "w_out": np.asarray(inputs["L0_attn_a_w_out"]),
            "norm_w": np.asarray(inputs["L0_attn_a_norm_w"]),
            "wq_gate": np.asarray(inputs["L0_ffn_wq_gate"]),
            "s_gate": np.asarray(inputs["L0_ffn_s_gate"]),
            "wq_up": np.asarray(inputs["L0_ffn_wq_up"]),
            "s_up": np.asarray(inputs["L0_ffn_s_up"]),
            "wq_down": np.asarray(inputs["L0_ffn_wq_down"]),
            "s_down": np.asarray(inputs["L0_ffn_s_down"]),
            "w_dkv": np.asarray(inputs["L0_attn_b_W_DKV"]),
            "w_dq": np.asarray(inputs["L0_attn_b_W_DQ"]),
            "w_uk": np.asarray(inputs["L0_attn_b_W_UK"]),
            "w_uv": np.asarray(inputs["L0_attn_b_W_UV"]),
            "w_uq": np.asarray(inputs["L0_attn_b_W_UQ"]),
            "w_kr": np.asarray(inputs["L0_attn_b_W_KR"]),
            "w_qr": np.asarray(inputs["L0_attn_b_W_QR"]),
            "w_o2": np.asarray(inputs["L0_attn_b_W_O"]),
            "b_o2": np.asarray(inputs["L0_attn_b_b_O"]).reshape(1, D),
        }
        for n, s, d in FUSED_CONST_SPECS:
            m[n] = np.ascontiguousarray(cst[n].reshape(s))
        in_maps.append(m)
    return in_maps


def _assemble_fused(results):
    out = np.zeros((B * T, D), np.float32)
    for c in range(NCORES):
        b = c // 4
        half = c % 2
        quarter = (c // 2) % 2
        t0 = b * T + half * SPAN + quarter * QT
        out[t0:t0 + QT] = results[c]["xq_out"]
    return out.reshape(B, T, D)


def kernel(**inputs):
    nc = build_fused()
    res = run_bass_kernel_spmd(nc, _fused_inmaps(inputs),
                               core_ids=list(range(NCORES)))
    return np.ascontiguousarray(_assemble_fused(res.results))


def kernel_unfused(**inputs):
    r1 = _run_p1(inputs)
    r2 = _run_p2(inputs, r1)
    r3 = _run_p3(inputs, r2)
    out = np.concatenate([r3[c]["xout"] for c in range(NCORES)], axis=0)
    return np.ascontiguousarray(out.reshape(B, T, D))


# version 05 (final)
# speedup vs baseline: 1.2965x; 1.2965x over previous
import numpy as np
import ml_dtypes
import concourse.bass as bass
import concourse.mybir as mybir
from concourse.bass_utils import run_bass_kernel_spmd

F32 = mybir.dt.float32
BF16 = mybir.dt.bfloat16
I32 = mybir.dt.int32
AF = mybir.ActivationFunctionType
ALU = mybir.AluOpType
AXL = mybir.AxisListType
bfloat16 = ml_dtypes.bfloat16

NCORES = 8
B, T, D = 2, 8192, 2048
H, DH, BLK = 16, 128, 256
HID = 5632
DC, DR = 512, 64
TPC = T * B // NCORES
NBLK = TPC // BLK
ND = D // 128
NH = HID // 128
SOFT_SC = 1.0 / float(np.sqrt(DH + DR))
LN_EPS = 1e-5
RMS_EPS = 1e-6
BIG = 1.0e30

ENGS = ("sp", "act", "dve", "pe", "pool")
NDMASEM = 8
CHANS = (tuple(f"sp{i}" for i in range(NDMASEM)) +
         tuple(f"pD{i}" for i in range(NDMASEM)) +
         ("act", "dve", "pe", "pool"))


def _semaphore_raw(self, name):
    return self.semaphore(name).__enter__()


bass.Bass.semaphore_raw = _semaphore_raw


class Buf:

    __slots__ = ("w", "rs", "name")

    def __init__(self, name=""):
        self.w = None
        self.rs = []
        self.name = name


def _evs(x):
    if x is None:
        return ()
    if isinstance(x, tuple) and x and isinstance(x[0], tuple) and \
            isinstance(x[0][0], str):
        return x
    return (x,)


class Sched:

    def __init__(self, nc):
        self.nc = nc
        self.ops = {e: [] for e in ENGS}
        self.cnt = {c: 0 for c in CHANS}
        self.sems = {}
        self.nbuf = 0
        self.dmaidx = {"sp": 0, "pool": 0}

    def buf(self, name=""):
        self.nbuf += 1
        return Buf(name or f"b{self.nbuf}")

    def alias_buf(self, parent, name=""):
        b = self.buf(name)
        b.w = parent.w
        b.rs = list(parent.rs)
        return b

    def op(self, eng, fn, reads=(), writes=(), region_of=(), deps=(), inc=None):
        if inc is None:
            inc = 16 if eng == "sp" else 1
        dep = set()
        for b in reads:
            dep.update(_evs(b.w))
        for b in tuple(writes) + tuple(region_of):
            dep.update(_evs(b.w))
            dep.update(b.rs)
        for d in deps:
            dep.update(_evs(d))
        if inc == 16:
            i = self.dmaidx[eng]
            self.dmaidx[eng] = (i + 1) % NDMASEM
            chan = (f"sp{i}" if eng == "sp" else f"pD{i}")
            if self.cnt[chan]:
                dep.add((chan, self.cnt[chan]))
        else:
            chan = eng
        self.cnt[chan] += inc
        ev = (chan, self.cnt[chan])
        self.ops[eng].append((sorted(dep), fn, inc, chan))
        for b in reads:
            b.rs.append(ev)
        for b in writes:
            b.w = ev
            b.rs = []
        return ev

    def seal(self, b, events):
        b.w = tuple(events)
        b.rs = []

    def dma(self, out, in_, eng="sp", reads=(), writes=(), region_of=(),
            deps=(), **kw):
        return self.op(eng, lambda q: q.dma_start(out=out, in_=in_, **kw),
                       reads=reads, writes=writes, region_of=region_of,
                       deps=deps, inc=16)

    def dmaT(self, out, in_, eng="sp", reads=(), writes=(), region_of=(),
             deps=()):
        return self.op(eng, lambda q: q.dma_start_transpose(out, in_),
                       reads=reads, writes=writes, region_of=region_of,
                       deps=deps, inc=16)

    def emit(self, block):
        for c in CHANS:
            self.sems[c] = self.nc.semaphore_raw(f"S_{c}")

        def mk(eng_name):
            def body(q):
                seen = {}
                for deps, fn, inc, chan in self.ops[eng_name]:
                    for (se, sv) in deps:
                        if seen.get(se, 0) < sv:
                            q.wait_ge(self.sems[se], sv)
                            seen[se] = sv
                    ins = fn(q)
                    if inc:
                        ins.then_inc(self.sems[chan], inc)
            body.__name__ = "_" + eng_name
            return body

        block.sync(mk("sp"))
        block.scalar(mk("act"))
        block.vector(mk("dve"))
        block.tensor(mk("pe"))
        block.gpsimd(mk("pool"))


class Bufs:
    def __init__(self, nc, S):
        self.nc = nc
        self.S = S
        self.tiles = {}

    def sb(self, name, shape, dtype):
        if name not in self.tiles:
            t = self.nc.sbuf_tensor(name, list(shape), dtype).__enter__()
            self.tiles[name] = (t.ap() if hasattr(t, "ap") else t,
                                self.S.buf(name))
        return self.tiles[name]

    def ps(self, name, shape, dtype):
        if name not in self.tiles:
            t = self.nc.psum_tensor(name, list(shape), dtype).__enter__()
            self.tiles[name] = (t.ap() if hasattr(t, "ap") else t,
                                self.S.buf(name))
        return self.tiles[name]

    def view(self, name, base_name, sl):
        if name not in self.tiles:
            t, b = self.tiles[base_name]
            self.tiles[name] = (t[:, sl], self.S.alias_buf(b, name))
        return self.tiles[name]

    def const(self, value, parts=128):
        name = f"cst_{value!r}_{parts}"
        if name not in self.tiles:
            t = self.nc.sbuf_tensor(name, [parts, 1], F32).__enter__()
            t = t.ap() if hasattr(t, "ap") else t
            b = self.S.buf(name)
            self.S.op("pool", lambda q: q.memset(t, float(value)), writes=[b])
            self.tiles[name] = (t, b)
        return self.tiles[name]


class Arena:

    def __init__(self, nc, S, name, cols, dtype=BF16):
        t = nc.sbuf_tensor(name, [128, cols], dtype).__enter__()
        self.t = t.ap() if hasattr(t, "ap") else t
        self.S = S
        self.views = []

    def view(self, name, lo, hi):
        evs = set()
        keep = []
        for (l2, h2, b2) in self.views:
            if l2 < hi and lo < h2:
                evs.update(_evs(b2.w))
                evs.update(b2.rs)
                if l2 < lo or h2 > hi:
                    keep.append((l2, h2, b2))
            else:
                keep.append((l2, h2, b2))
        b = self.S.buf(name)
        b.w = tuple(sorted(evs)) if evs else None
        self.views = keep + [(lo, hi, b)]
        return self.t[:, lo:hi], b


def emit_identity(S, ident, b_id):
    S.op("pool", lambda q: q.memset(ident, 0.0), writes=[b_id])
    S.op("pool", lambda q: q.affine_select(
        out=ident, in_=ident, compare_op=ALU.not_equal,
        fill=1.0, base=0, pattern=[[-1, ident.shape[1]]],
        channel_multiplier=1), reads=[b_id], writes=[b_id])


def host_consts(core):
    q = core % 4
    c = {}
    ar = np.arange(1, BLK + 1, dtype=np.float32)
    c["qdecay_idx"] = np.tile(ar, (128, 1))
    c["kdecay_idx"] = np.tile(BLK - ar, (128, 1))
    idx = ar[:, None] - ar[None, :]
    ddT = idx.T.copy()
    ddT[ddT < 0] = BIG
    c["ddT0"] = np.ascontiguousarray(ddT[0:128, :], np.float32)
    c["ddT1"] = np.ascontiguousarray(ddT[128:256, :], np.float32)
    c["blkj"] = (np.arange(NBLK, dtype=np.float32) * BLK).reshape(1, -1)
    dist = np.full(3, BIG, np.float32)
    for p in range(3):
        if p < q:
            dist[p] = float(TPC * (q - 1 - p))
    c["dist"] = dist.reshape(1, -1)
    c["positions"] = (q * TPC + np.arange(TPC)).astype(np.float32).reshape(1, -1)
    invf = 1.0 / 10000.0 ** (np.arange(DR // 2, dtype=np.float32) / (DR // 2))
    c["invfreq4"] = np.tile(invf, 4).reshape(-1, 1)
    return c


CONST_SPECS = [
    ("qdecay_idx", (128, BLK), F32),
    ("kdecay_idx", (128, BLK), F32),
    ("ddT0", (128, BLK), F32),
    ("ddT1", (128, BLK), F32),
    ("blkj", (1, NBLK), F32),
    ("dist", (1, 3), F32),
    ("positions", (1, TPC), F32),
    ("invfreq4", (128, 1), F32),
]


def emit_setup(S, bufs):
    nc = S.nc
    ident, b_id = bufs.sb("ident", (128, 128), BF16)
    ones1, b_o1 = bufs.sb("ones1", (1, 128), F32)
    onesb, b_ob = bufs.sb("onesb", (128, 1), BF16)
    emit_identity(S, ident, b_id)
    S.op("pool", lambda q: q.memset(ones1, 1.0), writes=[b_o1])
    S.op("pool", lambda q: q.memset(onesb, 1.0), writes=[b_ob])
    return (ident, b_id), (ones1, b_o1), (onesb, b_ob)


def emit_bcast_row(S, bufs, psum, b_ps, ones1, b_o1, row_ap, row_buf, name, n,
                   parts=128):
    t, b = bufs.sb(name, (parts, n), F32)
    S.op("pe", lambda q: q.matmul(psum[0:parts, 0:n], ones1[:, 0:parts],
                                  row_ap, start=True, stop=True),
         reads=[b_o1, row_buf], writes=[b_ps])
    S.op("dve", lambda q: q.tensor_copy(t, psum[0:parts, 0:n]),
         reads=[b_ps], writes=[b])
    return t, b


def emit_ln(S, bufs, x_hbm, b_x, ln_fm, b_ln, tok0, ntok, tag, ident, b_id,
            eps=LN_EPS):
    ntiles = ntok // 128
    xt = [bufs.sb(f"st_f{i}", (128, D), F32) for i in range(2)]
    sq = [bufs.sb(f"st_b{i}", (128, D), BF16) for i in range(2)]
    st = [bufs.sb(f"lnst{i}", (128, 8), F32) for i in range(2)]
    ltm = [bufs.sb(f"st_b{2 + i}", (128, D), BF16) for i in range(2)]
    pbf, b_pbf = bufs.ps("pbf", (128, 1024), BF16)
    wr_evs = []
    for i in range(ntiles):
        bb = i % 2
        (tx, b_tx), (tq, b_tq), (ts_, b_ts), (tl, b_tl) = \
            xt[bb], sq[bb], st[bb], ltm[bb]
        S.dma(tx, x_hbm[tok0 + i * 128: tok0 + (i + 1) * 128, :],
              reads=[b_x], writes=[b_tx])
        S.op("dve", lambda q, tx=tx, ts_=ts_: q.reduce_sum(
            ts_[:, 0:1], tx, axis=AXL.X), reads=[b_tx], writes=[b_ts])
        S.op("act", lambda q, tx=tx, tq=tq, ts_=ts_: q.activation(
            tq, tx, AF.Square, accum_out=ts_[:, 1:2]),
            reads=[b_tx], writes=[b_tq, b_ts])
        S.op("dve", lambda q, ts_=ts_: q.tensor_scalar_mul(
            ts_[:, 2:3], ts_[:, 0:1], 1.0 / D), reads=[b_ts], writes=[b_ts])
        S.op("act", lambda q, ts_=ts_: q.activation(
            ts_[:, 3:4], ts_[:, 0:1], AF.Square, scale=1.0 / D),
            reads=[b_ts], writes=[b_ts])
        S.op("dve", lambda q, ts_=ts_: q.tensor_scalar(
            ts_[:, 4:5], ts_[:, 1:2], 1.0 / D, ts_[:, 3:4], ALU.mult,
            ALU.subtract), reads=[b_ts], writes=[b_ts])
        te_, b_te_ = bufs.const(eps)
        S.op("act", lambda q, ts_=ts_, te_=te_: q.activation(
            ts_[:, 5:6], ts_[:, 4:5], AF.Ln, bias=te_),
            reads=[b_ts, b_te_], writes=[b_ts])
        S.op("act", lambda q, ts_=ts_: q.activation(
            ts_[:, 6:7], ts_[:, 5:6], AF.Exp, scale=-0.5),
            reads=[b_ts], writes=[b_ts])
        S.op("dve", lambda q, tx=tx, tl=tl, ts_=ts_: q.tensor_scalar(
            tl, tx, ts_[:, 2:3], ts_[:, 6:7], ALU.subtract, ALU.mult),
            reads=[b_tx, b_ts], writes=[b_tl])
        for g in range(2):
            def tgroup(q, tl=tl, g=g):
                r = None
                for c8 in range(8):
                    c = g * 8 + c8
                    r = q.transpose(pbf[:, c8 * 128:(c8 + 1) * 128],
                                    tl[:, c * 128:(c + 1) * 128], ident)
                return r
            S.op("pe", tgroup, reads=[b_tl, b_id], writes=[b_pbf])
            dst = ln_fm.rearrange("p (c t) -> p c t", c=ND)[
                :, g * 8:(g + 1) * 8, i * 128:(i + 1) * 128]
            wr_evs.append(S.op("dve", lambda q, dst=dst: q.tensor_copy(
                dst, pbf.rearrange("p (c t) -> p c t", c=8)),
                reads=[b_pbf], region_of=[b_ln]))
    S.seal(b_ln, wr_evs)


def build_p1():
    nc = bass.Bass("TRN2", target_bir_lowering=False)
    x = nc.dram_tensor("x", [TPC, D], F32, kind="ExternalInput").ap()
    w_qkv = nc.dram_tensor("w_qkv", [3 * D, D], F32, kind="ExternalInput").ap()
    w_gate = nc.dram_tensor("w_gate", [D, D], F32, kind="ExternalInput").ap()
    slope = nc.dram_tensor("slope", [1, H], F32, kind="ExternalInput").ap()
    consts = {n: nc.dram_tensor(n, list(s), d, kind="ExternalInput").ap()
              for n, s, d in CONST_SPECS}
    attn_local = nc.dram_tensor("attn_local", [D, TPC], BF16,
                                kind="ExternalOutput").ap()
    qdec_g = nc.dram_tensor("qdec_g", [D, TPC], BF16,
                            kind="ExternalOutput").ap()
    gate_o = nc.dram_tensor("gate_o", [D, TPC], BF16,
                            kind="ExternalOutput").ap()
    kvsum = nc.dram_tensor("kvsum", [D, DH], F32, kind="ExternalOutput").ap()
    wbf = nc.dram_tensor("wbf", [4 * D, D], BF16).ap()
    qkv_fm = nc.dram_tensor("qkv_fm", [3 * D, TPC], BF16).ap()

    S = Sched(nc)
    bufs = Bufs(nc, S)
    b_x, b_wbf, b_qkvfm = S.buf("x"), S.buf("wbf"), S.buf("qkv_fm")
    b_attnl, b_qdecg, b_gateo, b_kvsum = (S.buf("attn_local"), S.buf("qdec_g"),
                                          S.buf("gate_o"), S.buf("kvsum"))

    with nc.Block() as block:
        (ident, b_id), (ones1, b_o1), (onesb, b_ob) = emit_setup(S, bufs)

        ln_fm, b_ln = bufs.sb("ln_fm", (128, ND * TPC), BF16)
        emit_ln(S, bufs, x, b_x, ln_fm, b_ln, 0, TPC, "1", ident, b_id)

        wc = [bufs.tiles[f"st_f{i}"] for i in range(2)]
        wcb = [bufs.tiles[f"st_b{i}"] for i in range(2)]
        wbf_evs = []
        for i in range(4 * ND):
            bb = i % 2
            (tc, b_tc), (tb, b_tb) = wc[bb], wcb[bb]
            src = (w_qkv[i * 128:(i + 1) * 128, :] if i < 3 * ND else
                   w_gate[(i - 3 * ND) * 128:(i - 3 * ND + 1) * 128, :])
            S.dma(tc, src, writes=[b_tc])
            S.op("dve", lambda q, tc=tc, tb=tb: q.tensor_copy(tb, tc),
                 reads=[b_tc], writes=[b_tb])
            wbf_evs.append(S.dma(wbf[i * 128:(i + 1) * 128, :], tb, eng="pool",
                                 reads=[b_tb], region_of=[b_wbf]))
        S.seal(b_wbf, wbf_evs)

        pk, b_pk = [], []
        for i in range(6):
            t, b = bufs.ps(f"pk{i}", (128, 512), F32)
            pk.append(t)
            b_pk.append(b)

        wt = [bufs.sb(f"wt{i}", (128, 512 * ND), BF16) for i in range(2)]
        ev = [bufs.sb(f"evs{i}", (128, 512), BF16) for i in range(3)]
        qkv_evs, gate_evs = [], []
        for jb in range((4 * D) // 512):
            bb = jb % 2
            tw, b_tw = wt[bb]
            wt_evs = []
            for dc in range(ND):
                wt_evs.append(S.dmaT(
                    tw[:, dc * 512:(dc + 1) * 512],
                    wbf[jb * 512:(jb + 1) * 512, dc * 128:(dc + 1) * 128],
                    reads=[b_wbf], region_of=[b_tw]))
            S.seal(b_tw, wt_evs)
            for j4 in range(4):
                jc = jb * 4 + j4
                for nt in range(4):
                    k = (jc * 4 + nt) % 6
                    tp, b_tp = pk[k], b_pk[k]

                    def mmg(q, tw=tw, j4=j4, nt=nt, tp=tp):
                        r = None
                        for dc in range(ND):
                            r = q.matmul(
                                tp,
                                tw[:, dc * 512 + j4 * 128:
                                   dc * 512 + j4 * 128 + 128],
                                ln_fm[:, dc * TPC + nt * 512:
                                      dc * TPC + (nt + 1) * 512],
                                start=(dc == 0), stop=(dc == ND - 1))
                        return r
                    S.op("pe", mmg, reads=[b_tw, b_ln], writes=[b_tp])
                    te, b_te = ev[(jc * 4 + nt) % 3]
                    func = AF.Silu if jc < 3 * ND else AF.Sigmoid
                    S.op("act", lambda q, tp=tp, te=te, func=func:
                         q.activation(te, tp, func), reads=[b_tp],
                         writes=[b_te])
                    if jc < 3 * ND:
                        qkv_evs.append(S.dma(
                            qkv_fm[jc * 128:(jc + 1) * 128,
                                   nt * 512:(nt + 1) * 512],
                            te, eng="pool", reads=[b_te], region_of=[b_qkvfm]))
                    else:
                        row = (jc - 3 * ND) * 128
                        gate_evs.append(S.dma(
                            gate_o[row:row + 128, nt * 512:(nt + 1) * 512],
                            te, eng="pool", reads=[b_te], region_of=[b_gateo]))
        S.seal(b_qkvfm, qkv_evs)
        S.seal(b_gateo, gate_evs)

        sl_row, b_slr = bufs.sb("sl_row", (1, H), F32)
        S.dma(sl_row, slope, writes=[b_slr])
        sb_s, b_ss = emit_bcast_row(S, bufs, pk[0], b_pk[0], ones1, b_o1,
                                    sl_row, b_slr, "sb_s", H)
        bj_row, b_bjr = bufs.sb("bj_row", (1, NBLK), F32)
        S.dma(bj_row, consts["blkj"], writes=[b_bjr])
        bjb, b_bjb = emit_bcast_row(S, bufs, pk[1], b_pk[1], ones1, b_o1,
                                    bj_row, b_bjr, "bjb", NBLK)
        qdi, b_qdi = arf.view("qdi", 0, BLK)
        kdi, b_kdi = arf.view("kdi", BLK, 2 * BLK)
        dd0, b_dd0 = arf.view("dd0", 2 * BLK, 3 * BLK)
        dd1, b_dd1 = arf.view("dd1", 3 * BLK, 4 * BLK)
        S.dma(qdi, consts["qdecay_idx"], writes=[b_qdi])
        S.dma(kdi, consts["kdecay_idx"], writes=[b_kdi])
        S.dma(dd0, consts["ddT0"], writes=[b_dd0])
        S.dma(dd1, consts["ddT1"], writes=[b_dd1])
        dblk, b_dblk = bufs.sb("dblk", (128, H), F32)
        S.op("act", lambda q: q.activation(dblk, sb_s, AF.Exp,
                                           scale=-float(BLK)),
             reads=[b_ss], writes=[b_dblk])
        dgl, b_dgl = bufs.sb("dgl", (128, H * NBLK), F32)
        dgl_evs = []
        for h in range(H):
            dgl_evs.append(S.op("dve", lambda q, h=h: q.tensor_scalar_mul(
                dgl[:, h * NBLK:(h + 1) * NBLK], bjb, sb_s[:, h:h + 1]),
                reads=[b_bjb, b_ss], region_of=[b_dgl]))
        S.seal(b_dgl, dgl_evs)
        S.op("act", lambda q: q.activation(dgl, dgl, AF.Exp, scale=-1.0),
             reads=[b_dgl], writes=[b_dgl])
        qdb, b_qdb = bufs.view("qdb", "wt0", slice(0, H * BLK))
        kdb, b_kdb = bufs.view("kdb", "wt0", slice(H * BLK, 2 * H * BLK))
        ddb, b_ddb = bufs.view("ddb", "wt1", slice(0, 2 * H * BLK))
        tmpd, b_tmpd = bufs.sb("tmpd", (128, BLK), F32)
        for h in range(H):
            for (srcT, b_src, dstT, b_dst, col) in (
                    (qdi, b_qdi, qdb, b_qdb, h * BLK),
                    (kdi, b_kdi, kdb, b_kdb, h * BLK),
                    (dd0, b_dd0, ddb, b_ddb, 2 * h * BLK),
                    (dd1, b_dd1, ddb, b_ddb, (2 * h + 1) * BLK)):
                S.op("dve", lambda q, srcT=srcT, h=h: q.tensor_scalar_mul(
                    tmpd, srcT, sb_s[:, h:h + 1]),
                    reads=[b_src, b_ss], writes=[b_tmpd])
                S.op("act", lambda q, dstT=dstT, col=col: q.activation(
                    dstT[:, col:col + BLK], tmpd, AF.Exp, scale=-1.0),
                    reads=[b_tmpd], region_of=[b_dst])
        for b in (b_qdb, b_kdb, b_ddb):
            b.w = tuple(b.w if isinstance(b.w, tuple) else ())
        ready = ("act", S.cnt["act"])
        for b in (b_qdb, b_kdb, b_ddb):
            S.seal(b, [ready])

        pbf, b_pbf = bufs.tiles["pbf"]
        qf = [bufs.sb(f"qf{i}", (128, TPC), BF16) for i in range(2)]
        kf = [bufs.sb(f"kf{i}", (128, TPC), BF16) for i in range(2)]
        vf = [bufs.sb(f"vf{i}", (128, TPC), BF16) for i in range(2)]
        kvf, b_kvf = bufs.sb("kvf", (128, DH), F32)
        kvb, b_kvb = bufs.sb("kvb", (128, DH), BF16)
        qdx, b_qdx = bufs.sb("qdx", (128, BLK), BF16)
        qgx, b_qgx = bufs.sb("qgx", (128, BLK), BF16)
        kdx, b_kdx = bufs.sb("kdx", (128, BLK), BF16)
        vkm, b_vkm = bufs.sb("vkm", (128, 2 * 128), BF16)
        kkm, b_kkm = bufs.sb("kkm", (128, 2 * 128), BF16)
        ptt, b_ptt = bufs.sb("ptt", (128, 2 * BLK), BF16)
        aev = [bufs.sb(f"aev{i}", (128, BLK), BF16) for i in range(2)]
        attnl_evs, qdecg_evs, kvsum_evs = [], [], []
        for h in range(H):
            hb = h % 2
            (tq, b_tq), (tk, b_tk), (tv, b_tv) = qf[hb], kf[hb], vf[hb]
            S.dma(tq, qkv_fm[h * 128:(h + 1) * 128, :],
                  reads=[b_qkvfm], writes=[b_tq])
            S.dma(tk, qkv_fm[D + h * 128:D + (h + 1) * 128, :],
                  reads=[b_qkvfm], writes=[b_tk])
            S.dma(tv, qkv_fm[2 * D + h * 128:2 * D + (h + 1) * 128, :],
                  reads=[b_qkvfm], writes=[b_tv])
            S.op("dve", lambda q: q.memset(kvf, 0.0), writes=[b_kvf])
            S.op("dve", lambda q: q.memset(kvb, 0.0), writes=[b_kvb])
            for j in range(NBLK):
                t0 = j * BLK
                S.op("dve", lambda q, tq=tq, t0=t0, h=h: q.tensor_mul(
                    qdx, tq[:, t0:t0 + BLK], qdb[:, h * BLK:(h + 1) * BLK]),
                    reads=[b_tq, b_qdb], writes=[b_qdx])
                S.op("dve", lambda q, h=h, j=j: q.tensor_scalar_mul(
                    qgx, qdx, dgl[:, h * NBLK + j:h * NBLK + j + 1]),
                    reads=[b_qdx, b_dgl], writes=[b_qgx])
                qdecg_evs.append(S.dma(
                    qdec_g[h * 128:(h + 1) * 128, t0:t0 + BLK], qgx,
                    eng="pool", reads=[b_qgx], region_of=[b_qdecg]))
                S.op("dve", lambda q, tk=tk, t0=t0, h=h: q.tensor_mul(
                    kdx, tk[:, t0:t0 + BLK], kdb[:, h * BLK:(h + 1) * BLK]),
                    reads=[b_tk, b_kdb], writes=[b_kdx])

                def tgroup(q, tv=tv, t0=t0):
                    for cc in range(2):
                        q.transpose(pbf[:, cc * 128:(cc + 1) * 128],
                                    tv[:, t0 + cc * 128:t0 + (cc + 1) * 128],
                                    ident)
                    for cc in range(2):
                        r = q.transpose(pbf[:, 512 + cc * 128:
                                            512 + (cc + 1) * 128],
                                        kdx[:, cc * 128:(cc + 1) * 128], ident)
                    return r
                S.op("pe", tgroup, reads=[b_tv, b_kdx, b_id], writes=[b_pbf])
                S.op("dve", lambda q: q.tensor_copy(
                    vkm, pbf[:, 0:256]), reads=[b_pbf], writes=[b_vkm])
                S.op("dve", lambda q: q.tensor_copy(
                    kkm, pbf[:, 512:768]), reads=[b_pbf], writes=[b_kkm])
                for cc in range(2):
                    tp, b_tp = pk[4 + cc], b_pk[4 + cc]
                    S.op("pe", lambda q, tk=tk, tq=tq, t0=t0, cc=cc, tp=tp:
                         q.matmul(tp[:, 0:BLK],
                                  tk[:, t0 + cc * 128:t0 + (cc + 1) * 128],
                                  tq[:, t0:t0 + BLK], start=True, stop=True),
                         reads=[b_tk, b_tq], writes=[b_tp])
                    S.op("dve", lambda q, cc=cc, h=h, tp=tp: q.tensor_mul(
                        ptt[:, cc * BLK:(cc + 1) * BLK], tp[:, 0:BLK],
                        ddb[:, (2 * h + cc) * BLK:(2 * h + cc + 1) * BLK]),
                        reads=[b_tp, b_ddb], region_of=[b_ptt],
                        deps=[b_ptt.w])
                S.seal(b_ptt, [("dve", S.cnt["dve"])])
                tp, b_tp = pk[j % 2], b_pk[j % 2]

                def outg(q, tp=tp):
                    q.matmul(tp[:, 0:BLK], kvb, qdx, start=True, stop=False)
                    q.matmul(tp[:, 0:BLK], vkm[:, 0:128], ptt[:, 0:BLK],
                             start=False, stop=False)
                    return q.matmul(tp[:, 0:BLK], vkm[:, 128:256],
                                    ptt[:, BLK:2 * BLK], start=False,
                                    stop=True)
                S.op("pe", outg, reads=[b_kvb, b_qdx, b_vkm, b_ptt],
                     writes=[b_tp])
                ta, b_ta = aev[j % 2]
                S.op("act", lambda q, tp=tp, ta=ta: q.activation(
                    ta, tp[:, 0:BLK], AF.Copy), reads=[b_tp], writes=[b_ta])
                attnl_evs.append(S.dma(
                    attn_local[h * 128:(h + 1) * 128, t0:t0 + BLK], ta,
                    eng="pool", reads=[b_ta], region_of=[b_attnl]))
                tpv, b_tpv = pk[2 + (j % 2)], b_pk[2 + (j % 2)]

                def kvg(q, tpv=tpv):
                    q.matmul(tpv[:, 0:DH], kkm[:, 0:128], vkm[:, 0:128],
                             start=True, stop=False)
                    return q.matmul(tpv[:, 0:DH], kkm[:, 128:256],
                                    vkm[:, 128:256], start=False, stop=True)
                S.op("pe", kvg, reads=[b_kkm, b_vkm], writes=[b_tpv])
                S.op("dve", lambda q, h=h, tpv=tpv: q.scalar_tensor_tensor(
                    kvf, kvf, dblk[:, h:h + 1], tpv[:, 0:DH],
                    ALU.mult, ALU.add),
                    reads=[b_dblk, b_tpv, b_kvf], writes=[b_kvf])
                if j < NBLK - 1:
                    S.op("dve", lambda q: q.tensor_copy(kvb, kvf),
                         reads=[b_kvf], writes=[b_kvb])
            kvsum_evs.append(S.dma(kvsum[h * 128:(h + 1) * 128, :], kvf,
                                   eng="pool", reads=[b_kvf],
                                   region_of=[b_kvsum]))
        S.seal(b_attnl, attnl_evs)
        S.seal(b_qdecg, qdecg_evs)
        S.seal(b_kvsum, kvsum_evs)
        S.op("pool", lambda q: q.nop(),
             deps=[b_attnl.w, b_qdecg.w, b_gateo.w, b_kvsum.w])
        S.emit(block)
    return nc


def emit_resid_tm(S, bufs, delta_hbm, b_delta, xsrc, b_xs, xdst, b_xd, tag,
                  ntok=TPC):
    dst_evs = []
    dtm = [bufs.sb(f"st_b{i}", (128, D), BF16) for i in range(2)]
    xt = [bufs.sb(f"st_f{i}", (128, D), F32) for i in range(2)]
    for i in range(ntok // 128):
        bb = i % 2
        (tdt, b_tdt), (tx, b_tx) = dtm[bb], xt[bb]
        S.dma(tdt, delta_hbm[i * 128:(i + 1) * 128, :], reads=[b_delta],
              writes=[b_tdt])
        S.dma(tx, xsrc[i * 128:(i + 1) * 128, :], reads=[b_xs], writes=[b_tx])
        S.op("dve", lambda q, tx=tx, tdt=tdt: q.tensor_add(tx, tx, tdt),
             reads=[b_tdt, b_tx], writes=[b_tx])
        dst_evs.append(S.dma(xdst[i * 128:(i + 1) * 128, :], tx, eng="pool",
                             reads=[b_tx], region_of=[b_xd]))
    S.seal(b_xd, dst_evs)


def emit_dequant(S, bufs, wq, s_sb, b_ssb, scol, wbf_hbm, b_wbf, nrow_chunks,
                 width, tag):
    ti = [bufs.sb("dqi0", (128, 2048), I32)] * 2
    to = [bufs.sb("dqo0", (128, 2048), BF16)] * 2
    evs = []
    col_chunks = [(c0, min(2048, width - c0)) for c0 in range(0, width, 2048)]
    n = 0
    for i in range(nrow_chunks):
        for (c0, w) in col_chunks:
            bb = n % 2
            n += 1
            (tt, b_tt), (tb, b_tb) = ti[bb], to[bb]
            S.dma(tt[:, 0:w], wq[i * 128:(i + 1) * 128, c0:c0 + w],
                  writes=[b_tt])
            S.op("pool", lambda q, tt=tt, tb=tb, i=i, w=w: q.tensor_scalar(
                tb[:, 0:w], tt[:, 0:w], -128.0,
                s_sb[:, scol + i:scol + i + 1], ALU.add, ALU.mult),
                reads=[b_tt, b_ssb], writes=[b_tb])
            evs.append(S.dma(wbf_hbm[i * 128:(i + 1) * 128, c0:c0 + w],
                             tb[:, 0:w], eng="pool", reads=[b_tb],
                             region_of=[b_wbf]))
    S.seal(b_wbf, evs)


def emit_col_load(S, bufs, name, vec_hbm, ncol, parts=128):
    t, b = bufs.sb(name, (parts, ncol), F32)
    evs = []
    for c in range(ncol):
        evs.append(S.dma(t[:, c:c + 1],
                         vec_hbm[c * parts:(c + 1) * parts].rearrange(
                             "(p o) -> p o", o=1),
                         region_of=[b]))
    S.seal(b, evs)
    return t, b


def emit_cast_pass(S, bufs, src_hbm, dst_hbm, b_dst, nrow_chunks, width, tag):
    ti = [bufs.sb("st_f0", (128, D), F32)] * 2
    to = [bufs.sb(f"cpo{tag}0", (128, width), BF16)] * 2
    evs = []
    for i in range(nrow_chunks):
        bb = i % 2
        (tt, b_tt), (tb, b_tb) = ti[bb], to[bb]
        S.dma(tt, src_hbm[i * 128:(i + 1) * 128, :], writes=[b_tt])
        S.op("dve", lambda q, tt=tt, tb=tb: q.tensor_copy(tb, tt),
             reads=[b_tt], writes=[b_tb])
        evs.append(S.dma(dst_hbm[i * 128:(i + 1) * 128, :], tb, eng="pool",
                         reads=[b_tb], region_of=[b_dst]))
    S.seal(b_dst, evs)


def emit_ffn(S, bufs, arena, pk, b_pk, x_hbm, b_x, xdst, b_xd,
             wg_bf, b_wg, wu_bf, b_wu, wd_bf, b_wd, delta_hbm, b_delta,
             ident, b_id, tag, TG=512):
    NT = TG // 512
    NJB = HID // 512
    C_H = ND * TG
    C_ST = C_H + NH * TG
    delta_evs = []
    dev = [bufs.sb(f"fdev{i}", (128, 512), BF16) for i in range(3)]
    for g in range(TPC // TG):
        ln_f, b_lnf = arena.view(f"lnf{tag}{g}", 0, C_H)
        emit_ln(S, bufs, x_hbm, b_x, ln_f, b_lnf, g * TG, TG, f"f{tag}{g}",
                ident, b_id)
        h_slab, b_h = arena.view(f"hsl{tag}{g}", C_H, C_ST)
        wst = [arena.view(f"wst{tag}{g}{i}", C_ST + i * ND * 512,
                          C_ST + (i + 1) * ND * 512) for i in range(2)]
        for which, wsrc, b_ws in (("g", wg_bf, b_wg), ("u", wu_bf, b_wu)):
            for jb in range(NJB):
                bb = jb % 2
                tw, b_tw = wst[bb]
                wevs = [S.dmaT(tw[:, dc * 512:(dc + 1) * 512],
                               wsrc[jb * 512:(jb + 1) * 512,
                                    dc * 128:(dc + 1) * 128],
                               reads=[b_ws], region_of=[b_tw])
                        for dc in range(ND)]
                S.seal(b_tw, wevs)
                for j4 in range(4):
                    jc = jb * 4 + j4
                    for nt in range(NT):
                        k = (jc * NT + nt) % 4
                        tp, b_tp = pk[k], b_pk[k]

                        def mmg(q, tw=tw, j4=j4, nt=nt, tp=tp, ln_f=ln_f):
                            r = None
                            for dc in range(ND):
                                r = q.matmul(
                                    tp,
                                    tw[:, dc * 512 + j4 * 128:
                                       dc * 512 + j4 * 128 + 128],
                                    ln_f[:, dc * TG + nt * 512:
                                         dc * TG + (nt + 1) * 512],
                                    start=(dc == 0), stop=(dc == ND - 1))
                            return r
                        S.op("pe", mmg, reads=[b_tw, b_lnf], writes=[b_tp])
                        hcols = h_slab[:, jc * TG + nt * 512:
                                       jc * TG + (nt + 1) * 512]
                        if which == "g":
                            S.op("act", lambda q, tp=tp, hcols=hcols:
                                 q.activation(hcols, tp, AF.Silu),
                                 reads=[b_tp], region_of=[b_h])
                        else:
                            S.op("dve", lambda q, tp=tp, hcols=hcols:
                                 q.tensor_mul(hcols, hcols, tp),
                                 reads=[b_tp], region_of=[b_h])
            S.seal(b_h, [("act" if which == "g" else "dve",
                          S.cnt["act" if which == "g" else "dve"])])
        NHH = NH // 2
        wsdh = [arena.view(f"wsd{tag}{g}{i}", C_ST + i * NHH * 512,
                           C_ST + (i + 1) * NHH * 512) for i in range(2)]
        for dmb in range(4):
            for half in range(2):
                tw, b_tw = wsdh[half]
                wevs = [S.dmaT(tw[:, jj * 512:(jj + 1) * 512],
                               wd_bf[dmb * 512:(dmb + 1) * 512,
                                     (half * NHH + jj) * 128:
                                     (half * NHH + jj + 1) * 128],
                               reads=[b_wd], region_of=[b_tw])
                        for jj in range(NHH)]
                S.seal(b_tw, wevs)
            for tc in range(TG // 128):
                k = 4 + (tc % 2)
                tp, b_tp = pk[k], b_pk[k]
                for half in range(2):
                    tw, b_tw = wsdh[half]

                    def mmd(q, tc=tc, tp=tp, h_slab=h_slab, tw=tw,
                            half=half):
                        r = None
                        for jj in range(NHH):
                            jc = half * NHH + jj
                            r = q.matmul(
                                tp, h_slab[:, jc * TG + tc * 128:
                                           jc * TG + tc * 128 + 128],
                                tw[:, jj * 512:(jj + 1) * 512],
                                start=(jc == 0), stop=(jc == NH - 1))
                        return r
                    S.op("pe", mmd, reads=[b_h, b_tw],
                         writes=[b_tp] if half == 1 else [],
                         region_of=[b_tp] if half == 0 else [])
                te, b_te = dev[tc % 3]
                S.op("act", lambda q, tp=tp, te=te: q.activation(
                    te, tp, AF.Copy), reads=[b_tp], writes=[b_te])
                delta_evs.append(S.dma(
                    delta_hbm[g * TG + tc * 128:g * TG + (tc + 1) * 128,
                              dmb * 512:(dmb + 1) * 512],
                    te, eng="pool", reads=[b_te], region_of=[b_delta]))
    S.seal(b_delta, delta_evs)
    emit_resid_tm(S, bufs, delta_hbm, b_delta, x_hbm, b_x, xdst, b_xd,
                  f"f{tag}")


def build_p2():
    nc = bass.Bass("TRN2", target_bir_lowering=False)
    x = nc.dram_tensor("x", [TPC, D], F32, kind="ExternalInput").ap()
    attn_l = nc.dram_tensor("attn_local", [D, TPC], BF16,
                            kind="ExternalInput").ap()
    qdec_g = nc.dram_tensor("qdec_g", [D, TPC], BF16,
                            kind="ExternalInput").ap()
    gate_o = nc.dram_tensor("gate_o", [D, TPC], BF16,
                            kind="ExternalInput").ap()
    kvpre = nc.dram_tensor("kvpre", [3 * D, DH], F32,
                           kind="ExternalInput").ap()
    slope = nc.dram_tensor("slope", [1, H], F32, kind="ExternalInput").ap()
    w_out = nc.dram_tensor("w_out", [D, D], F32, kind="ExternalInput").ap()
    norm_w = nc.dram_tensor("norm_w", [D], F32, kind="ExternalInput").ap()
    wq_g = nc.dram_tensor("wq_gate", [HID, D], I32, kind="ExternalInput").ap()
    s_g = nc.dram_tensor("s_gate", [HID], F32, kind="ExternalInput").ap()
    wq_u = nc.dram_tensor("wq_up", [HID, D], I32, kind="ExternalInput").ap()
    s_u = nc.dram_tensor("s_up", [HID], F32, kind="ExternalInput").ap()
    wq_d = nc.dram_tensor("wq_down", [D, HID], I32, kind="ExternalInput").ap()
    s_d = nc.dram_tensor("s_down", [D], F32, kind="ExternalInput").ap()
    w_dkv = nc.dram_tensor("w_dkv", [D, DC], F32, kind="ExternalInput").ap()
    w_dq = nc.dram_tensor("w_dq", [D, DC], F32, kind="ExternalInput").ap()
    w_uk = nc.dram_tensor("w_uk", [DC, D], F32, kind="ExternalInput").ap()
    w_uv = nc.dram_tensor("w_uv", [DC, D], F32, kind="ExternalInput").ap()
    w_uq = nc.dram_tensor("w_uq", [DC, D], F32, kind="ExternalInput").ap()
    w_kr = nc.dram_tensor("w_kr", [D, DR], F32, kind="ExternalInput").ap()
    w_qr = nc.dram_tensor("w_qr", [DC, H * DR], F32,
                          kind="ExternalInput").ap()
    consts = {n: nc.dram_tensor(n, list(s), d, kind="ExternalInput").ap()
              for n, s, d in CONST_SPECS}
    x2 = nc.dram_tensor("x2", [TPC, D], F32, kind="ExternalOutput").ap()
    kv_pub = nc.dram_tensor("kv_pub", [2 * D + DR, TPC], BF16,
                            kind="ExternalOutput").ap()
    qf_pub = nc.dram_tensor("qf_pub", [D + H * DR, TPC], BF16,
                            kind="ExternalOutput").ap()
    x1 = nc.dram_tensor("x1", [TPC, D], F32, kind="ExternalOutput").ap()
    wo_bf = nc.dram_tensor("wo_bf", [D, D], BF16).ap()
    wg_bf = nc.dram_tensor("wg_bf", [HID, D], BF16).ap()
    wu_bf = nc.dram_tensor("wu_bf", [HID, D], BF16).ap()
    wd_bf = nc.dram_tensor("wd_bf", [D, HID], BF16).ap()
    u3_bf = nc.dram_tensor("u3_bf", [3 * DC, D], BF16).ap()
    delta_a = nc.dram_tensor("delta_a", [TPC, D], BF16).ap()
    delta_f = nc.dram_tensor("delta_f", [TPC, D], BF16).ap()

    S = Sched(nc)
    bufs = Bufs(nc, S)
    arena = Arena(nc, S, "arena", 54272, BF16)
    arf = Arena(nc, S, "arf", 3584, F32)
    b_x, b_x1, b_x2 = S.buf("x"), S.buf("x1"), S.buf("x2")
    b_wo, b_wg, b_wu, b_wd = (S.buf("wo_bf"), S.buf("wg_bf"),
                              S.buf("wu_bf"), S.buf("wd_bf"))
    b_u3 = S.buf("u3_bf")
    b_da, b_df = S.buf("delta_a"), S.buf("delta_f")
    b_kvp, b_qfp = S.buf("kv_pub"), S.buf("qf_pub")

    with nc.Block() as block:
        (ident, b_id), (ones1, b_o1), (onesb, b_ob) = emit_setup(S, bufs)
        pk, b_pk = [], []
        for i in range(6):
            t, b = bufs.ps(f"pk{i}", (128, 512), F32)
            pk.append(t)
            b_pk.append(b)
        bufs.ps("pbf", (128, 1024), BF16)

        emit_cast_pass(S, bufs, w_out, wo_bf, b_wo, ND, D, "wo")
        u3_evs = []
        for wi, wsrc in enumerate((w_uk, w_uv, w_uq)):
            for cc in range(4):
                (tt, b_tt) = bufs.tiles["st_f0"]
                (tb, b_tb) = bufs.tiles["cpowo0"]
                S.dma(tt, wsrc[cc * 128:(cc + 1) * 128, :], writes=[b_tt])
                S.op("dve", lambda q, tt=tt, tb=tb: q.tensor_copy(tb, tt),
                     reads=[b_tt], writes=[b_tb])
                u3_evs.append(S.dma(
                    u3_bf[wi * DC + cc * 128:wi * DC + (cc + 1) * 128, :],
                    tb, eng="pool", reads=[b_tb], region_of=[b_u3]))
        S.seal(b_u3, u3_evs)
        sgu, b_sgu = emit_col_load(S, bufs, "sgu", s_g, NH)
        suu, b_suu = emit_col_load(S, bufs, "suu", s_u, NH)
        sdd, b_sdd = emit_col_load(S, bufs, "sdd", s_d, ND)
        emit_dequant(S, bufs, wq_g, sgu, b_sgu, 0, wg_bf, b_wg, NH, D, "g")
        emit_dequant(S, bufs, wq_u, suu, b_suu, 0, wu_bf, b_wu, NH, D, "u")
        emit_dequant(S, bufs, wq_d, sdd, b_sdd, 0, wd_bf, b_wd, ND, HID, "d")

        sl_row, b_slr = bufs.sb("sl_row", (1, H), F32)
        S.dma(sl_row, slope, writes=[b_slr])
        sb_s, b_ss = emit_bcast_row(S, bufs, pk[0], b_pk[0], ones1, b_o1,
                                    sl_row, b_slr, "sb_s", H)
        di_row, b_dir = bufs.sb("di_row", (1, 3), F32)
        S.dma(di_row, consts["dist"], writes=[b_dir])
        d_b, b_db = emit_bcast_row(S, bufs, pk[1], b_pk[1], ones1, b_o1,
                                   di_row, b_dir, "d_b", 3)
        wpb, b_wpb = bufs.sb("wpb", (128, 3 * H), F32)
        for p in range(3):
            S.op("dve", lambda q, p=p: q.tensor_scalar_mul(
                wpb[:, p * H:(p + 1) * H], sb_s, d_b[:, p:p + 1]),
                reads=[b_ss, b_db], region_of=[b_wpb])
        S.seal(b_wpb, [("dve", S.cnt["dve"])])
        S.op("act", lambda q: q.activation(wpb, wpb, AF.Exp, scale=-1.0),
             reads=[b_wpb], writes=[b_wpb])
        nw, b_nw = emit_col_load(S, bufs, "nw", norm_w, ND)

        slab, b_slab = arena.view("slab", 0, ND * TPC)
        kvp = [bufs.sb(f"kvp{i}", (128, 3 * DH), F32) for i in range(2)]
        kvs, b_kvs = bufs.sb("kvs", (128, DH), F32)
        kvsb, b_kvsb = bufs.sb("kvsb", (128, DH), BF16)
        qdt = [bufs.sb(f"st_b{i}", (128, TPC), BF16) for i in range(2)]
        att = [bufs.sb(f"st_b{2 + i}", (128, TPC), BF16) for i in range(2)]
        sqt, b_sqt = bufs.sb("sqt", (128, TPC), BF16)
        eps_r, b_epsr = bufs.const(RMS_EPS, parts=1)
        for h in range(H):
            bb = h % 2
            (tkv, b_tkv), (tqd, b_tqd), (tat, b_tat) = kvp[bb], qdt[bb], att[bb]
            kvl_evs = []
            for p in range(3):
                kvl_evs.append(S.dma(
                    tkv[:, p * DH:(p + 1) * DH],
                    kvpre[p * D + h * 128:p * D + (h + 1) * 128, :],
                    region_of=[b_tkv]))
            S.seal(b_tkv, kvl_evs)
            S.op("dve", lambda q: q.memset(kvs, 0.0), writes=[b_kvs])
            for p in range(3):
                S.op("dve", lambda q, p=p, h=h, tkv=tkv: q.scalar_tensor_tensor(
                    kvs, tkv[:, p * DH:(p + 1) * DH],
                    wpb[:, p * H + h:p * H + h + 1], kvs, ALU.mult, ALU.add),
                    reads=[b_tkv, b_wpb, b_kvs], writes=[b_kvs])
            S.op("dve", lambda q: q.tensor_copy(kvsb, kvs),
                 reads=[b_kvs], writes=[b_kvsb])
            S.dma(tqd, qdec_g[h * 128:(h + 1) * 128, :], writes=[b_tqd])
            S.dma(tat, attn_l[h * 128:(h + 1) * 128, :], writes=[b_tat])
            for nt in range(4):
                tp, b_tp = pk[nt % 2], b_pk[nt % 2]
                S.op("pe", lambda q, tqd=tqd, nt=nt, tp=tp: q.matmul(
                    tp, kvsb, tqd[:, nt * 512:(nt + 1) * 512],
                    start=True, stop=True),
                    reads=[b_kvsb, b_tqd], writes=[b_tp])
                scols = slab[:, h * TPC + nt * 512:h * TPC + (nt + 1) * 512]
                S.op("dve", lambda q, tp=tp, scols=scols, tat=tat, nt=nt:
                     q.tensor_add(scols, tat[:, nt * 512:(nt + 1) * 512], tp),
                     reads=[b_tp, b_tat], region_of=[b_slab])
            S.seal(b_slab, [("dve", S.cnt["dve"])] if h == H - 1 else
                   list(_evs(b_slab.w)) + [("dve", S.cnt["dve"])])
            S.op("act", lambda q, h=h: q.activation(
                sqt, slab[:, h * TPC:(h + 1) * TPC], AF.Square),
                reads=[b_slab], writes=[b_sqt])
            for mt in range(4):
                tp, b_tp = pk[2 + mt], b_pk[2 + mt]
                S.op("pe", lambda q, mt=mt, tp=tp, h=h: q.matmul(
                    tp[0:1, :], onesb, sqt[:, mt * 512:(mt + 1) * 512],
                    start=(h == 0), stop=(h == H - 1)),
                    reads=[b_ob, b_sqt],
                    writes=[b_tp] if h == H - 1 else [],
                    region_of=[] if h == H - 1 else [b_tp])
        r_b, b_rb = arf.view("r_b", 0, TPC)
        rr, b_rr = r_b[0:1, :], S.buf("rr")
        for mt in range(4):
            tp, b_tp = pk[2 + mt], b_pk[2 + mt]
            S.op("act", lambda q, tp=tp, mt=mt: q.activation(
                rr[:, mt * 512:(mt + 1) * 512], tp[0:1, :], AF.Ln,
                scale=1.0 / D, bias=eps_r), reads=[b_tp, b_epsr],
                region_of=[b_rr])
        S.seal(b_rr, [("act", S.cnt["act"])])
        S.op("act", lambda q: q.activation(rr, rr, AF.Exp, scale=-0.5),
             reads=[b_rr], writes=[b_rr])
        rb_evs = []
        for mt in range(4):
            tp, b_tp = pk[mt % 2], b_pk[mt % 2]
            S.op("pe", lambda q, tp=tp, mt=mt: q.matmul(
                tp, ones1, rr[:, mt * 512:(mt + 1) * 512],
                start=True, stop=True), reads=[b_o1, b_rr], writes=[b_tp])
            rb_evs.append(S.op("dve", lambda q, tp=tp, mt=mt: q.tensor_copy(
                r_b[:, mt * 512:(mt + 1) * 512], tp), reads=[b_tp],
                region_of=[b_rb]))
        S.seal(b_rb, rb_evs)
        gt = [bufs.sb(f"gt{i}", (128, TPC), BF16) for i in range(2)]
        t1, b_t1 = bufs.sb("t1y", (128, TPC), BF16)
        for h in range(H):
            bb = h % 2
            (tg, b_tg) = gt[bb]
            S.dma(tg, gate_o[h * 128:(h + 1) * 128, :], writes=[b_tg])
            S.op("dve", lambda q, h=h: q.scalar_tensor_tensor(
                t1, slab[:, h * TPC:(h + 1) * TPC], nw[:, h:h + 1], r_b,
                ALU.mult, ALU.mult), reads=[b_slab, b_nw, b_rb],
                writes=[b_t1])
            S.op("dve", lambda q, h=h, tg=tg: q.tensor_mul(
                slab[:, h * TPC:(h + 1) * TPC], t1, tg),
                reads=[b_t1, b_tg], region_of=[b_slab])
        S.seal(b_slab, [("dve", S.cnt["dve"])])

        woT = [arena.view(f"woT{i}", ND * TPC + i * ND * 512,
                          ND * TPC + (i + 1) * ND * 512) for i in range(2)]
        dev = [bufs.sb(f"fdev{i}", (128, 512), BF16) for i in range(3)]
        da_evs = []
        for dmb in range(4):
            tw, b_tw = woT[dmb % 2]
            wevs = [S.dmaT(tw[:, jc * 512:(jc + 1) * 512],
                           wo_bf[dmb * 512:(dmb + 1) * 512,
                                 jc * 128:(jc + 1) * 128],
                           reads=[b_wo], region_of=[b_tw])
                    for jc in range(ND)]
            S.seal(b_tw, wevs)
            for tc in range(TPC // 128):
                tp, b_tp = pk[4 + tc % 2], b_pk[4 + tc % 2]

                def mmw(q, tc=tc, tp=tp, tw=tw):
                    r = None
                    for jc in range(ND):
                        r = q.matmul(
                            tp, slab[:, jc * TPC + tc * 128:
                                     jc * TPC + tc * 128 + 128],
                            tw[:, jc * 512:(jc + 1) * 512],
                            start=(jc == 0), stop=(jc == ND - 1))
                    return r
                S.op("pe", mmw, reads=[b_slab, b_tw], writes=[b_tp])
                te, b_te = dev[tc % 3]
                S.op("act", lambda q, tp=tp, te=te: q.activation(
                    te, tp, AF.Copy), reads=[b_tp], writes=[b_te])
                da_evs.append(S.dma(
                    delta_a[tc * 128:(tc + 1) * 128,
                            dmb * 512:(dmb + 1) * 512],
                    te, eng="pool", reads=[b_te], region_of=[b_da]))
        S.seal(b_da, da_evs)
        emit_resid_tm(S, bufs, delta_a, b_da, x, b_x, x1, b_x1, "a")

        emit_ffn(S, bufs, arena, pk, b_pk, x1, b_x1, x2, b_x2,
                 wg_bf, b_wg, wu_bf, b_wu, wd_bf, b_wd, delta_f, b_df,
                 ident, b_id, "1")

        C_LN, C_DKV, C_DQ = 0, 8192, 16384
        C_UP = 24576
        C_KR, C_QR = 40960, 41984
        C_CKV, C_CQ = 46080, 48128
        C_QRS = 50176
        dkv_sb, b_dkvs = arena.view("dkv_sb", C_DKV, C_DKV + 8192)
        dq_sb, b_dqs = arena.view("dq_sb", C_DQ, C_DQ + 8192)
        kr_sb_w, b_krs = arena.view("kr_sb_w", C_KR, C_KR + 1024)
        qr_sb_w, b_qrs = arena.view("qr_sb_w", C_QR, C_QR + 4096)
        for dc in range(ND):
            (tt, b_tt) = bufs.tiles["st_f0"]
            (tb, b_tb) = bufs.tiles["cpowo0"]
            S.dma(tt[:, 0:DC], w_dkv[dc * 128:(dc + 1) * 128, :],
                  writes=[b_tt])
            S.op("dve", lambda q, tt=tt, tb=tb: q.tensor_copy(
                tb[:, 0:DC], tt[:, 0:DC]), reads=[b_tt], writes=[b_tb])
            S.op("dve", lambda q, tb=tb, dc=dc: q.tensor_copy(
                dkv_sb[:, dc * 512:(dc + 1) * 512], tb[:, 0:DC]),
                reads=[b_tb], region_of=[b_dkvs])
            S.dma(tt[:, 0:DC], w_dq[dc * 128:(dc + 1) * 128, :],
                  writes=[b_tt])
            S.op("dve", lambda q, tt=tt, tb=tb: q.tensor_copy(
                tb[:, 0:DC], tt[:, 0:DC]), reads=[b_tt], writes=[b_tb])
            S.op("dve", lambda q, tb=tb, dc=dc: q.tensor_copy(
                dq_sb[:, dc * 512:(dc + 1) * 512], tb[:, 0:DC]),
                reads=[b_tb], region_of=[b_dqs])
            S.dma(tt[:, 0:DR], w_kr[dc * 128:(dc + 1) * 128, :],
                  writes=[b_tt])
            S.op("dve", lambda q, tt=tt, dc=dc: q.tensor_copy(
                kr_sb_w[:, dc * 64:(dc + 1) * 64], tt[:, 0:DR]),
                reads=[b_tt], region_of=[b_krs])
        S.seal(b_dkvs, [("dve", S.cnt["dve"])])
        S.seal(b_dqs, [("dve", S.cnt["dve"])])
        S.seal(b_krs, [("dve", S.cnt["dve"])])
        for cc in range(4):
            (tt, b_tt) = bufs.tiles["st_f0"]
            S.dma(tt[:, 0:H * DR], w_qr[cc * 128:(cc + 1) * 128, :],
                  writes=[b_tt])
            dstv = qr_sb_w[:, cc * 1024:(cc + 1) * 1024].rearrange(
                "p (two h f) -> p two h f", two=2, h=H, f=DR // 2)
            srcv = tt[:, 0:H * DR].rearrange(
                "p (h two f) -> p two h f", h=H, two=2, f=DR // 2)
            S.op("dve", lambda q, dstv=dstv, srcv=srcv: q.tensor_copy(
                dstv, srcv), reads=[b_tt], region_of=[b_qrs])
        S.seal(b_qrs, [("dve", S.cnt["dve"])])
        pos_row, b_pos = bufs.sb("pos_row", (1, TPC), F32)
        S.dma(pos_row, consts["positions"], writes=[b_pos])
        invf, b_invf = bufs.sb("invf", (128, 1), F32)
        S.dma(invf, consts["invfreq4"], writes=[b_invf])
        c_n2pi, b_n2pi = bufs.const(-2.0 * np.pi)
        c_hpi, b_hpi = bufs.const(0.5 * np.pi)
        TGC = 512
        kv_evs, qf_evs = [], []
        upw = [arena.view(f"upw{i}", C_UP + i * 8192, C_UP + (i + 1) * 8192)
               for i in range(2)]
        for g in range(TPC // TGC):
            lnc, b_lnc = arena.view(f"lnc{g}", C_LN, C_LN + ND * TGC)
            emit_ln(S, bufs, x2, b_x2, lnc, b_lnc, g * TGC, TGC, f"c{g}",
                    ident, b_id)
            ckv, b_ckv = arena.view(f"ckv{g}", C_CKV, C_CKV + 2048)
            cq, b_cq = arena.view(f"cq{g}", C_CQ, C_CQ + 2048)
            ev3 = [bufs.sb(f"evc{i}", (128, 512), BF16) for i in range(3)]
            for (dst, b_dst, wsb, b_wsb) in ((ckv, b_ckv, dkv_sb, b_dkvs),
                                             (cq, b_cq, dq_sb, b_dqs)):
                cevs = []
                for cc2 in range(4):
                    tp, b_tp = pk[cc2 % 2], b_pk[cc2 % 2]

                    def mmc(q, cc2=cc2, tp=tp, wsb=wsb, lnc=lnc):
                        r = None
                        for dc in range(ND):
                            r = q.matmul(
                                tp, wsb[:, dc * 512 + cc2 * 128:
                                        dc * 512 + cc2 * 128 + 128],
                                lnc[:, dc * TGC:(dc + 1) * TGC],
                                start=(dc == 0), stop=(dc == ND - 1))
                        return r
                    S.op("pe", mmc, reads=[b_wsb, b_lnc], writes=[b_tp])
                    cevs.append(S.op(
                        "act", lambda q, tp=tp, dst=dst, cc2=cc2:
                        q.activation(dst[:, cc2 * 512:(cc2 + 1) * 512], tp,
                                     AF.Copy), reads=[b_tp],
                        region_of=[b_dst]))
                S.seal(b_dst, cevs)
            for wi, which in ((0, "K"), (1, "V"), (2, "Q")):
                tw, b_tw = upw[wi % 2]
                wevs = [S.dma(tw[:, cc2 * 2048:(cc2 + 1) * 2048],
                              u3_bf[wi * DC + cc2 * 128:
                                    wi * DC + (cc2 + 1) * 128, :],
                              reads=[b_u3], region_of=[b_tw])
                        for cc2 in range(4)]
                S.seal(b_tw, wevs)
                if which in ("K", "Q"):
                    src, b_src = (ckv, b_ckv) if which == "K" else (cq, b_cq)
                    for ec in range(ND):
                        tp, b_tp = pk[2 + ec % 2], b_pk[2 + ec % 2]

                        def mmk(q, ec=ec, tp=tp, tw=tw, src=src):
                            r = None
                            for cc2 in range(4):
                                r = q.matmul(
                                    tp, tw[:, cc2 * 2048 + ec * 128:
                                           cc2 * 2048 + ec * 128 + 128],
                                    src[:, cc2 * 512:(cc2 + 1) * 512],
                                    start=(cc2 == 0), stop=(cc2 == 3))
                            return r
                        S.op("pe", mmk, reads=[b_tw, b_src], writes=[b_tp])
                        te, b_te = ev3[ec % 3]
                        S.op("act", lambda q, tp=tp, te=te: q.activation(
                            te, tp, AF.Copy), reads=[b_tp], writes=[b_te])
                        dst = kv_pub if which == "K" else qf_pub
                        evl = kv_evs if which == "K" else qf_evs
                        evl.append(S.dma(
                            dst[ec * 128:(ec + 1) * 128,
                                g * TGC:(g + 1) * TGC], te, eng="pool",
                            reads=[b_te],
                            region_of=[b_kvp if which == "K" else b_qfp]))
                else:
                    for tc in range(4):
                        for eb in range(4):
                            tp, b_tp = pk[2 + eb % 2], b_pk[2 + eb % 2]

                            def mmv(q, tc=tc, eb=eb, tp=tp, tw=tw, ckv=ckv):
                                r = None
                                for cc2 in range(4):
                                    r = q.matmul(
                                        tp, ckv[:, cc2 * 512 + tc * 128:
                                                cc2 * 512 + tc * 128 + 128],
                                        tw[:, cc2 * 2048 + eb * 512:
                                           cc2 * 2048 + (eb + 1) * 512],
                                        start=(cc2 == 0), stop=(cc2 == 3))
                                return r
                            S.op("pe", mmv, reads=[b_tw, b_ckv],
                                 writes=[b_tp])
                            te, b_te = ev3[eb % 3]
                            S.op("act", lambda q, tp=tp, te=te: q.activation(
                                te, tp, AF.Copy), reads=[b_tp], writes=[b_te])
                            kv_evs.append(S.dma(
                                kv_pub[D + g * TGC + tc * 128:
                                       D + g * TGC + (tc + 1) * 128,
                                       eb * 512:(eb + 1) * 512], te,
                                eng="pool", reads=[b_te], region_of=[b_kvp]))
            qrp, b_qrp = arena.view(f"qrp{g}", C_QRS, C_QRS + 8 * TGC)
            qrevs = []
            for rc in range(8):
                tp, b_tp = pk[4 + rc % 2], b_pk[4 + rc % 2]

                def mmr(q, rc=rc, tp=tp, cq=cq):
                    r = None
                    for cc2 in range(4):
                        r = q.matmul(
                            tp, qr_sb_w[:, cc2 * 1024 + rc * 128:
                                        cc2 * 1024 + rc * 128 + 128],
                            cq[:, cc2 * 512:(cc2 + 1) * 512],
                            start=(cc2 == 0), stop=(cc2 == 3))
                    return r
                S.op("pe", mmr, reads=[b_qrs, b_cq], writes=[b_tp])
                qrevs.append(S.op("act", lambda q, tp=tp, rc=rc: q.activation(
                    qrp[:, rc * TGC:(rc + 1) * TGC], tp, AF.Copy),
                    reads=[b_tp], region_of=[b_qrp]))
            S.seal(b_qrp, qrevs)
            krp, b_krp = bufs.sb("krp", (64, TGC), BF16)

            def mmkr(q, lnc=lnc):
                r = None
                for dc in range(ND):
                    r = q.matmul(pk[0][0:64, :],
                                 kr_sb_w[:, dc * 64:(dc + 1) * 64],
                                 lnc[:, dc * TGC:(dc + 1) * TGC],
                                 start=(dc == 0), stop=(dc == ND - 1))
                return r
            S.op("pe", mmkr, reads=[b_krs, b_lnc], writes=[b_pk[0]])
            S.op("act", lambda q: q.activation(krp, pk[0][0:64, :], AF.Copy),
                 reads=[b_pk[0]], writes=[b_krp])
            pb, b_pb = arf.view(f"pb{g}", 0, TGC)
            S.op("pe", lambda q, g=g: q.matmul(
                pk[1], ones1, pos_row[:, g * TGC:(g + 1) * TGC],
                start=True, stop=True), reads=[b_o1, b_pos],
                writes=[b_pk[1]])
            ang, b_ang = arf.view(f"ang{g}", TGC, 2 * TGC)
            S.op("dve", lambda q: q.tensor_copy(pb, pk[1]), reads=[b_pk[1]],
                 writes=[b_pb])
            S.op("dve", lambda q: q.tensor_scalar_mul(ang, pb, invf),
                 reads=[b_pb, b_invf], writes=[b_ang])
            ta, b_ta = arf.view(f"rota{g}", 5 * TGC, 6 * TGC)
            tb2, b_tb2 = arf.view(f"rotb{g}", 6 * TGC, 7 * TGC)
            sc_t = {}
            tki, b_tki = bufs.sb("tkir", (128, TGC), I32)
            for trig in ("sin", "cos"):
                tt1, b_tt1 = arf.view(f"tt1{g}{trig}", 2 * TGC, 3 * TGC)
                off_t = 3 * TGC if trig == "sin" else 4 * TGC
                tst, b_tst = arf.view(f"tst{g}{trig}", off_t, off_t + TGC)
                off = 0.0 if trig == "sin" else 0.5 * np.pi
                S.op("dve", lambda q, tt1=tt1, off=off: q.tensor_scalar(
                    tt1, ang, off, 1.0 / (2.0 * np.pi), ALU.add, ALU.mult),
                    reads=[b_ang], writes=[b_tt1])
                S.op("dve", lambda q, tt1=tt1: q.tensor_copy(tki, tt1),
                     reads=[b_tt1], writes=[b_tki])
                S.op("dve", lambda q, tt1=tt1: q.tensor_copy(tt1, tki),
                     reads=[b_tki], writes=[b_tt1])
                S.op("dve", lambda q, tt1=tt1, tst=tst: q.scalar_tensor_tensor(
                    tst, tt1, c_n2pi, ang, ALU.mult, ALU.add),
                    reads=[b_tt1, b_n2pi, b_ang], writes=[b_tst])
                S.op("dve", lambda q, tst=tst, off=off: q.tensor_single_scalar(
                    tb2, tst, float(np.pi) - off, ALU.is_gt),
                    reads=[b_tst], writes=[b_tb2])
                S.op("dve", lambda q, tst=tst: q.scalar_tensor_tensor(
                    tst, tb2, c_n2pi, tst, ALU.mult, ALU.add),
                    reads=[b_tb2, b_n2pi, b_tst], writes=[b_tst])
                if trig == "sin":
                    S.op("act", lambda q, tst=tst: q.activation(
                        tst, tst, AF.Sin), reads=[b_tst], writes=[b_tst])
                else:
                    S.op("act", lambda q, tst=tst: q.activation(
                        tst, tst, AF.Sin, bias=c_hpi),
                        reads=[b_tst, b_hpi], writes=[b_tst])
                sc_t[trig] = (tst, b_tst)
            (tsin, b_tsin), (tcos, b_tcos) = sc_t["sin"], sc_t["cos"]
            rot, b_rot = arena.view(f"rot{g}", C_CKV, C_CKV + 8 * TGC)
            rot_evs = []
            for c in range(4):
                x1c = qrp[:, c * TGC:(c + 1) * TGC]
                x2c = qrp[:, (4 + c) * TGC:(5 + c) * TGC]
                S.op("dve", lambda q, x1c=x1c: q.tensor_mul(ta, x1c, tcos),
                     reads=[b_qrp, b_tcos], writes=[b_ta])
                S.op("dve", lambda q, x2c=x2c: q.tensor_mul(tb2, x2c, tsin),
                     reads=[b_qrp, b_tsin], writes=[b_tb2])
                rot_evs.append(S.op(
                    "dve", lambda q, c=c: q.tensor_sub(
                        rot[:, c * TGC:(c + 1) * TGC], ta, tb2),
                    reads=[b_ta, b_tb2], region_of=[b_rot]))
                S.op("dve", lambda q, x1c=x1c: q.tensor_mul(ta, x1c, tsin),
                     reads=[b_qrp, b_tsin], writes=[b_ta])
                S.op("dve", lambda q, x2c=x2c: q.tensor_mul(tb2, x2c, tcos),
                     reads=[b_qrp, b_tcos], writes=[b_tb2])
                rot_evs.append(S.op(
                    "dve", lambda q, c=c: q.tensor_add(
                        rot[:, (4 + c) * TGC:(5 + c) * TGC], ta, tb2),
                    reads=[b_ta, b_tb2], region_of=[b_rot]))
            S.seal(b_rot, rot_evs)
            for h in range(H):
                c = h // 4
                r0 = (h % 4) * 32
                qf_evs.append(S.dma(
                    qf_pub[D + h * DR:D + h * DR + 32,
                           g * TGC:(g + 1) * TGC],
                    rot[r0:r0 + 32, c * TGC:(c + 1) * TGC], eng="pool",
                    reads=[b_rot], region_of=[b_qfp]))
                qf_evs.append(S.dma(
                    qf_pub[D + h * DR + 32:D + (h + 1) * DR,
                           g * TGC:(g + 1) * TGC],
                    rot[r0:r0 + 32, (4 + c) * TGC:(5 + c) * TGC], eng="pool",
                    reads=[b_rot], region_of=[b_qfp]))
            if "kr2" not in bufs.tiles:
                bufs.tiles["kr2"] = (sqt[0:32, 1024:1024 + TGC],
                                     S.alias_buf(b_sqt, "kr2"))
            kr2, b_kr2 = bufs.tiles["kr2"]
            S.op("dve", lambda q: q.tensor_copy(kr2, krp[32:64, :]),
                 reads=[b_krp], writes=[b_kr2])
            for part in range(2):
                ft, st = (tcos, tsin) if part == 0 else (tsin, tcos)
                S.op("dve", lambda q, ft=ft: q.tensor_mul(
                    ta[0:32, :], krp[0:32, :], ft[0:32, :]),
                    reads=[b_krp, b_tcos, b_tsin], writes=[b_ta])
                S.op("dve", lambda q, st=st: q.tensor_mul(
                    tb2[0:32, :], kr2, st[0:32, :]),
                    reads=[b_kr2, b_tcos, b_tsin], writes=[b_tb2])
                alu = ALU.subtract if part == 0 else ALU.add
                if f"krot{part}" not in bufs.tiles:
                    bufs.tiles[f"krot{part}"] = (
                        sqt[0:32, part * TGC:(part + 1) * TGC],
                        S.alias_buf(b_sqt, f"krot{part}"))
                tcb, b_tcb = bufs.tiles[f"krot{part}"]
                S.op("dve", lambda q, tcb=tcb, alu=alu: q.tensor_tensor(
                    tcb, ta[0:32, :], tb2[0:32, :], alu),
                    reads=[b_ta, b_tb2], writes=[b_tcb])
                kv_evs.append(S.dma(
                    kv_pub[2 * D + part * 32:2 * D + (part + 1) * 32,
                           g * TGC:(g + 1) * TGC], tcb, eng="pool",
                    reads=[b_tcb], region_of=[b_kvp]))
        S.seal(b_kvp, kv_evs)
        S.seal(b_qfp, qf_evs)
        S.op("pool", lambda q: q.nop(),
             deps=[b_kvp.w, b_qfp.w, b_x2.w])
        S.emit(block)
    return nc


def build_p3():
    nc = bass.Bass("TRN2", target_bir_lowering=False)
    x2 = nc.dram_tensor("x2", [TPC, D], F32, kind="ExternalInput").ap()
    kvg = nc.dram_tensor("kvg", [4 * (2 * D + DR), TPC], BF16,
                         kind="ExternalInput").ap()
    qf = nc.dram_tensor("qf", [D + H * DR, TPC], BF16,
                        kind="ExternalInput").ap()
    w_o = nc.dram_tensor("w_o", [D, D], F32, kind="ExternalInput").ap()
    b_o = nc.dram_tensor("b_o", [1, D], F32, kind="ExternalInput").ap()
    wq_g = nc.dram_tensor("wq_gate", [HID, D], I32, kind="ExternalInput").ap()
    s_g = nc.dram_tensor("s_gate", [HID], F32, kind="ExternalInput").ap()
    wq_u = nc.dram_tensor("wq_up", [HID, D], I32, kind="ExternalInput").ap()
    s_u = nc.dram_tensor("s_up", [HID], F32, kind="ExternalInput").ap()
    wq_d = nc.dram_tensor("wq_down", [D, HID], I32, kind="ExternalInput").ap()
    s_d = nc.dram_tensor("s_down", [D], F32, kind="ExternalInput").ap()
    xout = nc.dram_tensor("xout", [TPC, D], F32, kind="ExternalOutput").ap()
    x3 = nc.dram_tensor("x3", [TPC, D], F32).ap()
    wg_bf = nc.dram_tensor("wg_bf", [HID, D], BF16).ap()
    wu_bf = nc.dram_tensor("wu_bf", [HID, D], BF16).ap()
    wd_bf = nc.dram_tensor("wd_bf", [D, HID], BF16).ap()
    delta_f = nc.dram_tensor("delta_f", [TPC, D], BF16).ap()
    SLAB = 2 * D + DR

    S = Sched(nc)
    bufs = Bufs(nc, S)
    arena = Arena(nc, S, "arena", 65536, BF16)
    b_x2, b_x3, b_xout = S.buf("x2"), S.buf("x3"), S.buf("xout")
    b_wg, b_wu, b_wd = S.buf("wg_bf"), S.buf("wu_bf"), S.buf("wd_bf")
    b_df = S.buf("delta_f")

    with nc.Block() as block:
        (ident, b_id), (ones1, b_o1), (onesb, b_ob) = emit_setup(S, bufs)
        pk, b_pk = [], []
        for i in range(6):
            t, b = bufs.ps(f"pk{i}", (128, 512), F32)
            pk.append(t)
            b_pk.append(b)
        bufs.ps("pbf", (128, 1024), BF16)

        sgu, b_sgu = emit_col_load(S, bufs, "sgu", s_g, NH)
        suu, b_suu = emit_col_load(S, bufs, "suu", s_u, NH)
        sdd, b_sdd = emit_col_load(S, bufs, "sdd", s_d, ND)
        emit_dequant(S, bufs, wq_g, sgu, b_sgu, 0, wg_bf, b_wg, NH, D, "g")
        emit_dequant(S, bufs, wq_u, suu, b_suu, 0, wu_bf, b_wu, NH, D, "u")
        emit_dequant(S, bufs, wq_d, sdd, b_sdd, 0, wd_bf, b_wd, ND, HID, "d")

        C_CTX = 0
        C_KV = 32768
        ctx, b_ctx = arena.view("ctx", C_CTX, C_CTX + ND * TPC)
        krT, b_kr = arena.view("krT", C_KV, C_KV + 8192)
        ksb, b_ks = arena.view("ksb", C_KV + 8192, C_KV + 16384)
        vsb, b_vs = arena.view("vsb", C_KV + 16384, C_KV + 24576)
        qsb, b_qs = arena.view("qsb", C_KV + 24576, C_KV + 26624)
        qrs, b_qr = arena.view("qrs", C_KV + 26624, C_KV + 28672)
        psb = [bufs.sb(f"psb{i}", (128, 512), BF16) for i in range(2)]
        lrow, b_lr = bufs.sb("lrow", (1, 512), F32)
        lbs, b_lb = bufs.sb("lbs", (128, 512), F32)
        kr_evs = [S.dma(krT[0:64, s * TPC:(s + 1) * TPC],
                        kvg[s * SLAB + 2 * D:s * SLAB + 2 * D + DR, :],
                        region_of=[b_kr]) for s in range(4)]
        S.seal(b_kr, kr_evs)
        ctx_evs = []
        for h in range(H):
            kevs = [S.dma(ksb[:, s * TPC:(s + 1) * TPC],
                          kvg[s * SLAB + h * 128:s * SLAB + (h + 1) * 128, :],
                          region_of=[b_ks]) for s in range(4)]
            S.seal(b_ks, kevs)
            vevs = []
            for s in range(4):
                for kc in range(ND):
                    vevs.append(S.dma(
                        vsb[:, s * 2048 + kc * 128:s * 2048 + (kc + 1) * 128],
                        kvg[s * SLAB + D + kc * 128:
                            s * SLAB + D + (kc + 1) * 128,
                            h * 128:(h + 1) * 128],
                        region_of=[b_vs]))
            S.seal(b_vs, vevs)
            e_q = S.dma(qsb, qf[h * 128:(h + 1) * 128, :], writes=[b_qs])
            e_qr = S.dma(qrs[0:64, :], qf[D + h * DR:D + (h + 1) * DR, :],
                         writes=[b_qr])
            for mt in range(4):
                b_l = b_pk[4]
                for kc in range(64):
                    tp, b_tp = pk[kc % 2], b_pk[kc % 2]

                    def mmS(q, kc=kc, mt=mt, tp=tp):
                        q.matmul(tp, ksb[:, kc * 128:(kc + 1) * 128],
                                 qsb[:, mt * 512:(mt + 1) * 512],
                                 start=True, stop=False)
                        return q.matmul(
                            tp, krT[0:64, kc * 128:(kc + 1) * 128],
                            qrs[0:64, mt * 512:(mt + 1) * 512],
                            start=False, stop=True)
                    S.op("pe", mmS, reads=[b_ks, b_kr, b_qs, b_qr],
                         writes=[b_tp])
                    tpb, b_tpb = psb[kc % 2]
                    S.op("act", lambda q, tp=tp, tpb=tpb: q.activation(
                        tpb, tp, AF.Exp, scale=SOFT_SC),
                        reads=[b_tp], writes=[b_tpb])
                    S.op("pe", lambda q, tpb=tpb, kc=kc: q.matmul(
                        pk[4][0:1, :], onesb, tpb,
                        start=(kc == 0), stop=(kc == 63)),
                        reads=[b_ob, b_tpb],
                        writes=[b_l] if kc == 63 else [],
                        region_of=[b_l] if kc < 63 else [])
                    b_cx = b_pk[2 + mt % 2]
                    S.op("pe", lambda q, tpb=tpb, kc=kc, mt=mt: q.matmul(
                        pk[2 + mt % 2], vsb[:, kc * 128:(kc + 1) * 128], tpb,
                        start=(kc == 0), stop=(kc == 63)),
                        reads=[b_vs, b_tpb],
                        writes=[b_cx] if kc == 63 else [],
                        region_of=[b_cx] if kc < 63 else [])
                S.op("dve", lambda q: q.tensor_copy(lrow, pk[4][0:1, :]),
                     reads=[b_pk[4]], writes=[b_lr])
                S.op("dve", lambda q: q.reciprocal(lrow, lrow),
                     reads=[b_lr], writes=[b_lr])
                S.op("pe", lambda q: q.matmul(pk[5], ones1, lrow,
                                              start=True, stop=True),
                     reads=[b_o1, b_lr], writes=[b_pk[5]])
                S.op("dve", lambda q: q.tensor_copy(lbs, pk[5]),
                     reads=[b_pk[5]], writes=[b_lb])
                ccols = ctx[:, h * TPC + mt * 512:h * TPC + (mt + 1) * 512]
                ctx_evs.append(S.op(
                    "dve", lambda q, ccols=ccols, mt=mt: q.tensor_mul(
                        ccols, pk[2 + mt % 2], lbs),
                    reads=[b_pk[2 + mt % 2], b_lb], region_of=[b_ctx]))
        S.seal(b_ctx, ctx_evs)

        wosb, b_wo = arena.view("wosb", C_KV, C_KV + ND * TPC)
        bufs.sb("st_f0", (128, D), F32)
        bufs.sb("st_f1", (128, D), F32)
        wevs = []
        for ec in range(ND):
            (tt, b_tt) = bufs.tiles["st_f0"]
            S.dma(tt, w_o[ec * 128:(ec + 1) * 128, :], writes=[b_tt])
            wevs.append(S.op("dve", lambda q, tt=tt, ec=ec: q.tensor_copy(
                wosb[:, ec * 2048:(ec + 1) * 2048], tt),
                reads=[b_tt], region_of=[b_wo]))
        S.seal(b_wo, wevs)
        bo_row, b_bor = bufs.sb("bo_row", (1, D), F32)
        S.dma(bo_row, b_o, writes=[b_bor])
        bob, b_bob = arf.view("bob", 0, D)
        bob_evs = []
        for mt in range(4):
            S.op("pe", lambda q, mt=mt: q.matmul(
                pk[mt % 2], ones1, bo_row[:, mt * 512:(mt + 1) * 512],
                start=True, stop=True), reads=[b_o1, b_bor],
                writes=[b_pk[mt % 2]])
            bob_evs.append(S.op(
                "dve", lambda q, mt=mt: q.tensor_copy(
                    bob[:, mt * 512:(mt + 1) * 512], pk[mt % 2]),
                reads=[b_pk[mt % 2]], region_of=[b_bob]))
        S.seal(b_bob, bob_evs)
        x3_evs = []
        xts = [bufs.tiles[f"st_f{i}"] for i in range(2)]
        for tc in range(TPC // 128):
            (tx, b_tx) = xts[tc % 2]
            S.dma(tx, x2[tc * 128:(tc + 1) * 128, :], reads=[b_x2],
                  writes=[b_tx])
            S.op("dve", lambda q, tx=tx: q.tensor_add(tx, tx, bob),
                 reads=[b_bob], writes=[b_tx])
            for dmb in range(4):
                tp, b_tp = pk[2 + dmb % 2], b_pk[2 + dmb % 2]

                def mmo(q, tc=tc, dmb=dmb, tp=tp):
                    r = None
                    for ec in range(ND):
                        r = q.matmul(
                            tp, ctx[:, ec * TPC + tc * 128:
                                    ec * TPC + tc * 128 + 128],
                            wosb[:, ec * 2048 + dmb * 512:
                                 ec * 2048 + (dmb + 1) * 512],
                            start=(ec == 0), stop=(ec == ND - 1))
                    return r
                S.op("pe", mmo, reads=[b_ctx, b_wo], writes=[b_tp])
                S.op("dve", lambda q, tx=tx, tp=tp, dmb=dmb: q.tensor_add(
                    tx[:, dmb * 512:(dmb + 1) * 512],
                    tx[:, dmb * 512:(dmb + 1) * 512], tp),
                    reads=[b_tp], writes=[b_tx])
            x3_evs.append(S.dma(x3[tc * 128:(tc + 1) * 128, :], tx,
                                eng="pool", reads=[b_tx], region_of=[b_x3]))
        S.seal(b_x3, x3_evs)

        emit_ffn(S, bufs, arena, pk, b_pk, x3, b_x3, xout, b_xout,
                 wg_bf, b_wg, wu_bf, b_wu, wd_bf, b_wd, delta_f, b_df,
                 ident, b_id, "2")
        S.op("pool", lambda q: q.nop(), deps=[b_xout.w])
        S.emit(block)
    return nc


def _p3_inmaps(inputs, r2):
    in_maps = []
    for c in range(NCORES):
        b = c // 4
        slabs = [r2[4 * b + p]["kv_pub"] for p in range(4)]
        m = {
            "x2": r2[c]["x2"],
            "kvg": np.ascontiguousarray(np.concatenate(slabs, axis=0)),
            "qf": r2[c]["qf_pub"],
            "w_o": np.asarray(inputs["L0_attn_b_W_O"]),
            "b_o": np.asarray(inputs["L0_attn_b_b_O"]).reshape(1, D),
            "wq_gate": np.asarray(inputs["L0_ffn_wq_gate"]),
            "s_gate": np.asarray(inputs["L0_ffn_s_gate"]),
            "wq_up": np.asarray(inputs["L0_ffn_wq_up"]),
            "s_up": np.asarray(inputs["L0_ffn_s_up"]),
            "wq_down": np.asarray(inputs["L0_ffn_wq_down"]),
            "s_down": np.asarray(inputs["L0_ffn_s_down"]),
        }
        in_maps.append(m)
    return in_maps


def _run_p3(inputs, r2):
    nc = build_p3()
    res = run_bass_kernel_spmd(nc, _p3_inmaps(inputs, r2),
                               core_ids=list(range(NCORES)))
    return res.results


def _split_tokens(a):
    flat = np.ascontiguousarray(np.asarray(a).reshape(B * T, *a.shape[2:]))
    return [flat[c * TPC:(c + 1) * TPC] for c in range(NCORES)]


def _p1_inmaps(inputs):
    xs = _split_tokens(inputs["L0_attn_a_hidden_states"])
    in_maps = []
    for c in range(NCORES):
        cst = host_consts(c)
        m = {"x": np.ascontiguousarray(xs[c]),
             "w_qkv": np.asarray(inputs["L0_attn_a_w_qkv"]),
             "w_gate": np.asarray(inputs["L0_attn_a_w_gate"]),
             "slope": np.asarray(inputs["L0_attn_a_slope_rate"]).reshape(1, H)}
        for n, s, d in CONST_SPECS:
            m[n] = np.ascontiguousarray(cst[n].reshape(s))
        in_maps.append(m)
    return in_maps


def _run_p1(inputs):
    nc = build_p1()
    res = run_bass_kernel_spmd(nc, _p1_inmaps(inputs),
                               core_ids=list(range(NCORES)))
    return res.results


def _p2_inmaps(inputs, r1):
    xs = _split_tokens(inputs["L0_attn_a_hidden_states"])
    kvs = [r1[c]["kvsum"] for c in range(NCORES)]
    in_maps = []
    for c in range(NCORES):
        cst = host_consts(c)
        b = c // 4
        pre = [kvs[4 * b + p] for p in range(3)]
        m = {
            "x": np.ascontiguousarray(xs[c]),
            "attn_local": r1[c]["attn_local"],
            "qdec_g": r1[c]["qdec_g"],
            "gate_o": r1[c]["gate_o"],
            "kvpre": np.ascontiguousarray(np.concatenate(pre, axis=0)),
            "slope": np.asarray(
                inputs["L0_attn_a_slope_rate"]).reshape(1, H),
            "w_out": np.asarray(inputs["L0_attn_a_w_out"]),
            "norm_w": np.asarray(inputs["L0_attn_a_norm_w"]),
            "wq_gate": np.asarray(inputs["L0_ffn_wq_gate"]),
            "s_gate": np.asarray(inputs["L0_ffn_s_gate"]),
            "wq_up": np.asarray(inputs["L0_ffn_wq_up"]),
            "s_up": np.asarray(inputs["L0_ffn_s_up"]),
            "wq_down": np.asarray(inputs["L0_ffn_wq_down"]),
            "s_down": np.asarray(inputs["L0_ffn_s_down"]),
            "w_dkv": np.asarray(inputs["L0_attn_b_W_DKV"]),
            "w_dq": np.asarray(inputs["L0_attn_b_W_DQ"]),
            "w_uk": np.asarray(inputs["L0_attn_b_W_UK"]),
            "w_uv": np.asarray(inputs["L0_attn_b_W_UV"]),
            "w_uq": np.asarray(inputs["L0_attn_b_W_UQ"]),
            "w_kr": np.asarray(inputs["L0_attn_b_W_KR"]),
            "w_qr": np.asarray(inputs["L0_attn_b_W_QR"]),
        }
        for n, s, d in CONST_SPECS:
            m[n] = np.ascontiguousarray(cst[n].reshape(s))
        in_maps.append(m)
    return in_maps


def _run_p2(inputs, r1):
    nc = build_p2()
    res = run_bass_kernel_spmd(nc, _p2_inmaps(inputs, r1),
                               core_ids=list(range(NCORES)))
    return res.results


SPAN = 4096
NBK = SPAN // BLK
QT = 2048
MAGIC1 = 0x5EED0001
MAGIC2 = 0x5EED0002


def host_consts_fused(core):
    half = core % 2
    quarter = (core // 2) % 2
    c = {}
    ar = np.arange(1, BLK + 1, dtype=np.float32)
    c["qdecay_idx"] = np.tile(ar, (128, 1))
    c["kdecay_idx"] = np.tile(BLK - ar, (128, 1))
    idx = ar[:, None] - ar[None, :]
    ddT = idx.T.copy()
    ddT[ddT < 0] = BIG
    c["ddT0"] = np.ascontiguousarray(ddT[0:128, :], np.float32)
    c["ddT1"] = np.ascontiguousarray(ddT[128:256, :], np.float32)
    c["blkj"] = (np.arange(NBK, dtype=np.float32) * BLK).reshape(1, -1)
    c["dist0"] = np.full((128, 1), 0.0 if half == 1 else BIG, np.float32)
    c["positions_k"] = (half * SPAN +
                        np.arange(SPAN)).astype(np.float32).reshape(1, -1)
    c["positions_q"] = (half * SPAN + quarter * QT +
                        np.arange(QT)).astype(np.float32).reshape(1, -1)
    invf = 1.0 / 10000.0 ** (np.arange(DR // 2, dtype=np.float32) / (DR // 2))
    c["invfreq4"] = np.tile(invf, 4).reshape(-1, 1)
    c["qsel"] = np.full((128, 1), 1.0 if quarter == 0 else 0.0, np.float32)
    return c


FUSED_CONST_SPECS = [
    ("qdecay_idx", (128, BLK), F32),
    ("kdecay_idx", (128, BLK), F32),
    ("ddT0", (128, BLK), F32),
    ("ddT1", (128, BLK), F32),
    ("blkj", (1, NBK), F32),
    ("dist0", (128, 1), F32),
    ("positions_k", (1, SPAN), F32),
    ("positions_q", (1, QT), F32),
    ("invfreq4", (128, 1), F32),
    ("qsel", (128, 1), F32),
]


def emit_exchange_write(S, nc, sems, srcs_and_dsts, flag_sh, flag_tile,
                        b_ft, magic, deps, name):
    exsems = sems

    def closure(q):
        for arm in nc.switch_core_id(engines=[q], n=128):
            role = int(arm) % 2
            n = 0
            cnts = [0, 0, 0, 0]
            for (src_fn, dst_fn) in srcs_and_dsts:
                si = n % 4
                n += 1
                cnts[si] += 16
                q.dma_start(out=dst_fn(role), in_=src_fn(role)).then_inc(
                    exsems[si], 16)
            for si in range(4):
                if cnts[si]:
                    q.wait_ge(exsems[si], cnts[si])
            q.dma_start(out=flag_sh[role:role + 1, :],
                        in_=flag_tile).then_inc(exsems[0], 16)
            q.wait_ge(exsems[0], cnts[0] + 16)
        return q.nop()

    return S.op("pool", closure, reads=[b_ft], deps=list(deps), inc=1)


def emit_poll(S, nc, psem, flag_sh, fr_tile, b_fr, magic, deps, name):

    def closure(q):
        with q.register(f"nd{name}") as nd, q.register(f"tg{name}") as tg, \
             q.register(f"v0{name}") as v0, q.register(f"v1{name}") as v1:
            q.reg_mov(nd, 1)
            q.reg_mov(tg, 0)
            with q.While(nd):
                q.reg_add(tg, tg, 16)
                q.dma_start(out=fr_tile[0:1, 0:16],
                            in_=flag_sh[0:1, :]).then_inc(psem, 16)
                q.reg_add(tg, tg, 16)
                q.dma_start(out=fr_tile[0:1, 16:32],
                            in_=flag_sh[1:2, :]).then_inc(psem, 16)
                q.wait_ge(psem, tg)
                q.reg_load(v0, fr_tile[0:1, 0:1])
                q.reg_load(v1, fr_tile[0:1, 16:17])
                q.reg_alu(v0, v0, magic, ALU.subtract)
                q.reg_alu(v1, v1, magic, ALU.subtract)
                q.reg_alu(nd, v0, v1, ALU.bitwise_or)
        return q.nop()

    return S.op("pool", closure, deps=list(deps), reads=(), writes=[b_fr],
                inc=1)


def build_fused():
    nc = bass.Bass("TRN2", target_bir_lowering=False)
    x = nc.dram_tensor("x", [SPAN, D], F32, kind="ExternalInput").ap()
    w_qkv = nc.dram_tensor("w_qkv", [3 * D, D], F32, kind="ExternalInput").ap()
    w_gate = nc.dram_tensor("w_gate", [D, D], F32, kind="ExternalInput").ap()
    slope = nc.dram_tensor("slope", [1, H], F32, kind="ExternalInput").ap()
    w_out = nc.dram_tensor("w_out", [D, D], F32, kind="ExternalInput").ap()
    norm_w = nc.dram_tensor("norm_w", [D], F32, kind="ExternalInput").ap()
    wq_g = nc.dram_tensor("wq_gate", [HID, D], I32, kind="ExternalInput").ap()
    s_g = nc.dram_tensor("s_gate", [HID], F32, kind="ExternalInput").ap()
    wq_u = nc.dram_tensor("wq_up", [HID, D], I32, kind="ExternalInput").ap()
    s_u = nc.dram_tensor("s_up", [HID], F32, kind="ExternalInput").ap()
    wq_d = nc.dram_tensor("wq_down", [D, HID], I32, kind="ExternalInput").ap()
    s_d = nc.dram_tensor("s_down", [D], F32, kind="ExternalInput").ap()
    w_dkv = nc.dram_tensor("w_dkv", [D, DC], F32, kind="ExternalInput").ap()
    w_dq = nc.dram_tensor("w_dq", [D, DC], F32, kind="ExternalInput").ap()
    w_uk = nc.dram_tensor("w_uk", [DC, D], F32, kind="ExternalInput").ap()
    w_uv = nc.dram_tensor("w_uv", [DC, D], F32, kind="ExternalInput").ap()
    w_uq = nc.dram_tensor("w_uq", [DC, D], F32, kind="ExternalInput").ap()
    w_kr = nc.dram_tensor("w_kr", [D, DR], F32, kind="ExternalInput").ap()
    w_qr = nc.dram_tensor("w_qr", [DC, H * DR], F32,
                          kind="ExternalInput").ap()
    w_o2 = nc.dram_tensor("w_o2", [D, D], F32, kind="ExternalInput").ap()
    b_o2 = nc.dram_tensor("b_o2", [1, D], F32, kind="ExternalInput").ap()
    consts = {n: nc.dram_tensor(n, list(s), d, kind="ExternalInput").ap()
              for n, s, d in FUSED_CONST_SPECS}
    xq_out = nc.dram_tensor("xq_out", [QT, D], F32, kind="ExternalOutput").ap()
    wbf = nc.dram_tensor("wbf", [4 * D, D], BF16).ap()
    qkv_fm = nc.dram_tensor("qkv_fm", [3 * D, SPAN], BF16).ap()
    attn_l = nc.dram_tensor("attn_l", [D, SPAN], BF16).ap()
    qdec = nc.dram_tensor("qdec", [D, SPAN], BF16).ap()
    gate_s = nc.dram_tensor("gate_s", [D, SPAN], BF16).ap()
    kvsum_l = nc.dram_tensor("kvsum_l", [D, DH], F32).ap()
    wo_bf = nc.dram_tensor("wo_bf", [D, D], BF16).ap()
    x1 = nc.dram_tensor("x1", [SPAN, D], F32).ap()
    x2 = nc.dram_tensor("x2", [SPAN, D], F32).ap()
    xq = nc.dram_tensor("xq", [QT, D], F32).ap()
    x3 = nc.dram_tensor("x3", [QT, D], F32).ap()
    delta_a = nc.dram_tensor("delta_a", [SPAN, D], BF16).ap()
    delta_f = nc.dram_tensor("delta_f", [SPAN, D], BF16).ap()
    delta_2 = nc.dram_tensor("delta_2", [QT, D], BF16).ap()
    wg_bf = nc.dram_tensor("wg_bf", [HID, D], BF16).ap()
    wu_bf = nc.dram_tensor("wu_bf", [HID, D], BF16).ap()
    wd_bf = nc.dram_tensor("wd_bf", [D, HID], BF16).ap()
    u3_bf = nc.dram_tensor("u3_bf", [3 * DC, D], BF16).ap()
    Kl = nc.dram_tensor("Kl", [D, SPAN], BF16).ap()
    Vl = nc.dram_tensor("Vl", [SPAN, D], BF16).ap()
    kRl = nc.dram_tensor("kRl", [DR, SPAN], BF16).ap()
    qf_l = nc.dram_tensor("qf_l", [D + H * DR, QT], BF16).ap()
    sh_sum = nc.dram_tensor("sh_sum", [2, D, DH], F32,
                            addr_space="Shared").ap()
    sh_K = nc.dram_tensor("sh_K", [2, D, SPAN], BF16,
                          addr_space="Shared").ap()
    sh_V = nc.dram_tensor("sh_V", [2, SPAN, D], BF16,
                          addr_space="Shared").ap()
    sh_kR = nc.dram_tensor("sh_kR", [2, DR, SPAN], BF16,
                           addr_space="Shared").ap()
    fl_sum = nc.dram_tensor("fl_sum", [2, 16], I32, addr_space="Shared").ap()
    fl_kv = nc.dram_tensor("fl_kv", [2, 16], I32, addr_space="Shared").ap()

    S = Sched(nc)
    bufs = Bufs(nc, S)
    arena = Arena(nc, S, "arena", 61440, BF16)
    arf = Arena(nc, S, "arf", 3584, F32)
    exsemsA = [nc.semaphore_raw(f"EXA{i}") for i in range(4)]
    exsemsB = [nc.semaphore_raw(f"EXB{i}") for i in range(4)]
    psem1 = nc.semaphore_raw("PL1")
    psem2 = nc.semaphore_raw("PL2")
    b_x = S.buf("x")
    b_wbf, b_qkvfm = S.buf("wbf"), S.buf("qkv_fm")
    b_attnl, b_qdec, b_gates = S.buf("attn_l"), S.buf("qdec"), S.buf("gate_s")
    b_kvsl, b_wobf = S.buf("kvsum_l"), S.buf("wo_bf")
    b_x1, b_x2, b_xq, b_x3 = S.buf("x1"), S.buf("x2"), S.buf("xq"), S.buf("x3")
    b_da, b_df, b_d2 = S.buf("delta_a"), S.buf("delta_f"), S.buf("delta_2")
    b_wg, b_wu, b_wd, b_u3 = (S.buf("wg_bf"), S.buf("wu_bf"), S.buf("wd_bf"),
                              S.buf("u3_bf"))
    b_Kl, b_Vl, b_kRl, b_qfl = (S.buf("Kl"), S.buf("Vl"), S.buf("kRl"),
                                S.buf("qf_l"))
    b_shsum, b_shK, b_shV, b_shkR = (S.buf("sh_sum"), S.buf("sh_K"),
                                     S.buf("sh_V"), S.buf("sh_kR"))
    b_xqo = S.buf("xq_out")

    with nc.Block() as block:
        (ident, b_id), (ones1, b_o1), (onesb, b_ob) = emit_setup(S, bufs)
        pk_all, _bpka = bufs.ps("pk_all", (128, 3584), F32)
        pk, b_pk = [], []
        for i in range(7):
            pk.append(pk_all[:, i * 512:(i + 1) * 512])
            b_pk.append(S.buf(f"pk{i}"))
        bufs.ps("pbf", (128, 1024), BF16)
        ftile, b_ft = bufs.sb("ftile", (1, 16), I32)
        frtile, b_fr = bufs.sb("frtile", (1, 32), I32)

        bufs.sb("st_f0", (128, D), F32)
        bufs.sb("st_f1", (128, D), F32)
        bufs.sb("st_b0", (128, D), BF16)
        bufs.sb("st_b1", (128, D), BF16)
        bufs.sb("st_b2", (128, D), BF16)
        bufs.sb("st_b3", (128, D), BF16)
        wc = [bufs.tiles[f"st_f{i}"] for i in range(2)]
        wcb = [bufs.tiles[f"st_b{i}"] for i in range(2)]
        wbf_evs = []
        for i in range(4 * ND):
            bb = i % 2
            (tc, b_tc), (tb, b_tb) = wc[bb], wcb[bb]
            src = (w_qkv[i * 128:(i + 1) * 128, :] if i < 3 * ND else
                   w_gate[(i - 3 * ND) * 128:(i - 3 * ND + 1) * 128, :])
            S.dma(tc, src, writes=[b_tc])
            S.op("dve", lambda q, tc=tc, tb=tb: q.tensor_copy(tb, tc),
                 reads=[b_tc], writes=[b_tb])
            wbf_evs.append(S.dma(wbf[i * 128:(i + 1) * 128, :], tb, eng="pool",
                                 reads=[b_tb], region_of=[b_wbf]))
        S.seal(b_wbf, wbf_evs)

        ln_fm, b_ln = arena.view("p1ln", 0, ND * TPC)
        wt = [arena.view(f"p1wt{i}", ND * TPC + i * ND * 512,
                         ND * TPC + (i + 1) * ND * 512) for i in range(2)]
        ev = [bufs.sb(f"evs{i}", (128, 512), BF16) for i in range(3)]
        qkv_evs, gate_evs = [], []
        for ch in range(2):
            t0 = ch * TPC
            emit_ln(S, bufs, x, b_x, ln_fm, b_ln, t0, TPC, "1", ident, b_id)
            for jb in range((4 * D) // 512):
                bb = jb % 2
                tw, b_tw = wt[bb]
                wt_evs = [S.dmaT(tw[:, dc * 512:(dc + 1) * 512],
                                 wbf[jb * 512:(jb + 1) * 512,
                                     dc * 128:(dc + 1) * 128],
                                 reads=[b_wbf], region_of=[b_tw])
                          for dc in range(ND)]
                S.seal(b_tw, wt_evs)
                for j4 in range(4):
                    jc = jb * 4 + j4
                    for nt in range(4):
                        k = (jc * 4 + nt) % 6
                        tp, b_tp = pk[k], b_pk[k]

                        def mmg(q, tw=tw, j4=j4, nt=nt, tp=tp):
                            r = None
                            for dc in range(ND):
                                r = q.matmul(
                                    tp,
                                    tw[:, dc * 512 + j4 * 128:
                                       dc * 512 + j4 * 128 + 128],
                                    ln_fm[:, dc * TPC + nt * 512:
                                          dc * TPC + (nt + 1) * 512],
                                    start=(dc == 0), stop=(dc == ND - 1))
                            return r
                        S.op("pe", mmg, reads=[b_tw, b_ln], writes=[b_tp])
                        te, b_te = ev[(jc * 4 + nt) % 3]
                        func = AF.Silu if jc < 3 * ND else AF.Sigmoid
                        S.op("act", lambda q, tp=tp, te=te, func=func:
                             q.activation(te, tp, func), reads=[b_tp],
                             writes=[b_te])
                        cc0 = t0 + nt * 512
                        if jc < 3 * ND:
                            qkv_evs.append(S.dma(
                                qkv_fm[jc * 128:(jc + 1) * 128, cc0:cc0 + 512],
                                te, eng="pool", reads=[b_te],
                                region_of=[b_qkvfm]))
                        else:
                            row = (jc - 3 * ND) * 128
                            gate_evs.append(S.dma(
                                gate_s[row:row + 128, cc0:cc0 + 512],
                                te, eng="pool", reads=[b_te],
                                region_of=[b_gates]))
        S.seal(b_qkvfm, qkv_evs)
        S.seal(b_gates, gate_evs)

        sl_row, b_slr = bufs.sb("sl_row", (1, H), F32)
        S.dma(sl_row, slope, writes=[b_slr])
        sb_s, b_ss = emit_bcast_row(S, bufs, pk[0], b_pk[0], ones1, b_o1,
                                    sl_row, b_slr, "sb_s", H)
        bj_row, b_bjr = bufs.sb("bj_row", (1, NBK), F32)
        S.dma(bj_row, consts["blkj"], writes=[b_bjr])
        bjb, b_bjb = emit_bcast_row(S, bufs, pk[1], b_pk[1], ones1, b_o1,
                                    bj_row, b_bjr, "bjb", NBK)
        qdi, b_qdi = arf.view("qdi", 0, BLK)
        kdi, b_kdi = arf.view("kdi", BLK, 2 * BLK)
        dd0, b_dd0 = arf.view("dd0", 2 * BLK, 3 * BLK)
        dd1, b_dd1 = arf.view("dd1", 3 * BLK, 4 * BLK)
        S.dma(qdi, consts["qdecay_idx"], writes=[b_qdi])
        S.dma(kdi, consts["kdecay_idx"], writes=[b_kdi])
        S.dma(dd0, consts["ddT0"], writes=[b_dd0])
        S.dma(dd1, consts["ddT1"], writes=[b_dd1])
        dblk, b_dblk = bufs.sb("dblk", (128, H), F32)
        S.op("act", lambda q: q.activation(dblk, sb_s, AF.Exp,
                                           scale=-float(BLK)),
             reads=[b_ss], writes=[b_dblk])
        dgl, b_dgl = arf.view("dgl", 4 * BLK, 4 * BLK + H * NBK)
        dgl_evs = []
        for h in range(H):
            dgl_evs.append(S.op("dve", lambda q, h=h: q.tensor_scalar_mul(
                dgl[:, h * NBK:(h + 1) * NBK], bjb, sb_s[:, h:h + 1]),
                reads=[b_bjb, b_ss], region_of=[b_dgl]))
        S.seal(b_dgl, dgl_evs)
        S.op("act", lambda q: q.activation(dgl, dgl, AF.Exp, scale=-1.0),
             reads=[b_dgl], writes=[b_dgl])
        qdb, b_qdb = arena.view("qdb", 0, H * BLK)
        kdb, b_kdb = arena.view("kdb", H * BLK, 2 * H * BLK)
        ddb, b_ddb = arena.view("ddb", 2 * H * BLK, 4 * H * BLK)
        tmpd, b_tmpd = bufs.sb("tmpd", (128, BLK), F32)
        for h in range(H):
            for (srcT, b_src, dstT, b_dst, col) in (
                    (qdi, b_qdi, qdb, b_qdb, h * BLK),
                    (kdi, b_kdi, kdb, b_kdb, h * BLK),
                    (dd0, b_dd0, ddb, b_ddb, 2 * h * BLK),
                    (dd1, b_dd1, ddb, b_ddb, (2 * h + 1) * BLK)):
                S.op("dve", lambda q, srcT=srcT, h=h: q.tensor_scalar_mul(
                    tmpd, srcT, sb_s[:, h:h + 1]),
                    reads=[b_src, b_ss], writes=[b_tmpd])
                S.op("act", lambda q, dstT=dstT, col=col: q.activation(
                    dstT[:, col:col + BLK], tmpd, AF.Exp, scale=-1.0),
                    reads=[b_tmpd], region_of=[b_dst])
        ready = ("act", S.cnt["act"])
        for b in (b_qdb, b_kdb, b_ddb):
            S.seal(b, [ready])

        pbf, b_pbf = bufs.tiles["pbf"]
        C_QKV = 4 * H * BLK
        qf = [arena.view(f"sqf{i}", C_QKV + i * SPAN, C_QKV + (i + 1) * SPAN)
              for i in range(2)]
        kf = [arena.view(f"skf{i}", C_QKV + (2 + i) * SPAN,
                         C_QKV + (3 + i) * SPAN) for i in range(2)]
        vf = [arena.view(f"svf{i}", C_QKV + (4 + i) * SPAN,
                         C_QKV + (5 + i) * SPAN) for i in range(2)]
        C_SC = C_QKV + 6 * SPAN
        qdxF, b_qdxF = arena.view("qdxF", C_SC, C_SC + SPAN)
        qgxF, b_qgxF = arena.view("qgxF", C_SC + SPAN, C_SC + 2 * SPAN)
        kdxF, b_kdxF = arena.view("kdxF", C_SC + 2 * SPAN, C_SC + 3 * SPAN)
        vkmF, b_vkmF = arena.view("vkmF", C_SC + 3 * SPAN, C_SC + 4 * SPAN)
        kkmF, b_kkmF = arena.view("kkmF", C_SC + 4 * SPAN, C_SC + 5 * SPAN)

        def rep_ap(view_ap, inner, nrep, rep_inner):
            row = list(view_ap.ap[0])
            if rep_inner:
                return bass.AP(view_ap.tensor, view_ap.offset,
                               [row, [1, inner], [0, nrep]])
            return bass.AP(view_ap.tensor, view_ap.offset,
                           [row, [0, nrep], [1, inner]])

        kvf, b_kvf = bufs.sb("kvf", (128, DH), F32)
        kvb, b_kvb = bufs.sb("kvb", (128, DH), BF16)
        ptt, b_ptt = bufs.sb("ptt", (128, 2 * BLK), BF16)
        aev = [bufs.sb(f"aev{i}", (128, BLK), BF16) for i in range(2)]
        attnl_evs, qdec_evs, kvsl_evs = [], [], []
        for h in range(H):
            hb = h % 2
            (tq, b_tq), (tk, b_tk), (tv, b_tv) = qf[hb], kf[hb], vf[hb]
            S.dma(tq, qkv_fm[h * 128:(h + 1) * 128, :],
                  reads=[b_qkvfm], writes=[b_tq])
            S.dma(tk, qkv_fm[D + h * 128:D + (h + 1) * 128, :],
                  reads=[b_qkvfm], writes=[b_tk])
            S.dma(tv, qkv_fm[2 * D + h * 128:2 * D + (h + 1) * 128, :],
                  reads=[b_qkvfm], writes=[b_tv])
            S.op("dve", lambda q, tq=tq, h=h: q.tensor_tensor(
                qdxF.rearrange("p (j m) -> p j m", j=NBK),
                tq.rearrange("p (j m) -> p j m", j=NBK),
                rep_ap(qdb[:, h * BLK:(h + 1) * BLK], BLK, NBK, False),
                ALU.mult),
                reads=[b_tq, b_qdb], writes=[b_qdxF])
            S.op("dve", lambda q, h=h: q.tensor_tensor(
                qgxF.rearrange("p (j m) -> p j m", j=NBK),
                qdxF.rearrange("p (j m) -> p j m", j=NBK),
                rep_ap(dgl[:, h * NBK:(h + 1) * NBK], NBK, BLK, True),
                ALU.mult),
                reads=[b_qdxF, b_dgl], writes=[b_qgxF])
            qdec_evs.append(S.dma(qdec[h * 128:(h + 1) * 128, :], qgxF,
                                  eng="pool", reads=[b_qgxF],
                                  region_of=[b_qdec]))
            S.op("dve", lambda q, tk=tk, h=h: q.tensor_tensor(
                kdxF.rearrange("p (j m) -> p j m", j=NBK),
                tk.rearrange("p (j m) -> p j m", j=NBK),
                rep_ap(kdb[:, h * BLK:(h + 1) * BLK], BLK, NBK, False),
                ALU.mult),
                reads=[b_tk, b_kdb], writes=[b_kdxF])
            for src_t, b_src_t, dst_t, b_dst_t in (
                    (tv, b_tv, vkmF, b_vkmF), (kdxF, b_kdxF, kkmF, b_kkmF)):
                tevs = []
                for bt in range(SPAN // 1024):
                    def tg(q, src_t=src_t, bt=bt):
                        r = None
                        for c8 in range(8):
                            r = q.transpose(
                                pbf[:, c8 * 128:(c8 + 1) * 128],
                                src_t[:, bt * 1024 + c8 * 128:
                                      bt * 1024 + (c8 + 1) * 128], ident)
                        return r
                    S.op("pe", tg, reads=[b_src_t, b_id], writes=[b_pbf])
                    tevs.append(S.op(
                        "dve", lambda q, dst_t=dst_t, bt=bt: q.tensor_copy(
                            dst_t[:, bt * 1024:(bt + 1) * 1024], pbf),
                        reads=[b_pbf], region_of=[b_dst_t]))
                S.seal(b_dst_t, tevs)
            S.op("dve", lambda q: q.memset(kvf, 0.0), writes=[b_kvf])
            S.op("dve", lambda q: q.memset(kvb, 0.0), writes=[b_kvb])
            for j in range(NBK):
                t0 = j * BLK

                def mmsc(q, tk=tk, tq=tq, t0=t0):
                    r = None
                    for cc in range(2):
                        r = q.matmul(pk_all[:, (4 + cc) * 512:
                                            (4 + cc) * 512 + BLK],
                                     tk[:, t0 + cc * 128:t0 + (cc + 1) * 128],
                                     tq[:, t0:t0 + BLK], start=True,
                                     stop=True)
                    return r
                S.op("pe", mmsc, reads=[b_tk, b_tq],
                     writes=[b_pk[4], b_pk[5]])
                sview = bass.AP(pk_all.tensor, pk_all.offset + 4 * 512,
                                [list(pk_all.ap[0]), [512, 2], [1, BLK]])
                S.op("dve", lambda q, sview=sview, h=h: q.tensor_tensor(
                    ptt.rearrange("p (c m) -> p c m", c=2), sview,
                    ddb[:, 2 * h * BLK:2 * (h + 1) * BLK].rearrange(
                        "p (c m) -> p c m", c=2), ALU.mult),
                    reads=[b_pk[4], b_pk[5], b_ddb], writes=[b_ptt])
                tp, b_tp = pk[j % 2], b_pk[j % 2]

                def outg(q, tp=tp, t0=t0):
                    q.matmul(tp[:, 0:BLK], kvb, qdxF[:, t0:t0 + BLK],
                             start=True, stop=False)
                    q.matmul(tp[:, 0:BLK], vkmF[:, t0:t0 + 128],
                             ptt[:, 0:BLK], start=False, stop=False)
                    return q.matmul(tp[:, 0:BLK],
                                    vkmF[:, t0 + 128:t0 + 256],
                                    ptt[:, BLK:2 * BLK], start=False,
                                    stop=True)
                S.op("pe", outg, reads=[b_kvb, b_qdxF, b_vkmF, b_ptt],
                     writes=[b_tp])
                ta, b_ta = aev[j % 2]
                S.op("act", lambda q, tp=tp, ta=ta: q.activation(
                    ta, tp[:, 0:BLK], AF.Copy), reads=[b_tp], writes=[b_ta])
                attnl_evs.append(S.dma(
                    attn_l[h * 128:(h + 1) * 128, t0:t0 + BLK], ta,
                    eng="pool", reads=[b_ta], region_of=[b_attnl]))
                tpv, b_tpv = pk[2 + (j % 2)], b_pk[2 + (j % 2)]

                def kvg(q, tpv=tpv, t0=t0):
                    q.matmul(tpv[:, 0:DH], kkmF[:, t0:t0 + 128],
                             vkmF[:, t0:t0 + 128], start=True, stop=False)
                    return q.matmul(tpv[:, 0:DH],
                                    kkmF[:, t0 + 128:t0 + 256],
                                    vkmF[:, t0 + 128:t0 + 256], start=False,
                                    stop=True)
                S.op("pe", kvg, reads=[b_kkmF, b_vkmF], writes=[b_tpv])
                S.op("dve", lambda q, h=h, tpv=tpv: q.scalar_tensor_tensor(
                    kvf, kvf, dblk[:, h:h + 1], tpv[:, 0:DH],
                    ALU.mult, ALU.add),
                    reads=[b_dblk, b_tpv, b_kvf], writes=[b_kvf])
                if j < NBK - 1:
                    S.op("dve", lambda q: q.tensor_copy(kvb, kvf),
                         reads=[b_kvf], writes=[b_kvb])
            kvsl_evs.append(S.dma(kvsum_l[h * 128:(h + 1) * 128, :], kvf,
                                  eng="pool", reads=[b_kvf],
                                  region_of=[b_kvsl]))
        S.seal(b_attnl, attnl_evs)
        S.seal(b_qdec, qdec_evs)
        S.seal(b_kvsl, kvsl_evs)

        S.op("pool", lambda q: q.memset(ftile, MAGIC1), writes=[b_ft])
        ex1 = emit_exchange_write(
            S, nc, exsemsA,
            [(lambda r: kvsum_l, lambda r: sh_sum[r])],
            fl_sum, ftile, b_ft, MAGIC1, [b_kvsl.w], "s")
        S.seal(b_shsum, [ex1])
        pl1 = emit_poll(S, nc, psem1, fl_sum, frtile, b_fr, MAGIC1,
                        [ex1], "s")

        emit_cast_pass(S, bufs, w_out, wo_bf, b_wobf, ND, D, "wo")
        u3_evs = []
        for wi, wsrc in enumerate((w_uk, w_uv, w_uq)):
            for cc in range(4):
                (tt, b_tt) = bufs.tiles["st_f0"]
                (tb, b_tb) = bufs.tiles["cpowo0"]
                S.dma(tt, wsrc[cc * 128:(cc + 1) * 128, :], writes=[b_tt])
                S.op("dve", lambda q, tt=tt, tb=tb: q.tensor_copy(tb, tt),
                     reads=[b_tt], writes=[b_tb])
                u3_evs.append(S.dma(
                    u3_bf[wi * DC + cc * 128:wi * DC + (cc + 1) * 128, :],
                    tb, eng="pool", reads=[b_tb], region_of=[b_u3]))
        S.seal(b_u3, u3_evs)
        sgu, b_sgu = emit_col_load(S, bufs, "sgu", s_g, NH)
        suu, b_suu = emit_col_load(S, bufs, "suu", s_u, NH)
        sdd, b_sdd = emit_col_load(S, bufs, "sdd", s_d, ND)
        emit_dequant(S, bufs, wq_g, sgu, b_sgu, 0, wg_bf, b_wg, NH, D, "g")
        emit_dequant(S, bufs, wq_u, suu, b_suu, 0, wu_bf, b_wu, NH, D, "u")
        emit_dequant(S, bufs, wq_d, sdd, b_sdd, 0, wd_bf, b_wd, ND, HID, "d")

        d0b, b_d0b = bufs.sb("d0b", (128, 1), F32)
        S.dma(d0b, consts["dist0"], writes=[b_d0b])
        w0t, b_w0t = bufs.sb("w0t", (128, H), F32)
        S.op("dve", lambda q: q.tensor_scalar_mul(w0t, sb_s, d0b),
             reads=[b_ss, b_d0b], writes=[b_w0t])
        S.op("act", lambda q: q.activation(w0t, w0t, AF.Exp, scale=-1.0),
             reads=[b_w0t], writes=[b_w0t])
        nw, b_nw = emit_col_load(S, bufs, "nw", norm_w, ND)
        ksl, b_ksl = arena.view("ksl", ND * TPC + 2 * ND * 512,
                                ND * TPC + 2 * ND * 512 + H * DH)
        kvsf, b_kvsf = bufs.sb("kvsf", (128, DH), F32)
        kvp0, b_kvp0 = bufs.sb("kvp0", (128, DH), F32)
        ksl_evs = []
        for h in range(H):
            S.dma(kvp0, sh_sum[0][h * 128:(h + 1) * 128, :],
                  deps=[pl1], writes=[b_kvp0])
            S.op("dve", lambda q, h=h: q.tensor_scalar_mul(
                kvsf, kvp0, w0t[:, h:h + 1]),
                reads=[b_kvp0, b_w0t], writes=[b_kvsf])
            ksl_evs.append(S.op(
                "dve", lambda q, h=h: q.tensor_copy(
                    ksl[:, h * DH:(h + 1) * DH], kvsf),
                reads=[b_kvsf], region_of=[b_ksl]))
        S.seal(b_ksl, ksl_evs)

        slab, b_slab = arena.view("a2slab", 0, ND * TPC)
        woT = [arena.view(f"a2woT{i}", ND * TPC + i * ND * 512,
                          ND * TPC + (i + 1) * ND * 512) for i in range(2)]
        qdt = [bufs.tiles[f"st_b{i}"] for i in range(2)]
        att = [bufs.tiles[f"st_b{2 + i}"] for i in range(2)]
        sqt, b_sqt = bufs.sb("sqt", (128, TPC), BF16)
        eps_r, b_epsr = bufs.const(RMS_EPS, parts=1)
        gt = [bufs.tiles[f"st_b{i}"] for i in range(2)]
        t1, b_t1 = bufs.tiles["st_b2"]
        dev = [bufs.sb(f"fdev{i}", (128, 512), BF16) for i in range(3)]
        da_evs = []
        for ch in range(2):
            c0 = ch * TPC
            for h in range(H):
                bb = h % 2
                (tqd, b_tqd), (tat, b_tat) = qdt[bb], att[bb]
                S.dma(tqd, qdec[h * 128:(h + 1) * 128, c0:c0 + TPC],
                      reads=[b_qdec], writes=[b_tqd])
                S.dma(tat, attn_l[h * 128:(h + 1) * 128, c0:c0 + TPC],
                      reads=[b_attnl], writes=[b_tat])
                for nt in range(4):
                    tp, b_tp = pk[nt % 2], b_pk[nt % 2]
                    S.op("pe", lambda q, tqd=tqd, nt=nt, tp=tp, h=h: q.matmul(
                        tp, ksl[:, h * DH:(h + 1) * DH],
                        tqd[:, nt * 512:(nt + 1) * 512],
                        start=True, stop=True),
                        reads=[b_ksl, b_tqd], writes=[b_tp])
                    scols = slab[:, h * TPC + nt * 512:
                                 h * TPC + (nt + 1) * 512]
                    S.op("dve", lambda q, tp=tp, scols=scols, tat=tat, nt=nt:
                         q.tensor_add(scols,
                                      tat[:, nt * 512:(nt + 1) * 512], tp),
                         reads=[b_tp, b_tat], region_of=[b_slab])
                S.seal(b_slab, list(_evs(b_slab.w)) + [("dve",
                                                        S.cnt["dve"])])
                S.op("act", lambda q, h=h: q.activation(
                    sqt, slab[:, h * TPC:(h + 1) * TPC], AF.Square),
                    reads=[b_slab], writes=[b_sqt])
                for mt in range(4):
                    tp, b_tp = pk[2 + mt], b_pk[2 + mt]
                    S.op("pe", lambda q, mt=mt, tp=tp, h=h: q.matmul(
                        tp[0:1, :], onesb, sqt[:, mt * 512:(mt + 1) * 512],
                        start=(h == 0), stop=(h == H - 1)),
                        reads=[b_ob, b_sqt],
                        writes=[b_tp] if h == H - 1 else [],
                        region_of=[] if h == H - 1 else [b_tp])
            r_b, b_rb = arf.view(f"r_b{ch}", 0, TPC)
            rr, b_rr = r_b[0:1, :], S.buf(f"rr{ch}")
            rr_evs = []
            for mt in range(4):
                tp, b_tp = pk[2 + mt], b_pk[2 + mt]
                rr_evs.append(S.op("act", lambda q, tp=tp, mt=mt: q.activation(
                    rr[:, mt * 512:(mt + 1) * 512], tp[0:1, :], AF.Ln,
                    scale=1.0 / D, bias=eps_r), reads=[b_tp, b_epsr],
                    deps=[b_rb.w]))
            S.seal(b_rr, rr_evs)
            S.op("act", lambda q: q.activation(rr, rr, AF.Exp, scale=-0.5),
                 reads=[b_rr], writes=[b_rr])
            rb_evs = []
            for mt in range(4):
                tp, b_tp = pk[mt % 2], b_pk[mt % 2]
                S.op("pe", lambda q, tp=tp, mt=mt: q.matmul(
                    tp, ones1, rr[:, mt * 512:(mt + 1) * 512],
                    start=True, stop=True), reads=[b_o1, b_rr],
                    writes=[b_tp])
                rb_evs.append(S.op(
                    "dve", lambda q, tp=tp, mt=mt: q.tensor_copy(
                        r_b[:, mt * 512:(mt + 1) * 512], tp), reads=[b_tp],
                    region_of=[b_rb]))
            S.seal(b_rb, rb_evs)
            for h in range(H):
                bb = h % 2
                (tg, b_tg) = gt[bb]
                S.dma(tg, gate_s[h * 128:(h + 1) * 128, c0:c0 + TPC],
                      reads=[b_gates], writes=[b_tg])
                S.op("dve", lambda q, h=h: q.scalar_tensor_tensor(
                    t1, slab[:, h * TPC:(h + 1) * TPC], nw[:, h:h + 1], r_b,
                    ALU.mult, ALU.mult), reads=[b_slab, b_nw, b_rb],
                    writes=[b_t1])
                S.op("dve", lambda q, h=h, tg=tg: q.tensor_mul(
                    slab[:, h * TPC:(h + 1) * TPC], t1, tg),
                    reads=[b_t1, b_tg], region_of=[b_slab])
            S.seal(b_slab, [("dve", S.cnt["dve"])])
            for dmb in range(4):
                tw, b_tw = woT[dmb % 2]
                wevs = [S.dmaT(tw[:, jc * 512:(jc + 1) * 512],
                               wo_bf[dmb * 512:(dmb + 1) * 512,
                                     jc * 128:(jc + 1) * 128],
                               reads=[b_wobf], region_of=[b_tw])
                        for jc in range(ND)]
                S.seal(b_tw, wevs)
                for tc in range(TPC // 128):
                    tp, b_tp = pk[4 + tc % 2], b_pk[4 + tc % 2]

                    def mmw(q, tc=tc, tp=tp, tw=tw):
                        r = None
                        for jc in range(ND):
                            r = q.matmul(
                                tp, slab[:, jc * TPC + tc * 128:
                                         jc * TPC + tc * 128 + 128],
                                tw[:, jc * 512:(jc + 1) * 512],
                                start=(jc == 0), stop=(jc == ND - 1))
                        return r
                    S.op("pe", mmw, reads=[b_slab, b_tw], writes=[b_tp])
                    te, b_te = dev[tc % 3]
                    S.op("act", lambda q, tp=tp, te=te: q.activation(
                        te, tp, AF.Copy), reads=[b_tp], writes=[b_te])
                    da_evs.append(S.dma(
                        delta_a[c0 + tc * 128:c0 + (tc + 1) * 128,
                                dmb * 512:(dmb + 1) * 512],
                        te, eng="pool", reads=[b_te], region_of=[b_da]))
        S.seal(b_da, da_evs)
        emit_resid_tm(S, bufs, delta_a, b_da, x, b_x, x1, b_x1, "a",
                      ntok=SPAN)

        for ch in range(2):
            sl = slice(ch * TPC, (ch + 1) * TPC)
            b_x1c = S.alias_buf(b_x1, f"x1c{ch}")
            b_x2c = S.buf(f"x2c{ch}")
            b_dfc = S.alias_buf(b_df, f"dfc{ch}")
            emit_ffn(S, bufs, arena, pk, b_pk, x1[sl, :], b_x1c,
                     x2[sl, :], b_x2c, wg_bf, b_wg, wu_bf, b_wu, wd_bf, b_wd,
                     delta_f[sl, :], b_dfc, ident, b_id, f"1{ch}")
            S.seal(b_x2, list(_evs(b_x2.w)) + list(_evs(b_x2c.w)))

        qsel, b_qsel = bufs.sb("qselt", (128, 1), F32)
        S.dma(qsel, consts["qsel"], writes=[b_qsel])
        xq_evs = []
        for tc in range(QT // 128):
            (ta2, b_ta2) = bufs.tiles["st_f0"]
            (tb2_, b_tb2_) = bufs.tiles["st_f1"]
            S.dma(ta2, x2[tc * 128:(tc + 1) * 128, :], reads=[b_x2],
                  writes=[b_ta2])
            S.dma(tb2_, x2[TPC + tc * 128:TPC + (tc + 1) * 128, :],
                  reads=[b_x2], writes=[b_tb2_])
            S.op("dve", lambda q, ta2=ta2, tb2_=tb2_: q.tensor_sub(
                ta2, ta2, tb2_), reads=[b_tb2_], writes=[b_ta2])
            S.op("dve", lambda q, ta2=ta2, tb2_=tb2_: q.scalar_tensor_tensor(
                ta2, ta2, qsel, tb2_, ALU.mult, ALU.add),
                reads=[b_qsel, b_tb2_], writes=[b_ta2])
            xq_evs.append(S.dma(xq[tc * 128:(tc + 1) * 128, :], ta2,
                                eng="pool", reads=[b_ta2], region_of=[b_xq]))
        S.seal(b_xq, xq_evs)

        C_LN, C_DKV, C_DQ = 0, 8192, 16384
        C_UP = 24576
        C_KR, C_QR = 40960, 41984
        C_CKV, C_CQ = 46080, 48128
        C_QRS = 50176
        dkv_sb, b_dkvs = arena.view("dkv_sb", C_DKV, C_DKV + 8192)
        dq_sb, b_dqs = arena.view("dq_sb", C_DQ, C_DQ + 8192)
        kr_sb_w, b_krs = arena.view("kr_sb_w", C_KR, C_KR + 1024)
        qr_sb_w, b_qrs = arena.view("qr_sb_w", C_QR, C_QR + 4096)
        for dc in range(ND):
            (tt, b_tt) = bufs.tiles["st_f0"]
            (tb, b_tb) = bufs.tiles["cpowo0"]
            S.dma(tt[:, 0:DC], w_dkv[dc * 128:(dc + 1) * 128, :],
                  writes=[b_tt])
            S.op("dve", lambda q, tt=tt, tb=tb: q.tensor_copy(
                tb[:, 0:DC], tt[:, 0:DC]), reads=[b_tt], writes=[b_tb])
            S.op("dve", lambda q, tb=tb, dc=dc: q.tensor_copy(
                dkv_sb[:, dc * 512:(dc + 1) * 512], tb[:, 0:DC]),
                reads=[b_tb], region_of=[b_dkvs])
            S.dma(tt[:, 0:DC], w_dq[dc * 128:(dc + 1) * 128, :],
                  writes=[b_tt])
            S.op("dve", lambda q, tt=tt, tb=tb: q.tensor_copy(
                tb[:, 0:DC], tt[:, 0:DC]), reads=[b_tt], writes=[b_tb])
            S.op("dve", lambda q, tb=tb, dc=dc: q.tensor_copy(
                dq_sb[:, dc * 512:(dc + 1) * 512], tb[:, 0:DC]),
                reads=[b_tb], region_of=[b_dqs])
            S.dma(tt[:, 0:DR], w_kr[dc * 128:(dc + 1) * 128, :],
                  writes=[b_tt])
            S.op("dve", lambda q, tt=tt, dc=dc: q.tensor_copy(
                kr_sb_w[:, dc * 64:(dc + 1) * 64], tt[:, 0:DR]),
                reads=[b_tt], region_of=[b_krs])
        S.seal(b_dkvs, [("dve", S.cnt["dve"])])
        S.seal(b_dqs, [("dve", S.cnt["dve"])])
        S.seal(b_krs, [("dve", S.cnt["dve"])])
        for cc in range(4):
            (tt, b_tt) = bufs.tiles["st_f0"]
            S.dma(tt[:, 0:H * DR], w_qr[cc * 128:(cc + 1) * 128, :],
                  writes=[b_tt])
            dstv = qr_sb_w[:, cc * 1024:(cc + 1) * 1024].rearrange(
                "p (two h f) -> p two h f", two=2, h=H, f=DR // 2)
            srcv = tt[:, 0:H * DR].rearrange(
                "p (h two f) -> p two h f", h=H, two=2, f=DR // 2)
            S.op("dve", lambda q, dstv=dstv, srcv=srcv: q.tensor_copy(
                dstv, srcv), reads=[b_tt], region_of=[b_qrs])
        S.seal(b_qrs, [("dve", S.cnt["dve"])])
        posr, b_posr = bufs.sb("posr", (1, 512), F32)
        invf, b_invf = bufs.sb("invf", (128, 1), F32)
        S.dma(invf, consts["invfreq4"], writes=[b_invf])
        c_n2pi, b_n2pi = bufs.const(-2.0 * np.pi)
        c_hpi, b_hpi = bufs.const(0.5 * np.pi)
        TGC = 512
        upw = [arena.view(f"upw{i}", C_UP + i * 8192, C_UP + (i + 1) * 8192)
               for i in range(2)]
        _dqt, _bdq = bufs.tiles["dqi0"]
        tki, b_tki = _dqt[:, 0:TGC], S.alias_buf(_bdq, "tkir")

        def emit_trig(g_idx, pos_hbm, pcol0, tag):
            pb, b_pb = arf.view(f"pb{tag}", 0, TGC)
            ang, b_ang = arf.view(f"ang{tag}", TGC, 2 * TGC)
            ta_, b_ta_ = arf.view(f"rota{tag}", 5 * TGC, 6 * TGC)
            tb_, b_tb_ = arf.view(f"rotb{tag}", 6 * TGC, 7 * TGC)
            S.dma(posr, pos_hbm[:, pcol0:pcol0 + TGC], writes=[b_posr])
            S.op("pe", lambda q: q.matmul(
                pk[1], ones1, posr, start=True, stop=True),
                reads=[b_o1, b_posr], writes=[b_pk[1]])
            S.op("dve", lambda q: q.tensor_copy(pb, pk[1]), reads=[b_pk[1]],
                 writes=[b_pb])
            S.op("dve", lambda q: q.tensor_scalar_mul(ang, pb, invf),
                 reads=[b_pb, b_invf], writes=[b_ang])
            out = {}
            for trig in ("sin", "cos"):
                tt1, b_tt1 = arf.view(f"tt1{tag}{trig}", 2 * TGC, 3 * TGC)
                off_t = 3 * TGC if trig == "sin" else 4 * TGC
                tst, b_tst = arf.view(f"tst{tag}{trig}", off_t, off_t + TGC)
                off = 0.0 if trig == "sin" else 0.5 * np.pi
                S.op("dve", lambda q, tt1=tt1, off=off: q.tensor_scalar(
                    tt1, ang, off, 1.0 / (2.0 * np.pi), ALU.add, ALU.mult),
                    reads=[b_ang], writes=[b_tt1])
                S.op("dve", lambda q, tt1=tt1: q.tensor_copy(tki, tt1),
                     reads=[b_tt1], writes=[b_tki])
                S.op("dve", lambda q, tt1=tt1: q.tensor_copy(tt1, tki),
                     reads=[b_tki], writes=[b_tt1])
                S.op("dve", lambda q, tt1=tt1, tst=tst:
                     q.scalar_tensor_tensor(tst, tt1, c_n2pi, ang,
                                            ALU.mult, ALU.add),
                     reads=[b_tt1, b_n2pi, b_ang], writes=[b_tst])
                S.op("dve", lambda q, tst=tst, off=off:
                     q.tensor_single_scalar(tb_, tst, float(np.pi) - off,
                                            ALU.is_gt),
                     reads=[b_tst], writes=[b_tb_])
                S.op("dve", lambda q, tst=tst: q.scalar_tensor_tensor(
                    tst, tb_, c_n2pi, tst, ALU.mult, ALU.add),
                    reads=[b_tb_, b_n2pi, b_tst], writes=[b_tst])
                if trig == "sin":
                    S.op("act", lambda q, tst=tst: q.activation(
                        tst, tst, AF.Sin), reads=[b_tst], writes=[b_tst])
                else:
                    S.op("act", lambda q, tst=tst: q.activation(
                        tst, tst, AF.Sin, bias=c_hpi),
                        reads=[b_tst, b_hpi], writes=[b_tst])
                out[trig] = (tst, b_tst)
            return out, (ta_, b_ta_), (tb_, b_tb_)

        Kl_evs, Vl_evs, kRl_evs = [], [], []
        for g in range(SPAN // TGC):
            lnc, b_lnc = arena.view(f"lnc{g}", C_LN, C_LN + ND * TGC)
            emit_ln(S, bufs, x2, b_x2, lnc, b_lnc, g * TGC, TGC, f"c{g}",
                    ident, b_id)
            ckv, b_ckv = arena.view(f"ckv{g}", C_CKV, C_CKV + 2048)
            ev3 = [bufs.sb(f"evc{i}", (128, 512), BF16) for i in range(3)]
            cevs = []
            for cc2 in range(4):
                tp, b_tp = pk[cc2 % 2], b_pk[cc2 % 2]

                def mmc(q, cc2=cc2, tp=tp, lnc=lnc):
                    r = None
                    for dc in range(ND):
                        r = q.matmul(
                            tp, dkv_sb[:, dc * 512 + cc2 * 128:
                                       dc * 512 + cc2 * 128 + 128],
                            lnc[:, dc * TGC:(dc + 1) * TGC],
                            start=(dc == 0), stop=(dc == ND - 1))
                    return r
                S.op("pe", mmc, reads=[b_dkvs, b_lnc], writes=[b_tp])
                cevs.append(S.op(
                    "act", lambda q, tp=tp, cc2=cc2: q.activation(
                        ckv[:, cc2 * 512:(cc2 + 1) * 512], tp, AF.Copy),
                    reads=[b_tp], region_of=[b_ckv]))
            S.seal(b_ckv, cevs)
            for wi, which in ((0, "K"), (1, "V")):
                tw, b_tw = upw[wi % 2]
                wevs = [S.dma(tw[:, cc2 * 2048:(cc2 + 1) * 2048],
                              u3_bf[wi * DC + cc2 * 128:
                                    wi * DC + (cc2 + 1) * 128, :],
                              reads=[b_u3], region_of=[b_tw])
                        for cc2 in range(4)]
                S.seal(b_tw, wevs)
                if which == "K":
                    for ec in range(ND):
                        tp, b_tp = pk[2 + ec % 2], b_pk[2 + ec % 2]

                        def mmk(q, ec=ec, tp=tp, tw=tw):
                            r = None
                            for cc2 in range(4):
                                r = q.matmul(
                                    tp, tw[:, cc2 * 2048 + ec * 128:
                                           cc2 * 2048 + ec * 128 + 128],
                                    ckv[:, cc2 * 512:(cc2 + 1) * 512],
                                    start=(cc2 == 0), stop=(cc2 == 3))
                            return r
                        S.op("pe", mmk, reads=[b_tw, b_ckv], writes=[b_tp])
                        te, b_te = ev3[ec % 3]
                        S.op("act", lambda q, tp=tp, te=te: q.activation(
                            te, tp, AF.Copy), reads=[b_tp], writes=[b_te])
                        Kl_evs.append(S.dma(
                            Kl[ec * 128:(ec + 1) * 128,
                               g * TGC:(g + 1) * TGC], te, eng="pool",
                            reads=[b_te], region_of=[b_Kl]))
                else:
                    for tc in range(4):
                        for eb in range(4):
                            tp, b_tp = pk[2 + eb % 2], b_pk[2 + eb % 2]

                            def mmv(q, tc=tc, eb=eb, tp=tp, tw=tw, ckv=ckv):
                                r = None
                                for cc2 in range(4):
                                    r = q.matmul(
                                        tp, ckv[:, cc2 * 512 + tc * 128:
                                                cc2 * 512 + tc * 128 + 128],
                                        tw[:, cc2 * 2048 + eb * 512:
                                           cc2 * 2048 + (eb + 1) * 512],
                                        start=(cc2 == 0), stop=(cc2 == 3))
                                return r
                            S.op("pe", mmv, reads=[b_tw, b_ckv],
                                 writes=[b_tp])
                            te, b_te = ev3[eb % 3]
                            S.op("act", lambda q, tp=tp, te=te: q.activation(
                                te, tp, AF.Copy), reads=[b_tp],
                                writes=[b_te])
                            Vl_evs.append(S.dma(
                                Vl[g * TGC + tc * 128:
                                   g * TGC + (tc + 1) * 128,
                                   eb * 512:(eb + 1) * 512], te,
                                eng="pool", reads=[b_te], region_of=[b_Vl]))
            krp, b_krp = bufs.sb("krp", (64, TGC), BF16)

            def mmkr(q, lnc=lnc):
                r = None
                for dc in range(ND):
                    r = q.matmul(pk[4][0:64, :],
                                 kr_sb_w[:, dc * 64:(dc + 1) * 64],
                                 lnc[:, dc * TGC:(dc + 1) * TGC],
                                 start=(dc == 0), stop=(dc == ND - 1))
                return r
            S.op("pe", mmkr, reads=[b_krs, b_lnc], writes=[b_pk[4]])
            S.op("act", lambda q: q.activation(krp, pk[4][0:64, :], AF.Copy),
                 reads=[b_pk[4]], writes=[b_krp])
            sc, (ta_, b_ta_), (tb_, b_tb_) = emit_trig(
                g, consts["positions_k"], g * TGC, f"k{g}")
            (tsin, b_tsin), (tcos, b_tcos) = sc["sin"], sc["cos"]
            if "kr2" not in bufs.tiles:
                bufs.tiles["kr2"] = (sqt[0:32, 1024:1024 + TGC],
                                     S.alias_buf(b_sqt, "kr2"))
            kr2, b_kr2 = bufs.tiles["kr2"]
            S.op("dve", lambda q: q.tensor_copy(kr2, krp[32:64, :]),
                 reads=[b_krp], writes=[b_kr2])
            for part in range(2):
                ft_, st_ = (tcos, tsin) if part == 0 else (tsin, tcos)
                S.op("dve", lambda q, ft_=ft_: q.tensor_mul(
                    ta_[0:32, :], krp[0:32, :], ft_[0:32, :]),
                    reads=[b_krp, b_tcos, b_tsin], writes=[b_ta_])
                S.op("dve", lambda q, st_=st_: q.tensor_mul(
                    tb_[0:32, :], kr2, st_[0:32, :]),
                    reads=[b_kr2, b_tcos, b_tsin], writes=[b_tb_])
                alu = ALU.subtract if part == 0 else ALU.add
                if f"krot{part}" not in bufs.tiles:
                    bufs.tiles[f"krot{part}"] = (
                        sqt[0:32, part * TGC:(part + 1) * TGC],
                        S.alias_buf(b_sqt, f"krot{part}"))
                tcb, b_tcb = bufs.tiles[f"krot{part}"]
                S.op("dve", lambda q, tcb=tcb, alu=alu: q.tensor_tensor(
                    tcb, ta_[0:32, :], tb_[0:32, :], alu),
                    reads=[b_ta_, b_tb_], writes=[b_tcb])
                kRl_evs.append(S.dma(
                    kRl[part * 32:(part + 1) * 32, g * TGC:(g + 1) * TGC],
                    tcb, eng="pool", reads=[b_tcb], region_of=[b_kRl]))
        S.seal(b_Kl, Kl_evs)
        S.seal(b_Vl, Vl_evs)
        S.seal(b_kRl, kRl_evs)

        qf_evs = []
        for g in range(QT // TGC):
            lnc, b_lnc = arena.view(f"lnq{g}", C_LN, C_LN + ND * TGC)
            emit_ln(S, bufs, xq, b_xq, lnc, b_lnc, g * TGC, TGC, f"q{g}",
                    ident, b_id)
            cq, b_cq = arena.view(f"cq{g}", C_CQ, C_CQ + 2048)
            ev3 = [bufs.sb(f"evc{i}", (128, 512), BF16) for i in range(3)]
            cevs = []
            for cc2 in range(4):
                tp, b_tp = pk[cc2 % 2], b_pk[cc2 % 2]

                def mmc2(q, cc2=cc2, tp=tp, lnc=lnc):
                    r = None
                    for dc in range(ND):
                        r = q.matmul(
                            tp, dq_sb[:, dc * 512 + cc2 * 128:
                                      dc * 512 + cc2 * 128 + 128],
                            lnc[:, dc * TGC:(dc + 1) * TGC],
                            start=(dc == 0), stop=(dc == ND - 1))
                    return r
                S.op("pe", mmc2, reads=[b_dqs, b_lnc], writes=[b_tp])
                cevs.append(S.op(
                    "act", lambda q, tp=tp, cc2=cc2: q.activation(
                        cq[:, cc2 * 512:(cc2 + 1) * 512], tp, AF.Copy),
                    reads=[b_tp], region_of=[b_cq]))
            S.seal(b_cq, cevs)
            tw, b_tw = upw[0]
            wevs = [S.dma(tw[:, cc2 * 2048:(cc2 + 1) * 2048],
                          u3_bf[2 * DC + cc2 * 128:2 * DC + (cc2 + 1) * 128,
                                :],
                          reads=[b_u3], region_of=[b_tw])
                    for cc2 in range(4)]
            S.seal(b_tw, wevs)
            for ec in range(ND):
                tp, b_tp = pk[2 + ec % 2], b_pk[2 + ec % 2]

                def mmq(q, ec=ec, tp=tp, tw=tw, cq=cq):
                    r = None
                    for cc2 in range(4):
                        r = q.matmul(
                            tp, tw[:, cc2 * 2048 + ec * 128:
                                   cc2 * 2048 + ec * 128 + 128],
                            cq[:, cc2 * 512:(cc2 + 1) * 512],
                            start=(cc2 == 0), stop=(cc2 == 3))
                    return r
                S.op("pe", mmq, reads=[b_tw, b_cq], writes=[b_tp])
                te, b_te = ev3[ec % 3]
                S.op("act", lambda q, tp=tp, te=te: q.activation(
                    te, tp, AF.Copy), reads=[b_tp], writes=[b_te])
                qf_evs.append(S.dma(
                    qf_l[ec * 128:(ec + 1) * 128, g * TGC:(g + 1) * TGC],
                    te, eng="pool", reads=[b_te], region_of=[b_qfl]))
            qrp, b_qrp = arena.view(f"qrp{g}", C_QRS, C_QRS + 8 * TGC)
            qrevs = []
            for rc in range(8):
                tp, b_tp = pk[4 + rc % 2], b_pk[4 + rc % 2]

                def mmr(q, rc=rc, tp=tp, cq=cq):
                    r = None
                    for cc2 in range(4):
                        r = q.matmul(
                            tp, qr_sb_w[:, cc2 * 1024 + rc * 128:
                                        cc2 * 1024 + rc * 128 + 128],
                            cq[:, cc2 * 512:(cc2 + 1) * 512],
                            start=(cc2 == 0), stop=(cc2 == 3))
                    return r
                S.op("pe", mmr, reads=[b_qrs, b_cq], writes=[b_tp])
                qrevs.append(S.op("act", lambda q, tp=tp, rc=rc: q.activation(
                    qrp[:, rc * TGC:(rc + 1) * TGC], tp, AF.Copy),
                    reads=[b_tp], region_of=[b_qrp]))
            S.seal(b_qrp, qrevs)
            sc, (ta_, b_ta_), (tb_, b_tb_) = emit_trig(
                g, consts["positions_q"], g * TGC, f"Q{g}")
            (tsin, b_tsin), (tcos, b_tcos) = sc["sin"], sc["cos"]
            rot, b_rot = arena.view(f"rotq{g}", C_CKV, C_CKV + 8 * TGC)
            rot_evs = []
            for c in range(4):
                x1c = qrp[:, c * TGC:(c + 1) * TGC]
                x2c = qrp[:, (4 + c) * TGC:(5 + c) * TGC]
                S.op("dve", lambda q, x1c=x1c: q.tensor_mul(ta_, x1c, tcos),
                     reads=[b_qrp, b_tcos], writes=[b_ta_])
                S.op("dve", lambda q, x2c=x2c: q.tensor_mul(tb_, x2c, tsin),
                     reads=[b_qrp, b_tsin], writes=[b_tb_])
                rot_evs.append(S.op(
                    "dve", lambda q, c=c: q.tensor_sub(
                        rot[:, c * TGC:(c + 1) * TGC], ta_, tb_),
                    reads=[b_ta_, b_tb_], region_of=[b_rot]))
                S.op("dve", lambda q, x1c=x1c: q.tensor_mul(ta_, x1c, tsin),
                     reads=[b_qrp, b_tsin], writes=[b_ta_])
                S.op("dve", lambda q, x2c=x2c: q.tensor_mul(tb_, x2c, tcos),
                     reads=[b_qrp, b_tcos], writes=[b_tb_])
                rot_evs.append(S.op(
                    "dve", lambda q, c=c: q.tensor_add(
                        rot[:, (4 + c) * TGC:(5 + c) * TGC], ta_, tb_),
                    reads=[b_ta_, b_tb_], region_of=[b_rot]))
            S.seal(b_rot, rot_evs)
            for h in range(H):
                c = h // 4
                r0 = (h % 4) * 32
                qf_evs.append(S.dma(
                    qf_l[D + h * DR:D + h * DR + 32, g * TGC:(g + 1) * TGC],
                    rot[r0:r0 + 32, c * TGC:(c + 1) * TGC], eng="pool",
                    reads=[b_rot], region_of=[b_qfl]))
                qf_evs.append(S.dma(
                    qf_l[D + h * DR + 32:D + (h + 1) * DR,
                         g * TGC:(g + 1) * TGC],
                    rot[r0:r0 + 32, (4 + c) * TGC:(5 + c) * TGC], eng="pool",
                    reads=[b_rot], region_of=[b_qfl]))
        S.seal(b_qfl, qf_evs)

        S.op("pool", lambda q: q.memset(ftile, MAGIC2), reads=[b_fr],
             writes=[b_ft])
        kv_pairs = []
        for i in range(4):
            kv_pairs.append((lambda r, i=i: Kl[i * 512:(i + 1) * 512, :],
                             lambda r, i=i: sh_K[r][i * 512:(i + 1) * 512, :]))
        for i in range(4):
            kv_pairs.append((lambda r, i=i: Vl[i * 1024:(i + 1) * 1024, :],
                             lambda r, i=i: sh_V[r][i * 1024:(i + 1) * 1024,
                                                    :]))
        kv_pairs.append((lambda r: kRl, lambda r: sh_kR[r]))
        ex2 = emit_exchange_write(S, nc, exsemsB, kv_pairs, fl_kv, ftile,
                                  b_ft, MAGIC2,
                                  [b_Kl.w, b_Vl.w, b_kRl.w], "k")
        S.seal(b_shK, [ex2])
        S.seal(b_shV, [ex2])
        S.seal(b_shkR, [ex2])
        pl2 = emit_poll(S, nc, psem2, fl_kv, frtile, b_fr, MAGIC2,
                        [ex2], "k")

        C_CTX = 0
        C_KVA = 32768
        ctx, b_ctx = arena.view("ctx", C_CTX, C_CTX + ND * QT)
        krT, b_kr = arena.view("krT", C_KVA, C_KVA + 8192)
        ksb, b_ks = arena.view("ksb", C_KVA + 8192, C_KVA + 16384)
        vsb, b_vs = arena.view("vsb", C_KVA + 16384, C_KVA + 24576)
        qsb, b_qs = arena.view("qsb", C_KVA + 24576, C_KVA + 26624)
        qrs, b_qr = arena.view("qrs", C_KVA + 26624, C_KVA + 28672)
        psb = [bufs.sb(f"evs{i}", (128, 512), BF16) for i in range(2)]
        lrow, b_lr = posr[0:1, :], S.alias_buf(b_posr, "lrow")
        lbs, b_lb = arf.view("lbs", 0, 512)
        kr_evs = [S.dma(krT[0:64, s * SPAN:(s + 1) * SPAN], sh_kR[s],
                        deps=[pl2], region_of=[b_kr]) for s in range(2)]
        S.seal(b_kr, kr_evs)
        ctx_evs = []
        for h in range(H):
            kevs = [S.dma(ksb[:, s * SPAN:(s + 1) * SPAN],
                          sh_K[s][h * 128:(h + 1) * 128, :],
                          deps=[pl2], region_of=[b_ks]) for s in range(2)]
            S.seal(b_ks, kevs)
            vevs = []
            for s in range(2):
                for kc in range(SPAN // 128):
                    vevs.append(S.dma(
                        vsb[:, s * SPAN + kc * 128:s * SPAN + (kc + 1) * 128],
                        sh_V[s][kc * 128:(kc + 1) * 128,
                                h * 128:(h + 1) * 128],
                        deps=[pl2], region_of=[b_vs]))
            S.seal(b_vs, vevs)
            S.dma(qsb, qf_l[h * 128:(h + 1) * 128, :], reads=[b_qfl],
                  writes=[b_qs])
            S.dma(qrs[0:64, :], qf_l[D + h * DR:D + (h + 1) * DR, :],
                  reads=[b_qfl], writes=[b_qr])
            for mt in range(4):
                b_l = b_pk[4]
                b_cx = b_pk[2 + mt % 2]

                def emit_oa(kc):
                    tpb, b_tpb = psb[kc % 2]
                    S.op("pe", lambda q, tpb=tpb, kc=kc: q.matmul(
                        pk[4][0:1, :], onesb, tpb,
                        start=(kc == 0), stop=(kc == 63)),
                        reads=[b_ob, b_tpb],
                        writes=[b_l] if kc == 63 else [],
                        region_of=[b_l] if kc < 63 else [])
                    S.op("pe", lambda q, tpb=tpb, kc=kc, mt=mt: q.matmul(
                        pk[2 + mt % 2], vsb[:, kc * 128:(kc + 1) * 128], tpb,
                        start=(kc == 0), stop=(kc == 63)),
                        reads=[b_vs, b_tpb],
                        writes=[b_cx] if kc == 63 else [],
                        region_of=[b_cx] if kc < 63 else [])

                for kc in range(64):
                    tp, b_tp = pk[kc % 2], b_pk[kc % 2]

                    def mmS(q, kc=kc, mt=mt, tp=tp):
                        q.matmul(tp, ksb[:, kc * 128:(kc + 1) * 128],
                                 qsb[:, mt * 512:(mt + 1) * 512],
                                 start=True, stop=False)
                        return q.matmul(
                            tp, krT[0:64, kc * 128:(kc + 1) * 128],
                            qrs[0:64, mt * 512:(mt + 1) * 512],
                            start=False, stop=True)
                    S.op("pe", mmS, reads=[b_ks, b_kr, b_qs, b_qr],
                         writes=[b_tp])
                    tpb, b_tpb = psb[kc % 2]
                    S.op("act", lambda q, tp=tp, tpb=tpb: q.activation(
                        tpb, tp, AF.Exp, scale=SOFT_SC),
                        reads=[b_tp], writes=[b_tpb])
                    if kc > 0:
                        emit_oa(kc - 1)
                emit_oa(63)
                S.op("dve", lambda q: q.tensor_copy(lrow, pk[4][0:1, :]),
                     reads=[b_pk[4]], writes=[b_lr])
                S.op("dve", lambda q: q.reciprocal(lrow, lrow),
                     reads=[b_lr], writes=[b_lr])
                S.op("pe", lambda q: q.matmul(pk[5], ones1, lrow,
                                              start=True, stop=True),
                     reads=[b_o1, b_lr], writes=[b_pk[5]])
                S.op("dve", lambda q: q.tensor_copy(lbs, pk[5]),
                     reads=[b_pk[5]], writes=[b_lb])
                ccols = ctx[:, h * QT + mt * 512:h * QT + (mt + 1) * 512]
                ctx_evs.append(S.op(
                    "dve", lambda q, ccols=ccols, mt=mt: q.tensor_mul(
                        ccols, pk[2 + mt % 2], lbs),
                    reads=[b_pk[2 + mt % 2], b_lb], region_of=[b_ctx]))
        S.seal(b_ctx, ctx_evs)

        wos = [arena.view(f"wos{i}", C_KVA + i * ND * 512,
                          C_KVA + (i + 1) * ND * 512) for i in range(2)]
        bob, b_bob = arf.view("bob", 0, D)
        bob_evs = []
        for mt in range(4):
            S.dma(posr, b_o2[:, mt * 512:(mt + 1) * 512], writes=[b_posr])
            S.op("pe", lambda q, mt=mt: q.matmul(
                pk[mt % 2], ones1, posr,
                start=True, stop=True), reads=[b_o1, b_posr],
                writes=[b_pk[mt % 2]])
            bob_evs.append(S.op(
                "dve", lambda q, mt=mt: q.tensor_copy(
                    bob[:, mt * 512:(mt + 1) * 512], pk[mt % 2]),
                reads=[b_pk[mt % 2]], region_of=[b_bob]))
        S.seal(b_bob, bob_evs)
        x3_evs = []
        xts = [bufs.tiles[f"st_f{i}"] for i in range(2)]
        cbts = [bufs.tiles[f"st_b{i}"] for i in range(2)]
        for dmb in range(4):
            tw, b_tw = wos[dmb % 2]
            wevs = []
            for ec in range(ND):
                (tt, b_tt) = xts[ec % 2]
                (tb, b_tb) = cbts[ec % 2]
                S.dma(tt[:, 0:512],
                      w_o2[ec * 128:(ec + 1) * 128,
                           dmb * 512:(dmb + 1) * 512], writes=[b_tt])
                S.op("dve", lambda q, tt=tt, tb=tb: q.tensor_copy(
                    tb[:, 0:512], tt[:, 0:512]), reads=[b_tt], writes=[b_tb])
                wevs.append(S.op(
                    "dve", lambda q, tb=tb, ec=ec, tw=tw: q.tensor_copy(
                        tw[:, ec * 512:(ec + 1) * 512], tb[:, 0:512]),
                    reads=[b_tb], region_of=[b_tw]))
            S.seal(b_tw, wevs)
            for tc in range(QT // 128):
                tp, b_tp = pk[2 + tc % 2], b_pk[2 + tc % 2]

                def mmo(q, tc=tc, tp=tp, tw=tw):
                    r = None
                    for ec in range(ND):
                        r = q.matmul(
                            tp, ctx[:, ec * QT + tc * 128:
                                    ec * QT + tc * 128 + 128],
                            tw[:, ec * 512:(ec + 1) * 512],
                            start=(ec == 0), stop=(ec == ND - 1))
                    return r
                S.op("pe", mmo, reads=[b_ctx, b_tw], writes=[b_tp])
                (tx, b_tx) = xts[tc % 2]
                S.dma(tx[:, 0:512],
                      xq[tc * 128:(tc + 1) * 128,
                         dmb * 512:(dmb + 1) * 512], reads=[b_xq],
                      writes=[b_tx])
                S.op("dve", lambda q, tx=tx, dmb=dmb: q.tensor_add(
                    tx[:, 0:512], tx[:, 0:512],
                    bob[:, dmb * 512:(dmb + 1) * 512]),
                    reads=[b_bob], writes=[b_tx])
                S.op("dve", lambda q, tx=tx, tp=tp: q.tensor_add(
                    tx[:, 0:512], tx[:, 0:512], tp),
                    reads=[b_tp], writes=[b_tx])
                x3_evs.append(S.dma(
                    x3[tc * 128:(tc + 1) * 128, dmb * 512:(dmb + 1) * 512],
                    tx[:, 0:512], eng="pool", reads=[b_tx],
                    region_of=[b_x3]))
        S.seal(b_x3, x3_evs)

        emit_ffn(S, bufs, arena, pk, b_pk, x3, b_x3, xq_out, b_xqo,
                 wg_bf, b_wg, wu_bf, b_wu, wd_bf, b_wd, delta_2, b_d2,
                 ident, b_id, "2")
        S.op("pool", lambda q: q.nop(), deps=[b_xqo.w])
        S.emit(block)
    return nc


def _fused_inmaps(inputs):
    xs = np.asarray(inputs["L0_attn_a_hidden_states"]).reshape(B * T, D)
    in_maps = []
    for c in range(NCORES):
        cst = host_consts_fused(c)
        b = c // 4
        half = c % 2
        t0 = b * T + half * SPAN
        m = {
            "x": np.ascontiguousarray(xs[t0:t0 + SPAN]),
            "w_qkv": np.asarray(inputs["L0_attn_a_w_qkv"]),
            "w_gate": np.asarray(inputs["L0_attn_a_w_gate"]),
            "slope": np.asarray(
                inputs["L0_attn_a_slope_rate"]).reshape(1, H),
            "w_out": np.asarray(inputs["L0_attn_a_w_out"]),
            "norm_w": np.asarray(inputs["L0_attn_a_norm_w"]),
            "wq_gate": np.asarray(inputs["L0_ffn_wq_gate"]),
            "s_gate": np.asarray(inputs["L0_ffn_s_gate"]),
            "wq_up": np.asarray(inputs["L0_ffn_wq_up"]),
            "s_up": np.asarray(inputs["L0_ffn_s_up"]),
            "wq_down": np.asarray(inputs["L0_ffn_wq_down"]),
            "s_down": np.asarray(inputs["L0_ffn_s_down"]),
            "w_dkv": np.asarray(inputs["L0_attn_b_W_DKV"]),
            "w_dq": np.asarray(inputs["L0_attn_b_W_DQ"]),
            "w_uk": np.asarray(inputs["L0_attn_b_W_UK"]),
            "w_uv": np.asarray(inputs["L0_attn_b_W_UV"]),
            "w_uq": np.asarray(inputs["L0_attn_b_W_UQ"]),
            "w_kr": np.asarray(inputs["L0_attn_b_W_KR"]),
            "w_qr": np.asarray(inputs["L0_attn_b_W_QR"]),
            "w_o2": np.asarray(inputs["L0_attn_b_W_O"]),
            "b_o2": np.asarray(inputs["L0_attn_b_b_O"]).reshape(1, D),
        }
        for n, s, d in FUSED_CONST_SPECS:
            m[n] = np.ascontiguousarray(cst[n].reshape(s))
        in_maps.append(m)
    return in_maps


def _assemble_fused(results):
    out = np.zeros((B * T, D), np.float32)
    for c in range(NCORES):
        b = c // 4
        half = c % 2
        quarter = (c // 2) % 2
        t0 = b * T + half * SPAN + quarter * QT
        out[t0:t0 + QT] = results[c]["xq_out"]
    return out.reshape(B, T, D)


def kernel(**inputs):
    nc = build_fused()
    res = run_bass_kernel_spmd(nc, _fused_inmaps(inputs),
                               core_ids=list(range(NCORES)))
    return np.ascontiguousarray(_assemble_fused(res.results))


def kernel_unfused(**inputs):
    r1 = _run_p1(inputs)
    r2 = _run_p2(inputs, r1)
    r3 = _run_p3(inputs, r2)
    out = np.concatenate([r3[c]["xout"] for c in range(NCORES)], axis=0)
    return np.ascontiguousarray(out.reshape(B, T, D))
